# version 05
# baseline (speedup 1.0000x reference)
import numpy as np
from contextlib import ExitStack

import concourse.bass as bass
import concourse.mybir as mybir
from concourse.bass_utils import run_bass_kernel_spmd

F32 = mybir.dt.float32
F16 = mybir.dt.float16
BF16 = mybir.dt.bfloat16
AF = mybir.ActivationFunctionType
OP = mybir.AluOpType
AX = mybir.AxisListType

NCORES = 8
B = 16
S = 2048
D = 1024
KC = D // 128
BLOC = B // NCORES
T = BLOC * S
L0L = 4
H0 = 8
DH0 = 128
F0 = 2048
H1 = 16
DH1 = 64
E1 = 8
HID1 = 4096
EPS_LN = 1e-5
EPS_GN = 1e-5
EPS_L1 = 1e-3

SB = S // 128
SG = S // 512
TG = T // 512
TB = T // 128


class Em:

    ENGS = ("sync", "tensor", "vector", "scalar", "gpsimd")

    def __init__(self, nc, ctx):
        self.nc = nc
        self.ctx = ctx
        self.steps = {e: [] for e in self.ENGS}
        self.writer = {}
        self.readers = {}
        self.opcnt = {e: 0 for e in self.ENGS}
        self.sb = {}
        self.ps = {}
        self.sems = {}

    def _uniq(self, name):
        n = getattr(self, "_scope_n", 0)
        return name if n == 0 else f"{name}_s{n}"

    def sbuf(self, name, shape, dtype):
        if name in self.sb:
            return self.sb[name]
        t = self._scope().enter_context(
            self.nc.sbuf_tensor(self._uniq(name), shape, dtype))
        self.sb[name] = t
        return t

    def psum_bank(self, name, dtype=F32, cols=512):
        if name in self.ps:
            return self.ps[name]
        t = self._scope().enter_context(
            self.nc.psum_tensor(self._uniq(name), [128, cols], dtype))
        self.ps[name] = t
        return t

    def _scope(self):
        return self.scopes[-1] if getattr(self, "scopes", None) else self.ctx

    def push_scope(self):
        if not hasattr(self, "scopes"):
            self.scopes = []
        self._scope_n = getattr(self, "_scope_n", 0) + 1
        self.scopes.append(ExitStack())
        self._scope_saved = getattr(self, "_scope_saved", [])
        self._scope_saved.append((dict(self.sb), dict(self.ps)))

    def pop_scope(self):
        self.barrier()
        sc = self.scopes.pop()
        sc.close()
        sb0, ps0 = self._scope_saved.pop()
        self.sb = sb0
        self.ps = ps0

    def barrier(self):
        for e in self.ENGS:
            self.steps[e].append(["BARRIER", None, (e, self.opcnt[e], False), False])
            self.opcnt[e] += 1
        self.writer = {}
        self.readers = {}
        self.nbar = getattr(self, "nbar", 0) + 1

    def op(self, eng, fn, reads=(), writes=(), dma=False):
        deps = []
        for k in reads:
            w = self.writer.get(k)
            if w is not None:
                deps.append(w)
        for k in writes:
            w = self.writer.get(k)
            if w is not None:
                deps.append(w)
            for r in self.readers.get(k, ()):
                deps.append(r)
        opid = (eng, self.opcnt[eng], dma)
        self.steps[eng].append([fn, deps, opid, dma])
        self.opcnt[eng] += 1
        for k in reads:
            self.readers.setdefault(k, []).append(opid)
        for k in writes:
            self.writer[k] = opid
            self.readers[k] = []
        return opid

    def finalize(self, block):
        needed = {e: set() for e in self.ENGS}
        for e in self.ENGS:
            for fn, deps, opid, dma in self.steps[e]:
                if fn == "BARRIER":
                    continue
                for (pe, pidx, pdma) in deps:
                    if pdma or pe != e:
                        needed[pe].add(pidx)
        NQ = 8
        crank = {e: {} for e in self.ENGS}
        dslot = {e: {} for e in self.ENGS}
        for e in self.ENGS:
            ci = di = 0
            slotcnt = [0] * NQ
            for fn, deps, opid, dma in self.steps[e]:
                if fn == "BARRIER":
                    continue
                idx = opid[1]
                if dma:
                    s = di % NQ
                    slotcnt[s] += 1
                    dslot[e][idx] = (s, slotcnt[s])
                    di += 1
                elif idx in needed[e]:
                    ci += 1
                    crank[e][idx] = ci
        for e in self.ENGS:
            for pre in ("c_", "d_"):
                self.sems[pre + e] = self.ctx.enter_context(
                    self.nc.semaphore(pre + e))
            for s in range(NQ):
                self.sems[f"q_{e}_{s}"] = self.ctx.enter_context(
                    self.nc.semaphore(f"q_{e}_{s}"))
        self.sems["bar"] = self.ctx.enter_context(self.nc.semaphore("bar"))
        em = self

        def make_body(e):
            def body(eng):
                waited = {}
                drain_cover = -1
                drains = 0
                dcount_sofar = {en: [0] * NQ for en in em.ENGS}
                nbar_seen = 0
                for fn, deps, opid, dma in em.steps[e]:
                    if fn == "BARRIER":
                        for s in range(NQ):
                            sname = f"q_{e}_{s}"
                            v = 16 * dcount_sofar[e][s]
                            if v and waited.get(sname, 0) < v:
                                eng.wait_ge(em.sems[sname], v)
                                waited[sname] = v
                        if e in ("vector", "scalar", "gpsimd"):
                            eng.drain().then_inc(em.sems["bar"], 1)
                        else:
                            eng.nop().then_inc(em.sems["bar"], 1)
                        nbar_seen += 1
                        eng.wait_ge(em.sems["bar"], 5 * nbar_seen)
                        continue
                    need_drain_past = -1
                    for (pe, pidx, pdma) in deps:
                        if pe == e and not pdma:
                            if e != "tensor":
                                need_drain_past = max(need_drain_past, pidx)
                        else:
                            if pdma:
                                ps_, pc_ = dslot[pe][pidx]
                                sname = f"q_{pe}_{ps_}"
                                v = 16 * pc_
                            else:
                                sname = "c_" + pe
                                v = crank[pe][pidx]
                            if waited.get(sname, 0) < v:
                                eng.wait_ge(em.sems[sname], v)
                                waited[sname] = v
                    if need_drain_past >= 0 and need_drain_past > drain_cover:
                        drains += 1
                        eng.drain().then_inc(em.sems["d_" + e], 1)
                        eng.wait_ge(em.sems["d_" + e], drains)
                        drain_cover = opid[1] - 1
                    if dma:
                        s, cnt = dslot[e][opid[1]]
                        sname = f"q_{e}_{s}"
                        if cnt >= 2 and waited.get(sname, 0) < 16 * (cnt - 1):
                            eng.wait_ge(em.sems[sname], 16 * (cnt - 1))
                            waited[sname] = 16 * (cnt - 1)
                        dcount_sofar[e][s] = cnt
                    ins = fn(eng)
                    idx = opid[1]
                    if dma:
                        s, cnt = dslot[e][idx]
                        ins.then_inc(em.sems[f"q_{e}_{s}"], 16)
                    elif idx in crank[e]:
                        ins.then_inc(em.sems["c_" + e], 1)
            return body

        for e in self.ENGS:
            if self.steps[e]:
                getattr(block, e)(make_body(e))


def host_constants():
    c = {}
    c["ident"] = np.eye(128, dtype=np.float32)
    pos = np.arange(S, dtype=np.float64)
    inv = 1.0 / (10000.0 ** (np.arange(0, DH0, 2, dtype=np.float64) / DH0))
    ang = pos[None, :] * inv[:, None]
    c["rope_cos"] = np.concatenate([np.cos(ang)] * 2, 0).astype(np.float32)
    sn = np.sin(ang)
    c["rope_sinq"] = np.concatenate([-sn, sn], 0).astype(np.float32)
    c["rope_sink"] = np.concatenate([sn, -sn], 0).astype(np.float32)
    P = np.zeros((128, 128), dtype=np.float32)
    for i in range(64):
        P[i, 64 + i] = 1.0
        P[64 + i, i] = 1.0
    c["swapT"] = P.T.copy()
    g = 1.0 - np.exp(np.linspace(np.log(1 / 32), np.log(1 / 512), H0))
    lng = np.log(g)
    c["lngam"] = np.repeat(lng[None, :], 128, 0).astype(np.float32)
    ev = np.zeros((128, H0 * SB), dtype=np.float64)
    for h in range(H0):
        for j in range(SB):
            m = 128 * j + np.arange(128)
            ev[:, h * SB + j] = np.exp(-lng[h] * m)
    c["attev"] = ev.astype(np.float32)
    ramp = np.repeat(np.arange(512, dtype=np.float64)[None, :], 128, 0)
    c["ramp512"] = ramp.astype(np.float32)
    qb = np.zeros((128, H0 * SG), dtype=np.float64)
    for h in range(H0):
        for igr in range(SG):
            qb[:, h * SG + igr] = lng[h] * (512 * igr)
    c["qdecb"] = qb.astype(np.float32)
    dm = (np.arange(128)[None, :] >= np.arange(128)[:, None])
    c["diagmask"] = dm.astype(np.float32)
    bd = np.zeros((128, 128), dtype=np.float32)
    bd[:64, :64] = 1.0
    bd[64:, 64:] = 1.0
    c["blkones64"] = bd
    c["ones128"] = np.ones((128, 128), dtype=np.float32)
    return c


def emit_ingress(em, x_tok, x_d, xn_d, mx):
    sbuf, pb = em.sbuf, em.psum_bank
    xt = [sbuf(f"ig_xt{i}", [128, 1024], F32) for i in range(2)]
    xdt = [sbuf(f"ig_xd{i}", [128, 1024], F32) for i in range(2)]
    xsq = sbuf("ig_xsq", [128, 1024], F32)
    mu = sbuf("ig_mu", [128, 128], F32)
    var = sbuf("ig_var", [128, 128], F32)
    Av = sbuf("ig_A", [128, 128], F32)
    Bv = sbuf("ig_B", [128, 128], F32)
    xnt = [sbuf(f"ig_xn{i}", [128, 1024], F32) for i in range(2)]
    mxb = sbuf("ig_mxb", [128, TB], F32)
    mxc = sbuf("ig_mxc", [128, 2], F32)
    mxr = sbuf("ig_mxr", [1, 128], F32)
    id_sb = em.sb["ident"]
    ones_sb = em.sb["ones128"]
    eps_sb = em.sb["cp_epsln"]
    ps = [pb(f"ig_ps{i}") for i in range(4)]
    pss = pb("ig_pss")
    psq = pb("ig_psq")

    for tb in range(TB):
        pi = tb % 2
        xtt, xdd, xnn = xt[pi], xdt[pi], xnt[pi]
        kxt, kxd, kxn = f"ig_xt{pi}", f"ig_xd{pi}", f"ig_xn{pi}"
        em.op("sync",
              lambda e, xtt=xtt, tb=tb: e.dma_start(
                  out=xtt[:, :], in_=x_tok[tb * 128:(tb + 1) * 128, :]),
              writes=[kxt], dma=True)
        for c in range(KC):
            pk = f"ig_ps{c % 4}"
            em.op("tensor",
                  lambda e, xtt=xtt, c=c, p=ps[c % 4]: e.transpose(
                      p[:, 0:128], xtt[:, c * 128:(c + 1) * 128], id_sb[:, :]),
                  reads=[kxt, "ident"], writes=[pk])
            em.op("scalar",
                  lambda e, xdd=xdd, c=c, p=ps[c % 4]: e.activation(
                      out=xdd[:, c * 128:(c + 1) * 128], in_=p[:, 0:128],
                      func=AF.Copy),
                  reads=[pk], writes=[kxd + f"c{c}"])
        rx = [kxd + f"c{c}" for c in range(KC)]
        em.op("gpsimd",
              lambda e, xdd=xdd, tb=tb: e.dma_start(
                  out=x_d[:, :, tb * 128:(tb + 1) * 128].rearrange("k p t -> p k t"),
                  in_=xdd[:, :].rearrange("p (k c) -> p k c", k=KC)),
              reads=rx, dma=True)
        em.op("scalar",
              lambda e, xdd=xdd: e.activation(out=xsq[:, :], in_=xdd[:, :],
                                              func=AF.Square),
              reads=rx, writes=["ig_xsq"])
        for c in range(KC):
            em.op("tensor",
                  lambda e, xdd=xdd, c=c: e.matmul(
                      pss[:, 0:128], ones_sb[:, :],
                      xdd[:, c * 128:(c + 1) * 128],
                      start=(c == 0), stop=(c == KC - 1)),
                  reads=rx + ["ones128"], writes=(["ig_pss"] if c in (0, KC - 1) else []))
        for c in range(KC):
            em.op("tensor",
                  lambda e, c=c: e.matmul(
                      psq[:, 0:128], ones_sb[:, :],
                      xsq[:, c * 128:(c + 1) * 128],
                      start=(c == 0), stop=(c == KC - 1)),
                  reads=["ig_xsq", "ones128"], writes=(["ig_psq"] if c in (0, KC - 1) else []))
        em.op("vector", lambda e: e.tensor_scalar_mul(mu[:, :], pss[:, 0:128], 1.0 / D),
              reads=["ig_pss"], writes=["ig_mu"])
        em.op("vector", lambda e: e.tensor_scalar_mul(var[:, :], psq[:, 0:128], 1.0 / D),
              reads=["ig_psq"], writes=["ig_var"])
        em.op("vector",
              lambda e: e.tensor_tensor(Bv[:, :], mu[:, :], mu[:, :], OP.mult),
              reads=["ig_mu"], writes=["ig_B"])
        em.op("vector",
              lambda e: e.tensor_tensor(var[:, :], var[:, :], Bv[:, :], OP.subtract),
              reads=["ig_var", "ig_B"], writes=["ig_var"])
        em.op("scalar",
              lambda e: e.activation(out=Av[:, :], in_=var[:, :], func=AF.Ln,
                                     bias=eps_sb[:, 0:1]),
              reads=["ig_var", "cp_epsln"], writes=["ig_A"])
        em.op("scalar",
              lambda e: e.activation(out=Av[:, :], in_=Av[:, :], func=AF.Exp,
                                     scale=em.sb["cp_mhalf"][:, 0:1]),
              reads=["ig_A", "cp_mhalf"], writes=["ig_A"])
        em.op("vector",
              lambda e: e.scalar_tensor_tensor(Bv[:, :], mu[:, :], -1.0, Av[:, :],
                                               OP.mult, OP.mult),
              reads=["ig_mu", "ig_A"], writes=["ig_B"])
        em.op("vector",
              lambda e, xdd=xdd, xnn=xnn: e.tensor_tensor(
                  xnn[:, :].rearrange("p (c t) -> p c t", c=KC),
                  xdd[:, :].rearrange("p (c t) -> p c t", c=KC),
                  Av[:, :].unsqueeze(1).broadcast_to([128, KC, 128]), OP.mult),
              reads=rx + ["ig_A"], writes=[kxn])
        em.op("vector",
              lambda e, xnn=xnn: e.tensor_tensor(
                  xnn[:, :].rearrange("p (c t) -> p c t", c=KC),
                  xnn[:, :].rearrange("p (c t) -> p c t", c=KC),
                  Bv[:, :].unsqueeze(1).broadcast_to([128, KC, 128]), OP.add),
              reads=[kxn, "ig_B"], writes=[kxn])
        em.op("gpsimd",
              lambda e, xnn=xnn, tb=tb: e.dma_start(
                  out=xn_d[:, :, tb * 128:(tb + 1) * 128].rearrange("k p t -> p k t"),
                  in_=xnn[:, :].rearrange("p (k c) -> p k c", k=KC)),
              reads=[kxn], dma=True)
        em.op("vector",
              lambda e, xnn=xnn, tb=tb: e.reduce_max(
                  mxb[:, tb:tb + 1], xnn[:, :], axis=AX.X,
                  apply_absolute_value=True),
              reads=[kxn], writes=[f"ig_mxb{tb}"])
    em.op("vector",
          lambda e: e.reduce_max(mxc[:, 0:1], mxb[:, :], axis=AX.X),
          reads=[f"ig_mxb{tb}" for tb in range(TB)], writes=["ig_mxc"])
    em.op("tensor",
          lambda e: e.transpose(ps[0][0:1, 0:128], mxc[:, 0:1], id_sb[:, :]),
          reads=["ig_mxc", "ident"], writes=["ig_ps0"])
    em.op("scalar",
          lambda e: e.activation(out=mxr[:, :], in_=ps[0][0:1, 0:128], func=AF.Copy),
          reads=["ig_ps0"], writes=["ig_mxr"])
    em.op("vector",
          lambda e: e.reduce_max(mxc[0:1, 1:2], mxr[:, :], axis=AX.X),
          reads=["ig_mxr"], writes=["ig_mxf"])
    em.op("gpsimd",
          lambda e: e.dma_start(out=mx[:, :], in_=mxc[0:1, 1:2]),
          reads=["ig_mxf"], writes=["MX"], dma=True)


def load_consts(em, dram_consts):
    for name, ap in dram_consts.items():
        t = em.sbuf("sb_" + name, [ap.shape[0], ap.shape[1]], F32)
        em.sb[name] = t
        em.op("sync", lambda e, t=t, ap=ap: e.dma_start(out=t[:, :], in_=ap[:, :]),
              writes=[name], dma=True)
    for nm, v in [("cp_epsln", EPS_LN), ("cp_mhalf", -0.5), ("cp_epsgn", EPS_GN),
                  ("cp_epsl1", EPS_L1), ("cp_one", 1.0)]:
        t = em.sbuf(nm, [128, 1], F32)
        em.op("vector", lambda e, t=t, v=v: e.memset(t[:, :], v), writes=[nm])


def build_p1():
    nc = bass.Bass("TRN2", target_bir_lowering=False, debug=False)
    x_tok = nc.dram_tensor("x_tok", [T, D], F32, kind="ExternalInput").ap()
    dcon = {}
    for nm in ("ident", "ones128"):
        shp = {"ident": [128, 128], "ones128": [128, 128]}[nm]
        dcon[nm] = nc.dram_tensor(nm, shp, F32, kind="ExternalInput").ap()
    x_d = nc.dram_tensor("x_d", [KC, 128, T], F32, kind="ExternalOutput").ap()
    xn_d = nc.dram_tensor("xn_d", [KC, 128, T], F32, kind="ExternalOutput").ap()
    mx = nc.dram_tensor("mx", [1, 1], F32, kind="ExternalOutput").ap()
    with ExitStack() as ctx:
        em = Em(nc, ctx)
        load_consts(em, dcon)
        emit_ingress(em, x_tok, x_d, xn_d, mx)
        blk = ctx.enter_context(nc.Block())
        em.finalize(blk)
    return nc




def emit_decay_tables(em):
    sbuf = em.sbuf
    lngam = sbuf("lngam", [128, H0], F32)
    neglng = sbuf("neglng", [128, H0], F32)
    attev = sbuf("attev", [128, H0 * SB], F32)
    qdecb = sbuf("qdecb", [128, H0 * SG], F32)
    tt = sbuf("dt_t", [128, 8], F32)
    evb = sbuf("dt_evb", [128, H0 * SB], F32)
    cj = em.sb["cjramp"]
    hidx = em.sb["hidx"]
    j128 = em.sb["j128"]
    ig512 = em.sb["ig512"]
    cseed = em.sb["cseed"]
    em.op("scalar", lambda e: e.activation(out=tt[:, 0:1], in_=cseed[:, 0:1],
                                           func=AF.Ln),
          reads=["cseed"], writes=["dt_a0"])
    em.op("scalar", lambda e: e.activation(out=tt[:, 1:2], in_=cseed[:, 1:2],
                                           func=AF.Ln),
          reads=["cseed"], writes=["dt_a1"])
    em.op("vector", lambda e: e.reciprocal(tt[:, 2:3], cseed[:, 2:3]),
          reads=["cseed"], writes=["dt_r7"])
    em.op("vector", lambda e: e.tensor_tensor(tt[:, 3:4], tt[:, 1:2], tt[:, 0:1],
                                              OP.subtract),
          reads=["dt_a0", "dt_a1"], writes=["dt_d"])
    em.op("vector", lambda e: e.tensor_tensor(tt[:, 3:4], tt[:, 3:4], tt[:, 2:3],
                                              OP.mult),
          reads=["dt_d", "dt_r7"], writes=["dt_d"])
    em.op("vector", lambda e: e.tensor_scalar(tt[:, 4:5 + H0 - 1] if False else lngam[:, :],
                                              hidx[:, :], tt[:, 3:4], tt[:, 0:1],
                                              OP.mult, OP.add),
          reads=["hidx", "dt_d", "dt_a0"], writes=["dt_ah"])
    em.op("scalar", lambda e: e.activation(out=lngam[:, :], in_=lngam[:, :],
                                           func=AF.Exp),
          reads=["dt_ah"], writes=["dt_ea"])
    em.op("vector", lambda e: e.tensor_scalar(lngam[:, :], lngam[:, :], -1.0, 1.0,
                                              OP.mult, OP.add),
          reads=["dt_ea"], writes=["dt_gam"])
    em.op("scalar", lambda e: e.activation(out=lngam[:, :], in_=lngam[:, :],
                                           func=AF.Ln),
          reads=["dt_gam"], writes=["lngam"])
    em.op("vector", lambda e: e.tensor_scalar_mul(neglng[:, :], lngam[:, :], -1.0),
          reads=["lngam"], writes=["neglng"])
    for h in range(H0):
        em.op("vector", lambda e, h=h: e.tensor_scalar_mul(
            evb[:, h * SB:(h + 1) * SB], j128[:, :], neglng[:, h:h + 1]),
            reads=["j128", "neglng"], writes=[f"evb{h}"])
        em.op("vector", lambda e, h=h: e.tensor_scalar_mul(
            qdecb[:, h * SG:(h + 1) * SG], ig512[:, :], lngam[:, h:h + 1]),
            reads=["ig512", "lngam"], writes=["qdecb"])
        for j in range(SB):
            em.op("scalar", lambda e, h=h, j=j: e.activation(
                out=attev[:, h * SB + j:h * SB + j + 1], in_=cj[:, 0:1],
                func=AF.Exp, scale=neglng[:, h:h + 1],
                bias=evb[:, h * SB + j:h * SB + j + 1]),
                reads=["cjramp", "neglng", f"evb{h}"], writes=["attev"])


def emit_l0_stack(em, b, aps, layers=None, dbg=None):
    if layers is None:
        layers = list(range(L0L))
    sbuf = em.sbuf
    nm = "l0_"
    xn = sbuf(nm + "xn", [128, KC * S], BF16)
    qt = sbuf(nm + "qt", [128, H0 * S], BF16)
    kt = sbuf(nm + "kt", [128, H0 * S], BF16)
    vt = sbuf(nm + "vt", [128, SB * D], BF16)
    gg = sbuf(nm + "gg", [128, H0 * 512], BF16)
    wb = sbuf(nm + "wb", [128, KC * D], BF16)
    wob = sbuf(nm + "wob", [128, (KC // 2) * D], BF16)
    w1b = qt
    w2b = kt
    hb = vt
    Aall = sbuf(nm + "A", [128, S], BF16)
    Ball = sbuf(nm + "B", [128, S], BF16)
    xg = sbuf(nm + "xg", [128, KC * 512], F16)
    xg2 = sbuf(nm + "xg2", [128, KC * 512], F16)
    t1 = sbuf(nm + "t1", [128, 512], BF16)
    t2 = sbuf(nm + "t2", [128, 512], BF16)
    t3 = sbuf(nm + "t3", [128, 512], BF16)
    tf = sbuf(nm + "tf", [128, 512], F32)
    ybf = t1
    ysq = t3
    atb = Aall
    dq = t3
    mum = sbuf(nm + "mu", [128, 512], F32)
    msq = sbuf(nm + "msq", [128, 512], F32)
    rst = msq
    ps = [em.ps[f"pb{i}"] for i in range(8)]
    cos = em.sb["rope_cos_bf"]
    sinq = em.sb["rope_sinq_bf"]
    swapT = em.sb["swapT_bf"]
    ones = em.sb["ones_bf"]
    ones_hf = em.sb["ones_hf"]
    ramp = em.sb["ramp512_hf16"]
    qdecb = em.sb["qdecb"]
    attev = em.sb["attev"]
    lngam = em.sb["lngam"]
    diagm = em.sb["diagmask_bf"]
    lnp = em.sb["lnp"]
    gnp = em.sb["gnp"]
    epsl = em.sb["cp_epsln"]
    epsg = em.sb["cp_epsgn"]
    mhalf = em.sb["cp_mhalf"]

    def xi_load(dst, dkey, src_ap, src_key, g):
        eng = "gpsimd" if src_ap.dtype != F16 else "sync"
        em.op(eng, lambda e: e.dma_start(
            out=dst[:, :].rearrange("p (k t) -> p k t", k=KC),
            in_=src_ap[:, :, g * 512:(g + 1) * 512].rearrange("k p t -> p k t")),
            reads=[src_key], writes=[dkey], dma=True)

    def ln_stats(src_ap, src_key):
        for g in range(SG):
            xi_load(xg, nm + "xg", src_ap, src_key, g)
            em.op("scalar", lambda e: e.activation(
                out=xg2[:, :], in_=xg[:, :], func=AF.Square),
                reads=[nm + "xg"], writes=[nm + "xg2"])
            for c in range(KC):
                em.op("tensor", lambda e, c=c: e.matmul(
                    ps[0][:, 0:512], ones_hf[:, :], xg[:, c * 512:(c + 1) * 512],
                    start=(c == 0), stop=(c == KC - 1)),
                    reads=[nm + "xg", "ones_hf"],
                    writes=(["pb0"] if c in (0, KC - 1) else []))
            for c in range(KC):
                em.op("tensor", lambda e, c=c: e.matmul(
                    ps[1][:, 0:512], ones_hf[:, :], xg2[:, c * 512:(c + 1) * 512],
                    start=(c == 0), stop=(c == KC - 1)),
                    reads=[nm + "xg2", "ones_hf"],
                    writes=(["pb1"] if c in (0, KC - 1) else []))
            em.op("vector", lambda e: e.tensor_scalar_mul(
                mum[:, :], ps[0][:, 0:512], 1.0 / D),
                reads=["pb0"], writes=[nm + "mu"])
            em.op("vector", lambda e: e.tensor_scalar_mul(
                msq[:, :], ps[1][:, 0:512], 1.0 / D),
                reads=["pb1"], writes=[nm + "msq"])
            em.op("vector", lambda e: e.tensor_tensor(
                tf[:, :], mum[:, :], mum[:, :], OP.mult),
                reads=[nm + "mu"], writes=[nm + "tf"])
            em.op("vector", lambda e: e.tensor_tensor(
                rst[:, :], msq[:, :], tf[:, :], OP.subtract),
                reads=[nm + "msq", nm + "tf"], writes=[nm + "rst"])
            em.op("scalar", lambda e: e.activation(
                out=rst[:, :], in_=rst[:, :], func=AF.Ln, bias=epsl[:, 0:1]),
                reads=[nm + "rst", "cp_epsln"], writes=[nm + "rst"])
            em.op("scalar", lambda e, g=g: e.activation(
                out=Aall[:, g * 512:(g + 1) * 512], in_=rst[:, :], func=AF.Exp,
                scale=mhalf[:, 0:1]),
                reads=[nm + "rst", "cp_mhalf"], writes=[nm + f"A{g}"])
            em.op("vector", lambda e, g=g: e.scalar_tensor_tensor(
                Ball[:, g * 512:(g + 1) * 512], mum[:, :], -1.0,
                Aall[:, g * 512:(g + 1) * 512], OP.mult, OP.mult),
                reads=[nm + "mu", nm + f"A{g}"], writes=[nm + f"B{g}"])

    def ln_apply(src_ap, src_key, into_xg=True):
        for g in range(SG):
            xi_load(xg, nm + "xg", src_ap, src_key, g)
            for c in range(KC):
                em.op("vector", lambda e, c=c, g=g: e.tensor_tensor(
                    tf[:, :], xg[:, c * 512:(c + 1) * 512],
                    Aall[:, g * 512:(g + 1) * 512], OP.mult),
                    reads=[nm + "xg", nm + f"A{g}"], writes=[nm + "tf"])
                em.op("vector", lambda e, c=c, g=g: e.tensor_tensor(
                    tf[:, :], tf[:, :], Ball[:, g * 512:(g + 1) * 512], OP.add),
                    reads=[nm + "tf", nm + f"B{g}"], writes=[nm + "tf"])
                em.op("scalar", lambda e, c=c, g=g: e.activation(
                    out=xn[:, c * S + g * 512:c * S + (g + 1) * 512],
                    in_=tf[:, :], func=AF.Identity, scale=lnp[:, c:c + 1],
                    bias=lnp[:, KC + c:KC + c + 1]),
                    reads=[nm + "tf", "lnp"], writes=[nm + f"xn{c}_{g}"])

    def load_w(dst, dkey, w_ap, wkey):
        em.op("gpsimd", lambda e: e.dma_start(
            out=dst[:, :].rearrange("p (k d) -> p k d", k=KC),
            in_=w_ap.rearrange("(k p) d -> p k d", p=128)),
            reads=[wkey], writes=[dkey], dma=True)

    for li, l in enumerate(layers):
        src_ap = aps["xi0"] if li == 0 else aps["xi"][l % 2]
        src_key = "XI0" if li == 0 else f"XI{l % 2}"
        ln_stats(src_ap, src_key)
        ln_apply(src_ap, src_key)
        xnk = lambda c, g: nm + f"xn{c}_{g}"
        load_w(wb, nm + "wb", aps["wv"][l], "WV")
        for j in range(SB):
            gj, oj = j // 4, (j % 4) * 128
            for hf in range(2):
                for k in range(KC):
                    em.op("tensor", lambda e, j=j, hf=hf, k=k, gj=gj, oj=oj: e.matmul(
                        ps[2][:, 0:512],
                        xn[:, k * S + j * 128:k * S + (j + 1) * 128],
                        wb[:, k * D + hf * 512:k * D + (hf + 1) * 512],
                        start=(k == 0), stop=(k == KC - 1)),
                        reads=[xnk(k, gj), nm + "wb"],
                        writes=(["pb2"] if k in (0, KC - 1) else []))
                em.op("scalar", lambda e, j=j, hf=hf: e.activation(
                    out=vt[:, j * D + hf * 512:j * D + (hf + 1) * 512],
                    in_=ps[2][:, 0:512], func=AF.Copy),
                    reads=["pb2"], writes=[nm + f"vt{j}_{hf}"])
        for which in ("k", "q"):
            load_w(wb, nm + "wb", aps["w" + which][l], "W" + which)
            for h in range(H0):
                for g in range(SG):
                    for k in range(KC):
                        em.op("tensor", lambda e, h=h, g=g, k=k: e.matmul(
                            ps[2][:, 0:512],
                            wb[:, k * D + h * 128:k * D + (h + 1) * 128],
                            xn[:, k * S + g * 512:k * S + (g + 1) * 512],
                            start=(k == 0), stop=(k == KC - 1)),
                            reads=[xnk(k, g), nm + "wb"],
                            writes=(["pb2"] if k in (0, KC - 1) else []))
                    em.op("scalar", lambda e: e.activation(
                        out=t1[:, :], in_=ps[2][:, 0:512], func=AF.Copy),
                        reads=["pb2"], writes=[nm + "t1"])
                    em.op("tensor", lambda e: e.matmul(
                        ps[3][:, 0:512], swapT[:, :], t1[:, :],
                        start=True, stop=True),
                        reads=[nm + "t1", "swapT_bf"], writes=["pb3"])
                    em.op("vector", lambda e, g=g: e.tensor_tensor(
                        t2[:, :], t1[:, :], cos[:, g * 512:(g + 1) * 512],
                        OP.mult),
                        reads=[nm + "t1", "rope_cos_bf"], writes=[nm + "t2"])
                    em.op("vector", lambda e, g=g: e.tensor_tensor(
                        t3[:, :], ps[3][:, 0:512],
                        sinq[:, g * 512:(g + 1) * 512], OP.mult),
                        reads=["pb3", "rope_sinq_bf"], writes=[nm + "t3"])
                    if which == "k":
                        em.op("vector", lambda e, h=h, g=g: e.tensor_tensor(
                            kt[:, h * S + g * 512:h * S + (g + 1) * 512],
                            t2[:, :], t3[:, :], OP.subtract),
                            reads=[nm + "t2", nm + "t3"],
                            writes=[nm + f"kt{h}_{g}"])
                    else:
                        em.op("vector", lambda e: e.tensor_tensor(
                            t2[:, :], t2[:, :], t3[:, :], OP.add),
                            reads=[nm + "t2", nm + "t3"], writes=[nm + "t2"])
                        em.op("scalar", lambda e, h=h, g=g: e.activation(
                            out=dq[:, :], in_=ramp[:, :], func=AF.Exp,
                            scale=lngam[:, h:h + 1],
                            bias=qdecb[:, h * SG + g:h * SG + g + 1]),
                            reads=["ramp512_hf16", "lngam", "qdecb"],
                            writes=[nm + "t3"])
                        em.op("vector", lambda e, h=h, g=g: e.tensor_tensor(
                            qt[:, h * S + g * 512:h * S + (g + 1) * 512],
                            t2[:, :], dq[:, :], OP.mult),
                            reads=[nm + "t2", nm + "t3"],
                            writes=[nm + f"qt{h}_{g}"])
        if dbg is not None and l == layers[0]:
            dbg_keys = {
                "xn": [nm + f"xn{c}_{g}" for c in range(KC) for g in range(SG)],
                "qt": [nm + f"qt{h}_{g}" for h in range(H0) for g in range(SG)],
                "kt": [nm + f"kt{h}_{g}" for h in range(H0) for g in range(SG)],
                "vt": [nm + f"vt{j}_{hf}" for j in range(SB) for hf in range(2)],
            }
            for name, tile in (("xn", xn), ("qt", qt), ("kt", kt), ("vt", vt)):
                em.op("gpsimd", lambda e, name=name, tile=tile: e.dma_start(
                    out=dbg[name][:, :], in_=tile[:, :]),
                    reads=dbg_keys[name], writes=[], dma=True)
        em.barrier()
        load_w(wb, nm + "wb", aps["wg"][l], "WG")
        for ig in range(SG):
            for h in range(H0):
                for k in range(KC):
                    em.op("tensor", lambda e, h=h, ig=ig, k=k: e.matmul(
                        ps[5][:, 0:512],
                        wb[:, k * D + h * 128:k * D + (h + 1) * 128],
                        xn[:, k * S + ig * 512:k * S + (ig + 1) * 512],
                        start=(k == 0), stop=(k == KC - 1)),
                        reads=[xnk(k, ig), nm + "wb"],
                        writes=(["pb5"] if k in (0, KC - 1) else []))
                em.op("scalar", lambda e, h=h: e.activation(
                    out=gg[:, h * 512:(h + 1) * 512], in_=ps[5][:, 0:512],
                    func=AF.Silu),
                    reads=["pb5"], writes=[nm + f"gg{h}"])
            for h in range(H0):
                jmax = 4 * ig + 3
                for j in range(jmax + 1):
                    off = max(0, (j - 4 * ig) * 128)
                    w = 512 - off
                    em.op("tensor", lambda e, h=h, ig=ig, j=j, off=off, w=w: e.matmul(
                        ps[6][:, 0:w],
                        kt[:, h * S + j * 128:h * S + (j + 1) * 128],
                        qt[:, h * S + ig * 512 + off:h * S + (ig + 1) * 512],
                        start=True, stop=True),
                        reads=[nm + f"kt{h}_{j // 4}", nm + f"qt{h}_{ig}"],
                        writes=["pb6"])
                    em.op("scalar", lambda e, h=h, j=j, off=off, w=w: e.activation(
                        out=atb[:, off:off + w], in_=ps[6][:, 0:w],
                        func=AF.Identity, scale=attev[:, h * SB + j:h * SB + j + 1]),
                        reads=["pb6", "attev"], writes=[nm + "atb"])
                    if j >= 4 * ig:
                        em.op("vector", lambda e, off=off: e.tensor_tensor(
                            atb[:, off:off + 128], atb[:, off:off + 128],
                            diagm[:, :], OP.mult),
                            reads=[nm + "atb", "diagmask_bf"], writes=[nm + "atb"])
                        if off > 0:
                            em.op("vector", lambda e, off=off: e.memset(
                                atb[:, 0:off], 0.0),
                                writes=[nm + "atb"])
                    em.op("tensor", lambda e, h=h, j=j, jmax=jmax: e.matmul(
                        ps[7][:, 0:512],
                        vt[:, j * D + h * 128:j * D + (h + 1) * 128],
                        atb[:, 0:512],
                        start=(j == 0), stop=(j == jmax)),
                        reads=[nm + f"vt{j}_{h // 4}", nm + "atb"],
                        writes=(["pb7"] if j in (0, jmax) else []))
                em.op("scalar", lambda e: e.activation(
                    out=ybf[:, :], in_=ps[7][:, 0:512], func=AF.Copy),
                    reads=["pb7"], writes=[nm + "ybf"])
                em.op("scalar", lambda e: e.activation(
                    out=ysq[:, :], in_=ybf[:, :], func=AF.Square),
                    reads=[nm + "ybf"], writes=[nm + "ysq"])
                em.op("tensor", lambda e: e.matmul(
                    ps[0][:, 0:512], ones[:, :], ybf[:, :], start=True, stop=True),
                    reads=[nm + "ybf", "ones_bf"], writes=["pb0"])
                em.op("tensor", lambda e: e.matmul(
                    ps[1][:, 0:512], ones[:, :], ysq[:, :], start=True, stop=True),
                    reads=[nm + "ysq", "ones_bf"], writes=["pb1"])
                em.op("vector", lambda e: e.tensor_scalar_mul(
                    mum[:, :], ps[0][:, 0:512], 1.0 / DH0),
                    reads=["pb0"], writes=[nm + "mu"])
                em.op("vector", lambda e: e.tensor_scalar_mul(
                    msq[:, :], ps[1][:, 0:512], 1.0 / DH0),
                    reads=["pb1"], writes=[nm + "msq"])
                em.op("vector", lambda e: e.tensor_tensor(
                    tf[:, :], mum[:, :], mum[:, :], OP.mult),
                    reads=[nm + "mu"], writes=[nm + "tf"])
                em.op("vector", lambda e: e.tensor_tensor(
                    rst[:, :], msq[:, :], tf[:, :], OP.subtract),
                    reads=[nm + "msq", nm + "tf"], writes=[nm + "rst"])
                em.op("scalar", lambda e: e.activation(
                    out=rst[:, :], in_=rst[:, :], func=AF.Ln, bias=epsg[:, 0:1]),
                    reads=[nm + "rst", "cp_epsgn"], writes=[nm + "rst"])
                em.op("scalar", lambda e: e.activation(
                    out=rst[:, :], in_=rst[:, :], func=AF.Exp, scale=mhalf[:, 0:1]),
                    reads=[nm + "rst", "cp_mhalf"], writes=[nm + "rst"])
                em.op("vector", lambda e: e.tensor_tensor(
                    tf[:, :], ybf[:, :], mum[:, :], OP.subtract),
                    reads=[nm + "ybf", nm + "mu"], writes=[nm + "tf"])
                em.op("vector", lambda e: e.tensor_tensor(
                    tf[:, :], tf[:, :], rst[:, :], OP.mult),
                    reads=[nm + "tf", nm + "rst"], writes=[nm + "tf"])
                em.op("scalar", lambda e, l=l, h=h: e.activation(
                    out=t2[:, :], in_=tf[:, :], func=AF.Identity,
                    scale=gnp[:, l * 2 * KC + h:l * 2 * KC + h + 1],
                    bias=gnp[:, l * 2 * KC + KC + h:l * 2 * KC + KC + h + 1]),
                    reads=[nm + "tf", "gnp"], writes=[nm + "t2"])
                em.op("vector", lambda e, h=h: e.tensor_tensor(
                    gg[:, h * 512:(h + 1) * 512], gg[:, h * 512:(h + 1) * 512],
                    t2[:, :], OP.mult),
                    reads=[nm + f"gg{h}", nm + "t2"], writes=[nm + f"gg{h}"])
            xi_load(xg2, nm + "xg2", src_ap, src_key, ig)
            for hhalf in range(2):
                em.op("gpsimd", lambda e, l=l, hhalf=hhalf: e.dma_start(
                    out=wob[:, :].rearrange("p (k d) -> p k d", k=KC // 2),
                    in_=aps["wo"][l][hhalf * 512:(hhalf + 1) * 512, :]
                    .rearrange("(k p) d -> p k d", p=128)),
                    reads=["WO"], writes=[nm + "wob"], dma=True)
                for m in range(KC):
                    for hh in range(H0 // 2):
                        h = hhalf * 4 + hh
                        em.op("tensor", lambda e, m=m, h=h, hh=hh: e.matmul(
                            ps[m][:, 0:512],
                            wob[:, hh * D + m * 128:hh * D + (m + 1) * 128],
                            gg[:, h * 512:(h + 1) * 512],
                            start=(h == 0), stop=(h == H0 - 1)),
                            reads=[nm + f"gg{h}", nm + "wob"],
                            writes=(["pb%d" % m] if h in (0, H0 - 1) else []))
            for m in range(KC):
                em.op("vector", lambda e, m=m: e.tensor_tensor(
                    xg2[:, m * 512:(m + 1) * 512], ps[m][:, 0:512],
                    xg2[:, m * 512:(m + 1) * 512], OP.add),
                    reads=["pb%d" % m], writes=[nm + "xg2"])
            em.op("gpsimd", lambda e, ig=ig: e.dma_start(
                out=aps["yr"][:, :, ig * 512:(ig + 1) * 512]
                .rearrange("k p t -> p k t"),
                in_=xg2[:, :].rearrange("p (k t) -> p k t", k=KC)),
                reads=[nm + "xg2"], writes=["YR"], dma=True)
        em.barrier()
        ln_stats(aps["yr"], "YR")
        load_w(w1b, nm + "w1b", aps["w1"][l], "W1")
        em.op("gpsimd", lambda e, l=l: e.dma_start(
            out=w2b[:, :].rearrange("p (k d) -> p k d", k=F0 // 128),
            in_=aps["w2"][l].rearrange("(k p) d -> p k d", p=128)),
            reads=["W2"], writes=[nm + "w2b"], dma=True)
        if li == len(layers) - 1 and "sout" in aps:
            dst_ap = aps["sout"]
            dst_key = "SOUT"
        else:
            dst_ap = aps["xi"][(l + 1) % 2]
            dst_key = f"XI{(l + 1) % 2}"
        for g in range(SG):
            xi_load(xg, nm + "xg", aps["yr"], "YR", g)
            for c in range(KC):
                em.op("vector", lambda e, c=c, g=g: e.tensor_tensor(
                    tf[:, :], xg[:, c * 512:(c + 1) * 512],
                    Aall[:, g * 512:(g + 1) * 512], OP.mult),
                    reads=[nm + "xg", nm + f"A{g}"], writes=[nm + "tf"])
                em.op("vector", lambda e, c=c, g=g: e.tensor_tensor(
                    tf[:, :], tf[:, :], Ball[:, g * 512:(g + 1) * 512], OP.add),
                    reads=[nm + "tf", nm + f"B{g}"], writes=[nm + "tf"])
                em.op("scalar", lambda e, c=c, g=g: e.activation(
                    out=xn[:, c * S + g * 512:c * S + (g + 1) * 512],
                    in_=tf[:, :], func=AF.Identity, scale=lnp[:, c:c + 1],
                    bias=lnp[:, KC + c:KC + c + 1]),
                    reads=[nm + "tf", "lnp"], writes=[nm + f"xn{c}_{g}"])
            for mh in range(F0 // 128):
                for k in range(KC):
                    em.op("tensor", lambda e, mh=mh, g=g, k=k: e.matmul(
                        ps[3][:, 0:512],
                        w1b[:, k * F0 + mh * 128:k * F0 + (mh + 1) * 128],
                        xn[:, k * S + g * 512:k * S + (g + 1) * 512],
                        start=(k == 0), stop=(k == KC - 1)),
                        reads=[xnk(k, g), nm + "w1b"],
                        writes=(["pb3"] if k in (0, KC - 1) else []))
                em.op("scalar", lambda e, mh=mh: e.activation(
                    out=hb[:, mh * 512:(mh + 1) * 512], in_=ps[3][:, 0:512],
                    func=AF.Gelu_apprx_tanh),
                    reads=["pb3"], writes=[nm + f"hb{mh}"])
            for m in range(KC):
                for k2 in range(F0 // 128):
                    em.op("tensor", lambda e, m=m, k2=k2: e.matmul(
                        ps[4][:, 0:512],
                        w2b[:, k2 * D + m * 128:k2 * D + (m + 1) * 128],
                        hb[:, k2 * 512:(k2 + 1) * 512],
                        start=(k2 == 0), stop=(k2 == F0 // 128 - 1)),
                        reads=[nm + f"hb{k2}", nm + "w2b"],
                        writes=(["pb4"] if k2 in (0, F0 // 128 - 1) else []))
                em.op("vector", lambda e, m=m: e.tensor_tensor(
                    xg[:, m * 512:(m + 1) * 512], ps[4][:, 0:512],
                    xg[:, m * 512:(m + 1) * 512], OP.add),
                    reads=["pb4", nm + "xg"], writes=[nm + "xg"])
            em.op("gpsimd", lambda e, g=g, dst_ap=dst_ap: e.dma_start(
                out=dst_ap[:, :, g * 512:(g + 1) * 512]
                .rearrange("k p t -> p k t"),
                in_=xg[:, :].rearrange("p (k t) -> p k t", k=KC)),
                reads=[nm + "xg"], writes=[dst_key], dma=True)
        em.barrier()


def setup_common_consts(em, dcon):
    for name, ap in dcon.items():
        dt = BF16 if name.endswith("_bf") else (F16 if name.endswith("_hf16") else F32)
        t = em.sbuf("c_" + name, [ap.shape[0], ap.shape[1]], dt)
        em.sb[name] = t
        eng = "gpsimd" if dt != F32 else "sync"
        em.op(eng, lambda e, t=t, ap=ap: e.dma_start(out=t[:, :], in_=ap[:, :]),
              writes=[name], dma=True)
    for nmv, v in [("cp_epsln", EPS_LN), ("cp_mhalf", -0.5), ("cp_epsgn", EPS_GN),
                   ("cp_epsl1", EPS_L1)]:
        t = em.sbuf(nmv, [128, 1], F32)
        em.op("vector", lambda e, t=t, v=v: e.memset(t[:, :], v), writes=[nmv])
    t = em.sbuf("ones_bf", [128, 128], BF16)
    em.op("vector", lambda e, t=t: e.memset(t[:, :], 1.0), writes=["ones_bf"])
    t = em.sbuf("ones_hf", [128, 128], F16)
    em.op("vector", lambda e, t=t: e.memset(t[:, :], 1.0), writes=["ones_hf"])


def l0_const_arrays():
    c = {}
    pos = np.arange(S, dtype=np.float64)
    inv = 1.0 / (10000.0 ** (np.arange(0, DH0, 2, dtype=np.float64) / DH0))
    ang = pos[None, :] * inv[:, None]
    c["rope_cos_bf"] = np.concatenate([np.cos(ang)] * 2, 0).astype(np.float32)
    sn = np.sin(ang)
    c["rope_sinq_bf"] = np.concatenate([-sn, sn], 0).astype(np.float32)
    P = np.zeros((128, 128), dtype=np.float32)
    for i in range(64):
        P[i, 64 + i] = 1.0
        P[64 + i, i] = 1.0
    c["swapT_bf"] = P.T.copy()
    c["diagmask_bf"] = (np.arange(128)[None, :] >= np.arange(128)[:, None]).astype(np.float32)
    c["ramp512_hf16"] = np.repeat(np.arange(512, dtype=np.float64)[None, :], 128, 0).astype(np.float32)
    c["cjramp"] = np.arange(128, dtype=np.float64)[:, None].astype(np.float32)
    c["hidx"] = np.repeat(np.arange(H0, dtype=np.float64)[None, :], 128, 0).astype(np.float32)
    c["j128"] = np.repeat((128.0 * np.arange(SB))[None, :], 128, 0).astype(np.float32)
    c["ig512"] = np.repeat((512.0 * np.arange(SG))[None, :], 128, 0).astype(np.float32)
    cs = np.zeros((128, 4))
    cs[:, 0] = 1.0 / 32
    cs[:, 1] = 1.0 / 512
    cs[:, 2] = 7.0
    c["cseed"] = cs.astype(np.float32)
    return c


def build_l0_test(layers=(0,), dbg_names=()):
    nc = bass.Bass("TRN2", target_bir_lowering=False, debug=False)
    xi0 = nc.dram_tensor("xi0", [KC, 128, S], F32, kind="ExternalInput").ap()
    wq = nc.dram_tensor("wq", [L0L, D, D], F32, kind="ExternalInput").ap()
    wk = nc.dram_tensor("wk", [L0L, D, D], F32, kind="ExternalInput").ap()
    wv = nc.dram_tensor("wv", [L0L, D, D], F32, kind="ExternalInput").ap()
    wg = nc.dram_tensor("wg", [L0L, D, D], F32, kind="ExternalInput").ap()
    wo = nc.dram_tensor("wo", [L0L, D, D], F32, kind="ExternalInput").ap()
    w1 = nc.dram_tensor("w1", [L0L, D, F0], F32, kind="ExternalInput").ap()
    w2 = nc.dram_tensor("w2", [L0L, F0, D], F32, kind="ExternalInput").ap()
    lnp = nc.dram_tensor("lnp_d", [128, 2 * KC], F32, kind="ExternalInput").ap()
    gnp = nc.dram_tensor("gnp_d", [128, L0L * 2 * KC], F32, kind="ExternalInput").ap()
    carr = l0_const_arrays()
    dcon = {k: nc.dram_tensor("dc_" + k, list(v.shape), F32,
                              kind="ExternalInput").ap()
            for k, v in carr.items()}
    dbgshapes = {"xn": [128, KC * S], "qt": [128, H0 * S], "kt": [128, H0 * S],
                 "vt": [128, SB * D]}
    dbg = {k: nc.dram_tensor("dbg_" + k, dbgshapes[k], F32,
                             kind="ExternalOutput").ap() for k in dbg_names} or None
    xi_a = nc.dram_tensor("xi_a", [KC, 128, S], F16, kind="ExternalOutput").ap()
    xi_b = nc.dram_tensor("xi_b", [KC, 128, S], F16, kind="ExternalOutput").ap()
    yr = nc.dram_tensor("yr_s", [KC, 128, S], F16, kind="ExternalOutput").ap()
    with ExitStack() as ctx:
        em = Em(nc, ctx)
        for i in range(8):
            em.psum_bank(f"pb{i}")
        setup_common_consts(em, dcon)
        em.sb["lnp"] = em.sbuf("lnp", [128, 2 * KC], F32)
        em.op("sync", lambda e: e.dma_start(out=em.sb["lnp"][:, :], in_=lnp[:, :]),
              writes=["lnp"], dma=True)
        em.sb["gnp"] = em.sbuf("gnp", [128, L0L * 2 * KC], F32)
        em.op("sync", lambda e: e.dma_start(out=em.sb["gnp"][:, :], in_=gnp[:, :]),
              writes=["gnp"], dma=True)
        emit_decay_tables(em)
        aps = {"xi0": xi0, "xi": [xi_a, xi_b], "yr": yr,
               "wq": wq, "wk": wk, "wv": wv, "wg": wg, "wo": wo,
               "w1": w1, "w2": w2}
        emit_l0_stack(em, 0, aps, layers=list(layers), dbg=dbg)
        blk = ctx.enter_context(nc.Block())
        em.finalize(blk)
    return nc, carr


def emit_gate_combine(em, aps):
    sbuf = em.sbuf
    nm = "gc_"
    wraw = sbuf(nm + "wraw", [128, KC * D], BF16)
    wT = sbuf(nm + "wT", [128, KC * D], BF16)
    xq = sbuf(nm + "xq", [128, KC * 512], F32)
    cb = sbuf(nm + "cb", [128, KC * 512], F32)
    qbf = sbuf(nm + "qbf", [128, KC * 512], BF16)
    xc = sbuf(nm + "xc", [128, KC * 512], F32)
    sc = sbuf(nm + "sc", [128, KC * 512], F16)
    gt = sbuf(nm + "gt", [128, 512], BF16)
    mxt = sbuf(nm + "mxt", [1, 8], F32)
    ones1 = sbuf(nm + "ones1", [1, 128], F32)
    vecs = sbuf(nm + "vecs", [128, 4], F32)
    wsc = sbuf(nm + "wsc", [128, KC], F32)
    wsx = sbuf(nm + "wsx", [128, KC], F32)
    wbi = sbuf(nm + "wbi", [128, KC], F32)
    identb = em.sb["ident_bf"]
    ps = [em.ps[f"pb{i}"] for i in range(7)]
    pbt = em.ps["pbt"]
    cb2 = em.sb[nm + "cb2"]

    em.op("vector", lambda e: e.memset(ones1[:, :], 1.0), writes=[nm + "ones1"])
    nmx = aps["mxs"].shape[1]
    em.op("sync", lambda e: e.dma_start(out=mxt[0:1, 0:nmx], in_=aps["mxs"][:, :]),
          writes=[nm + "mxt"], dma=True)
    em.op("sync", lambda e: e.dma_start(out=wsc[:, :], in_=aps["wsc"][:, :]),
          writes=[nm + "wsc"], dma=True)
    em.op("sync", lambda e: e.dma_start(out=wbi[:, :], in_=aps["wbias"][:, :]),
          writes=[nm + "wbi"], dma=True)
    em.op("vector", lambda e: e.reduce_max(mxt[0:1, 0:1], mxt[0:1, 0:nmx], axis=AX.X),
          reads=[nm + "mxt"], writes=[nm + "m1"])
    em.op("vector", lambda e: e.tensor_scalar(mxt[0:1, 1:2], mxt[0:1, 0:1],
                                              1.0 / 127.0, EPS_LN, OP.mult, OP.max),
          reads=[nm + "m1"], writes=[nm + "xs"])
    em.op("vector", lambda e: e.reciprocal(mxt[0:1, 2:3], mxt[0:1, 1:2]),
          reads=[nm + "xs"], writes=[nm + "rec"])
    em.op("tensor", lambda e: e.matmul(ps[0][:, 0:2], ones1[:, :],
                                       mxt[0:1, 1:3], start=True, stop=True),
          reads=[nm + "xs", nm + "rec", nm + "ones1"], writes=["pb0"])
    em.op("scalar", lambda e: e.activation(out=vecs[:, 0:2], in_=ps[0][:, 0:2],
                                           func=AF.Copy),
          reads=["pb0"], writes=[nm + "vecs"])
    em.op("vector", lambda e: e.tensor_scalar_mul(wsx[:, :], wsc[:, :],
                                                  vecs[:, 0:1]),
          reads=[nm + "wsc", nm + "vecs"], writes=[nm + "wsx"])
    em.op("gpsimd", lambda e: e.dma_start(
        out=wraw[:, :].rearrange("p (n k) -> p n k", n=KC),
        in_=aps["wint8"].rearrange("(n p) k -> p n k", p=128)),
        writes=[nm + "wraw"], dma=True)
    for n in range(KC):
        for k in range(KC):
            em.op("tensor", lambda e, n=n, k=k: e.transpose(
                pbt[:, 0:128],
                wraw[:, n * D + k * 128:n * D + (k + 1) * 128], identb[:, :]),
                reads=[nm + "wraw", "ident_bf"], writes=["pbt"])
            em.op("scalar", lambda e, n=n, k=k: e.activation(
                out=wT[:, k * D + n * 128:k * D + (n + 1) * 128],
                in_=pbt[:, 0:128], func=AF.Copy),
                reads=["pbt"], writes=[nm + f"wT{k}_{n}"])
    wTkeys = [nm + f"wT{k}_{n}" for k in range(KC) for n in range(KC)]
    for g in range(TG):
        em.op("sync", lambda e, g=g: e.dma_start(
            out=xq[:, :].rearrange("p (k t) -> p k t", k=KC),
            in_=aps["xn"][:, :, g * 512:(g + 1) * 512].rearrange("k p t -> p k t")),
            reads=[], writes=[nm + "xq"], dma=True)
        em.op("vector", lambda e: e.tensor_scalar_mul(xq[:, :], xq[:, :],
                                                      vecs[:, 1:2]),
            reads=[nm + "xq", nm + "vecs"], writes=[nm + "xq"])
        em.op("vector", lambda e: e.tensor_scalar_add(cb[:, :], xq[:, :], 128.5),
              reads=[nm + "xq"], writes=[nm + "cb"])
        em.op("vector", lambda e: e.tensor_scalar_add(xq[:, :], cb[:, :], 0.0),
              reads=[nm + "cb"], writes=[nm + "xq"])
        for bit in range(7, -1, -1):
            em.op("vector", lambda e, bit=bit: e.tensor_scalar(
                cb2[:, :],
                xq[:, :], float(2 ** bit), float(2 ** bit), OP.is_ge, OP.mult),
                reads=[nm + "xq"], writes=[nm + "cb2"])
            em.op("vector", lambda e: e.tensor_tensor(
                xq[:, :], xq[:, :], cb2[:, :], OP.subtract),
                reads=[nm + "xq", nm + "cb2"], writes=[nm + "xq"])
        em.op("vector", lambda e: e.tensor_tensor(cb[:, :], cb[:, :], xq[:, :],
                                                  OP.subtract),
              reads=[nm + "cb", nm + "xq"], writes=[nm + "cb"])
        em.op("vector", lambda e: e.tensor_scalar(cb[:, :], cb[:, :], -128.0, 127.0,
                                                  OP.add, OP.min),
              reads=[nm + "cb"], writes=[nm + "cb"])
        em.op("vector", lambda e: e.tensor_scalar_max(qbf[:, :], cb[:, :], -128.0),
              reads=[nm + "cb"], writes=[nm + "qbf"])
        em.op("sync", lambda e, g=g: e.dma_start(
            out=xc[:, :].rearrange("p (k t) -> p k t", k=KC),
            in_=aps["x"][:, :, g * 512:(g + 1) * 512].rearrange("k p t -> p k t")),
            writes=[nm + "xc"], dma=True)
        em.op("sync", lambda e, g=g: e.dma_start(
            out=sc[:, :].rearrange("p (k t) -> p k t", k=KC),
            in_=aps["s0"][:, :, g * 512:(g + 1) * 512].rearrange("k p t -> p k t")),
            writes=[nm + "sc"], dma=True)
        for m in range(KC):
            for k in range(KC):
                em.op("tensor", lambda e, m=m, k=k: e.matmul(
                    ps[2][:, 0:512],
                    wT[:, k * D + m * 128:k * D + (m + 1) * 128],
                    qbf[:, k * 512:(k + 1) * 512],
                    start=(k == 0), stop=(k == KC - 1)),
                    reads=[nm + "qbf"] + ([w for w in wTkeys] if k == 0 and m == 0 else []),
                    writes=(["pb2"] if k in (0, KC - 1) else []))
            em.op("scalar", lambda e, m=m: e.activation(
                out=gt[:, :], in_=ps[2][:, 0:512], func=AF.Silu,
                scale=wsx[:, m:m + 1], bias=wbi[:, m:m + 1]),
                reads=["pb2", nm + "wsx", nm + "wbi"], writes=[nm + "gt"])
            em.op("vector", lambda e, m=m: e.tensor_tensor(
                sc[:, m * 512:(m + 1) * 512], sc[:, m * 512:(m + 1) * 512],
                gt[:, :], OP.mult),
                reads=[nm + "sc", nm + "gt"], writes=[nm + "sc"])
            em.op("vector", lambda e, m=m: e.tensor_tensor(
                xc[:, m * 512:(m + 1) * 512], xc[:, m * 512:(m + 1) * 512],
                sc[:, m * 512:(m + 1) * 512], OP.add),
                reads=[nm + "xc", nm + "sc"], writes=[nm + "xc"])
        em.op("gpsimd", lambda e, g=g: e.dma_start(
            out=aps["x1"][:, :, g * 512:(g + 1) * 512].rearrange("k p t -> p k t"),
            in_=xc[:, :].rearrange("p (k t) -> p k t", k=KC)),
            reads=[nm + "xc"], writes=["X1"], dma=True)


def build_gc_test():
    nc = bass.Bass("TRN2", target_bir_lowering=False, debug=False)
    xn = nc.dram_tensor("xn", [KC, 128, T], F32, kind="ExternalInput").ap()
    x = nc.dram_tensor("x", [KC, 128, T], F32, kind="ExternalInput").ap()
    s0 = nc.dram_tensor("s0", [KC, 128, T], F16, kind="ExternalInput").ap()
    wint8 = nc.dram_tensor("wint8", [D, D], mybir.dt.int8, kind="ExternalInput").ap()
    wsc = nc.dram_tensor("wsc", [128, KC], F32, kind="ExternalInput").ap()
    wbias = nc.dram_tensor("wbias", [128, KC], F32, kind="ExternalInput").ap()
    mxs = nc.dram_tensor("mxs", [1, 8], F32, kind="ExternalInput").ap()
    identb_d = nc.dram_tensor("identb_d", [128, 128], F32, kind="ExternalInput").ap()
    x1 = nc.dram_tensor("x1", [KC, 128, T], F32, kind="ExternalOutput").ap()
    with ExitStack() as ctx:
        em = Em(nc, ctx)
        for i in range(7):
            em.psum_bank(f"pb{i}")
        t = em.ctx.enter_context(nc.psum_tensor("pbt", [128, 1024], BF16))
        em.ps["pbt"] = t
        tb = em.sbuf("ident_bf", [128, 128], BF16)
        em.op("gpsimd", lambda e: e.dma_start(out=tb[:, :], in_=identb_d[:, :]),
              writes=["ident_bf"], dma=True)
        em.sbuf("gc_cb2", [128, KC * 512], F32)
        emit_gate_combine(em, {"xn": xn, "x": x, "s0": s0, "wint8": wint8,
                               "wsc": wsc, "wbias": wbias, "mxs": mxs, "x1": x1})
        blk = ctx.enter_context(nc.Block())
        em.finalize(blk)
    return nc


def emit_l1_attn(em, b, aps):
    sbuf = em.sbuf
    nm = "l1_"
    qb = sbuf(nm + "q", [128, KC * S], BF16)
    kb = sbuf(nm + "k", [128, KC * S], BF16)
    vb = sbuf(nm + "v", [128, KC * S], BF16)
    xnb = sbuf(nm + "xn", [128, KC * S], BF16)
    wrw = sbuf(nm + "wrw", [128, D], BF16)
    wtr = sbuf(nm + "wtr", [128, D], BF16)
    wpT = sbuf(nm + "wpT", [128, KC * D], BF16)
    zer = sbuf(nm + "zeros", [128, S], BF16)
    xg = sbuf(nm + "xg", [128, KC * 512], F16)
    yg = sbuf(nm + "yg", [128, KC * 512], F32)
    Aall = sbuf(nm + "A", [128, S], BF16)
    Ball = sbuf(nm + "B", [128, S], BF16)
    t1 = sbuf(nm + "t1", [128, 512], F32)
    t2 = sbuf(nm + "t2", [128, 512], F32)
    tb1 = sbuf(nm + "tb1", [128, 512], BF16)
    mum = sbuf(nm + "mu", [128, 512], F32)
    msq = sbuf(nm + "msq", [128, 512], F32)
    scv = sbuf(nm + "scv", [128, 2], F32)
    bqs = sbuf(nm + "bqs", [128, 2 * KC], F32)
    ones1 = sbuf(nm + "ones1", [1, 128], F32)
    bqkv = sbuf(nm + "bqkv", [128, 3 * KC], F32)
    bproj = sbuf(nm + "bproj", [128, KC], F32)
    identb = em.sb["ident_bf"]
    ones_hf = em.sb["ones_hf"]
    blk64 = em.sb["blkones64_bf"]
    epsl = em.sb["cp_epsln"]
    mone = em.sb["cp_mone"]
    mhalf = em.sb["cp_mhalf"]
    ps = [em.ps[f"pb{i}"] for i in range(7)]
    pbt = em.ps["pbt"]

    em.op("vector", lambda e: e.memset(ones1[:, :], 1.0), writes=[nm + "ones1"])
    em.op("vector", lambda e: e.memset(zer[:, :], 0.0), writes=[nm + "zeros"])
    em.op("sync", lambda e: e.dma_start(out=bqkv[:, :], in_=aps["bqkv"][:, :]),
          writes=[nm + "bqkv"], dma=True)
    em.op("sync", lambda e: e.dma_start(out=bproj[:, :], in_=aps["bproj"][:, :]),
          writes=[nm + "bproj"], dma=True)
    em.op("sync", lambda e: e.dma_start(out=scv[0:1, 0:1], in_=aps["scl"][:, :]),
          writes=[nm + "scv0"], dma=True)
    em.op("tensor", lambda e: e.matmul(ps[0][:, 0:1], ones1[:, :], scv[0:1, 0:1],
                                       start=True, stop=True),
          reads=[nm + "scv0", nm + "ones1"], writes=["pb0"])
    em.op("scalar", lambda e: e.activation(out=scv[:, 1:2], in_=ps[0][:, 0:1],
                                           func=AF.Copy),
          reads=["pb0"], writes=[nm + "scv"])
    em.op("vector", lambda e: e.tensor_scalar_mul(bqs[:, :], bqkv[:, 0:2 * KC],
                                                  scv[:, 1:2]),
          reads=[nm + "bqkv", nm + "scv"], writes=[nm + "bqs"])

    src_ap, src_key = aps["x1"], "X1"
    off = b * S

    def xi_load(dst, dkey, g):
        em.op("gpsimd", lambda e, g=g, dst=dst: e.dma_start(
            out=dst[:, :].rearrange("p (k t) -> p k t", k=KC),
            in_=src_ap[:, :, off + g * 512:off + (g + 1) * 512]
            .rearrange("k p t -> p k t")),
            reads=[src_key], writes=[dkey], dma=True)

    for g in range(SG):
        xi_load(xg, nm + "xg", g)
        em.op("scalar", lambda e: e.activation(
            out=yg[:, :], in_=xg[:, :], func=AF.Square),
            reads=[nm + "xg"], writes=[nm + "ygs"])
        for c in range(KC):
            em.op("tensor", lambda e, c=c: e.matmul(
                ps[0][:, 0:512], ones_hf[:, :], xg[:, c * 512:(c + 1) * 512],
                start=(c == 0), stop=(c == KC - 1)),
                reads=[nm + "xg", "ones_hf"],
                writes=(["pb0"] if c in (0, KC - 1) else []))
        for c in range(KC):
            em.op("tensor", lambda e, c=c: e.matmul(
                ps[1][:, 0:512], em.sb["ones_f32"][:, :],
                yg[:, c * 512:(c + 1) * 512],
                start=(c == 0), stop=(c == KC - 1)),
                reads=[nm + "ygs", "ones_f32"],
                writes=(["pb1"] if c in (0, KC - 1) else []))
        em.op("vector", lambda e: e.tensor_scalar_mul(mum[:, :], ps[0][:, 0:512],
                                                      1.0 / D),
              reads=["pb0"], writes=[nm + "mu"])
        em.op("vector", lambda e: e.tensor_scalar_mul(msq[:, :], ps[1][:, 0:512],
                                                      1.0 / D),
              reads=["pb1"], writes=[nm + "msq"])
        em.op("vector", lambda e: e.tensor_tensor(t1[:, :], mum[:, :], mum[:, :],
                                                  OP.mult),
              reads=[nm + "mu"], writes=[nm + "t1"])
        em.op("vector", lambda e: e.tensor_tensor(msq[:, :], msq[:, :], t1[:, :],
                                                  OP.subtract),
              reads=[nm + "msq", nm + "t1"], writes=[nm + "msq"])
        em.op("scalar", lambda e: e.activation(out=msq[:, :], in_=msq[:, :],
                                               func=AF.Ln, bias=epsl[:, 0:1]),
              reads=[nm + "msq", "cp_epsln"], writes=[nm + "msq"])
        em.op("scalar", lambda e, g=g: e.activation(
            out=Aall[:, g * 512:(g + 1) * 512], in_=msq[:, :], func=AF.Exp,
            scale=mhalf[:, 0:1]),
            reads=[nm + "msq", "cp_mhalf"], writes=[nm + f"A{g}"])
        em.op("vector", lambda e, g=g: e.scalar_tensor_tensor(
            Ball[:, g * 512:(g + 1) * 512], mum[:, :], -1.0,
            Aall[:, g * 512:(g + 1) * 512], OP.mult, OP.mult),
            reads=[nm + "mu", nm + f"A{g}"], writes=[nm + f"B{g}"])
        for c in range(KC):
            em.op("vector", lambda e, c=c, g=g: e.tensor_tensor(
                t1[:, :], xg[:, c * 512:(c + 1) * 512],
                Aall[:, g * 512:(g + 1) * 512], OP.mult),
                reads=[nm + "xg", nm + f"A{g}"], writes=[nm + "t1"])
            em.op("vector", lambda e, c=c, g=g: e.tensor_tensor(
                xnb[:, c * S + g * 512:c * S + (g + 1) * 512], t1[:, :],
                Ball[:, g * 512:(g + 1) * 512], OP.add),
                reads=[nm + "t1", nm + f"B{g}"], writes=[nm + f"xn{c}_{g}"])
        em.op("gpsimd", lambda e, g=g: e.dma_start(
            out=aps["xn2"][:, :, off + g * 512:off + (g + 1) * 512]
            .rearrange("k p t -> p k t"),
            in_=xnb.rearrange("p (k s) -> p k s", k=KC)[:, :, g * 512:(g + 1) * 512]),
            reads=[nm + f"xn{c}_{g}" for c in range(KC)], writes=["XN2"], dma=True)

    for dc in range(3 * KC):
        em.op("gpsimd", lambda e, dc=dc: e.dma_start(
            out=wrw[:, :], in_=aps["wqkv"][dc * 128:(dc + 1) * 128, :]),
            writes=[nm + "wrw"], dma=True)
        for k in range(KC):
            em.op("tensor", lambda e, k=k: e.transpose(
                pbt[:, 0:128], wrw[:, k * 128:(k + 1) * 128], identb[:, :]),
                reads=[nm + "wrw", "ident_bf"], writes=["pbt"])
            em.op("scalar", lambda e, k=k: e.activation(
                out=wtr[:, k * 128:(k + 1) * 128], in_=pbt[:, 0:128],
                func=AF.Copy),
                reads=["pbt"], writes=[nm + f"wtr{k}"])
        dst = qb if dc < KC else (kb if dc < 2 * KC else vb)
        c = dc % KC
        pref = "q" if dc < KC else ("k" if dc < 2 * KC else "v")
        for g in range(SG):
            for k in range(KC):
                em.op("tensor", lambda e, k=k, g=g: e.matmul(
                    ps[2][:, 0:512], wtr[:, k * 128:(k + 1) * 128],
                    xnb[:, k * S + g * 512:k * S + (g + 1) * 512],
                    start=(k == 0), stop=(k == KC - 1)),
                    reads=[nm + f"wtr{kk}" for kk in range(KC)] + [nm + f"xn{k}_{g}"],
                    writes=(["pb2"] if k in (0, KC - 1) else []))
            if dc < 2 * KC:
                em.op("scalar", lambda e, dc=dc: e.activation(
                    out=t1[:, :], in_=ps[2][:, 0:512], func=AF.Identity,
                    scale=scv[:, 1:2], bias=bqs[:, dc:dc + 1]),
                    reads=["pb2", nm + "scv", nm + "bqs"], writes=[nm + "t1"])
                em.op("vector", lambda e: e.tensor_scalar_min(t2[:, :], t1[:, :], 0.0),
                      reads=[nm + "t1"], writes=[nm + "t2"])
                em.op("scalar", lambda e: e.activation(out=t2[:, :], in_=t2[:, :],
                                                       func=AF.Exp),
                      reads=[nm + "t2"], writes=[nm + "t2"])
                em.op("vector", lambda e, dst=dst, c=c, g=g, pref=pref:
                      e.scalar_tensor_tensor(
                          dst[:, c * S + g * 512:c * S + (g + 1) * 512], t1[:, :],
                          0.0, t2[:, :], OP.max, OP.add),
                      reads=[nm + "t1", nm + "t2"],
                      writes=[nm + f"{pref}{c}_{g}"])
            else:
                em.op("scalar", lambda e, dst=dst, c=c, g=g, dc=dc: e.activation(
                    out=dst[:, c * S + g * 512:c * S + (g + 1) * 512],
                    in_=ps[2][:, 0:512], func=AF.Identity,
                    bias=bqkv[:, dc:dc + 1]),
                    reads=["pb2", nm + "bqkv"], writes=[nm + f"v{c}_{g}"])

    for c in range(KC):
        kkeys = [nm + f"k{c}_{g}" for g in range(SG)]
        vkeys = [nm + f"v{c}_{g}" for g in range(SG)]
        em.op("vector", lambda e, c=c: e.tensor_tensor(
            vb[:, c * S:(c + 1) * S], kb[:, c * S:(c + 1) * S],
            vb[:, c * S:(c + 1) * S], OP.mult),
            reads=kkeys + vkeys, writes=[nm + f"kv{c}"])
        em.op("vector", lambda e, c=c: e.tensor_tensor_scan(
            kb[:, c * S:(c + 1) * S], kb[:, c * S:(c + 1) * S],
            zer[:, 0:S], 0.0, OP.add, OP.add),
            reads=kkeys + [nm + "zeros"], writes=[nm + f"kc{c}"])
        em.op("vector", lambda e, c=c: e.tensor_tensor_scan(
            vb[:, c * S:(c + 1) * S], vb[:, c * S:(c + 1) * S],
            zer[:, 0:S], 0.0, OP.add, OP.add),
            reads=[nm + f"kv{c}", nm + "zeros"], writes=[nm + f"kvc{c}"])

    for g in range(SG):
        for c in range(KC):
            em.op("vector", lambda e, c=c, g=g: e.tensor_tensor(
                tb1[:, :], qb[:, c * S + g * 512:c * S + (g + 1) * 512],
                kb[:, c * S + g * 512:c * S + (g + 1) * 512], OP.mult),
                reads=[nm + f"q{c}_{g}", nm + f"kc{c}"], writes=[nm + "tb1"])
            em.op("tensor", lambda e, c=c: e.matmul(
                ps[3][:, 0:512], blk64[:, :], tb1[:, :], start=True, stop=True),
                reads=[nm + "tb1", "blkones64_bf"], writes=["pb3"])
            em.op("vector", lambda e: e.tensor_scalar_max(
                t1[:, :], ps[3][:, 0:512], EPS_L1),
                reads=["pb3"], writes=[nm + "t1"])
            em.op("scalar", lambda e: e.activation(out=t1[:, :], in_=t1[:, :],
                                                   func=AF.Ln),
                  reads=[nm + "t1"], writes=[nm + "t1"])
            em.op("scalar", lambda e: e.activation(out=t1[:, :], in_=t1[:, :],
                                                   func=AF.Exp,
                                                   scale=mone[:, 0:1]),
                  reads=[nm + "t1", "cp_mone"], writes=[nm + "t1"])
            em.op("vector", lambda e, c=c, g=g: e.tensor_tensor(
                t2[:, :], qb[:, c * S + g * 512:c * S + (g + 1) * 512],
                vb[:, c * S + g * 512:c * S + (g + 1) * 512], OP.mult),
                reads=[nm + f"q{c}_{g}", nm + f"kvc{c}"], writes=[nm + "t2"])
            em.op("vector", lambda e, c=c, g=g: e.tensor_tensor(
                qb[:, c * S + g * 512:c * S + (g + 1) * 512], t2[:, :], t1[:, :],
                OP.mult),
                reads=[nm + "t2", nm + "t1"], writes=[nm + f"y{c}_{g}"])

    for k in range(KC):
        em.op("gpsimd", lambda e, k=k: e.dma_start(
            out=wrw[:, :], in_=aps["wproj"][k * 128:(k + 1) * 128, :]),
            writes=[nm + "wrw"], dma=True)
        for m in range(KC):
            em.op("tensor", lambda e, m=m: e.transpose(
                pbt[:, 0:128], wrw[:, m * 128:(m + 1) * 128], identb[:, :]),
                reads=[nm + "wrw", "ident_bf"], writes=["pbt"])
            em.op("scalar", lambda e, k=k, m=m: e.activation(
                out=wpT[:, k * D + m * 128:k * D + (m + 1) * 128],
                in_=pbt[:, 0:128], func=AF.Copy),
                reads=["pbt"], writes=[nm + f"wpT{k}_{m}"])
    for g in range(SG):
        for m in range(KC):
            for k in range(KC):
                em.op("tensor", lambda e, k=k, m=m, g=g: e.matmul(
                    ps[4][:, 0:512],
                    wpT[:, k * D + m * 128:k * D + (m + 1) * 128],
                    qb[:, k * S + g * 512:k * S + (g + 1) * 512],
                    start=(k == 0), stop=(k == KC - 1)),
                    reads=[nm + f"wpT{k}_{m}", nm + f"y{k}_{g}"],
                    writes=(["pb4"] if k in (0, KC - 1) else []))
            em.op("scalar", lambda e, m=m: e.activation(
                out=xg[:, m * 512:(m + 1) * 512], in_=ps[4][:, 0:512],
                func=AF.Identity, bias=bproj[:, m:m + 1]),
                reads=["pb4", nm + "bproj"], writes=[nm + "xg"])
        em.op("gpsimd", lambda e, g=g: e.dma_start(
            out=aps["s1"][:, :, off + g * 512:off + (g + 1) * 512]
            .rearrange("k p t -> p k t"),
            in_=xg[:, :].rearrange("p (k t) -> p k t", k=KC)),
            reads=[nm + "xg"], writes=["S1"], dma=True)


def build_l1_test():
    nc = bass.Bass("TRN2", target_bir_lowering=False, debug=False)
    x1 = nc.dram_tensor("x1", [KC, 128, S], F32, kind="ExternalInput").ap()
    wqkv = nc.dram_tensor("wqkv", [3 * D, D], F32, kind="ExternalInput").ap()
    bqkv = nc.dram_tensor("bqkv_d", [128, 3 * KC], F32, kind="ExternalInput").ap()
    wproj = nc.dram_tensor("wproj", [D, D], F32, kind="ExternalInput").ap()
    bproj = nc.dram_tensor("bproj_d", [128, KC], F32, kind="ExternalInput").ap()
    scl = nc.dram_tensor("scl", [1, 1], F32, kind="ExternalInput").ap()
    identb_d = nc.dram_tensor("identb_d", [128, 128], F32, kind="ExternalInput").ap()
    blk_d = nc.dram_tensor("blk_d", [128, 128], F32, kind="ExternalInput").ap()
    s1 = nc.dram_tensor("s1", [KC, 128, S], F16, kind="ExternalOutput").ap()
    xn2 = nc.dram_tensor("xn2", [KC, 128, S], BF16, kind="ExternalOutput").ap()
    with ExitStack() as ctx:
        em = Em(nc, ctx)
        for i in range(7):
            em.psum_bank(f"pb{i}")
        t = em.ctx.enter_context(nc.psum_tensor("pbt", [128, 1024], BF16))
        em.ps["pbt"] = t
        for cn, ap2, dt in (("ident_bf", identb_d, BF16), ("blkones64_bf", blk_d, BF16)):
            tb = em.sbuf(cn, [128, 128], dt)
            em.op("gpsimd", lambda e, tb=tb, ap2=ap2: e.dma_start(out=tb[:, :],
                                                                  in_=ap2[:, :]),
                  writes=[cn], dma=True)
        th = em.sbuf("ones_hf", [128, 128], F16)
        em.op("vector", lambda e: e.memset(th[:, :], 1.0), writes=["ones_hf"])
        tf32 = em.sbuf("ones_f32", [128, 128], F32)
        em.op("vector", lambda e: e.memset(tf32[:, :], 1.0), writes=["ones_f32"])
        for nmv, v in (("cp_epsln", EPS_LN), ("cp_mhalf", -0.5), ("cp_mone", -1.0)):
            tv = em.sbuf(nmv, [128, 1], F32)
            em.op("vector", lambda e, tv=tv, v=v: e.memset(tv[:, :], v), writes=[nmv])
        emit_l1_attn(em, 0, {"x1": x1, "wqkv": wqkv, "bqkv": bqkv, "wproj": wproj,
                             "bproj": bproj, "scl": scl, "s1": s1, "xn2": xn2})
        blk = ctx.enter_context(nc.Block())
        em.finalize(blk)
    return nc


def emit_l1_moe(em, aps):
    sbuf = em.sbuf
    nm = "moe_"
    QT = 1024
    NQT = T // QT
    xq = sbuf(nm + "xq", [128, KC * QT], BF16)
    hb = sbuf(nm + "hb", [128, (HID1 // 128) * QT], BF16)
    oa = sbuf(nm + "oa", [128, KC * QT], F16)
    xc = sbuf(nm + "xc", [128, KC * QT], F32)
    sc = sbuf(nm + "sc", [128, KC * QT], F16)
    wroll = sbuf(nm + "wroll", [128, KC * 128], BF16)
    w2roll = sbuf(nm + "w2roll", [128, (HID1 // 128) * 128], BF16)
    wbc = sbuf(nm + "wbc", [128, QT], F16)
    wTq = sbuf(nm + "wTq", [8, QT], F32)
    wrow = sbuf(nm + "wrow", [1, QT], F32)
    brt = sbuf(nm + "brt", [128, E1], F32)
    b1t = sbuf(nm + "b1t", [128, HID1 // 128], F32)
    b2t = sbuf(nm + "b2t", [8, 128], BF16)
    l32 = sbuf(nm + "l32", [128, E1], F32)
    e32 = sbuf(nm + "e32", [128, E1], F32)
    v1 = sbuf(nm + "v1", [128, 4], F32)
    gt = sbuf(nm + "gt", [128, 512], BF16)
    ot = sbuf(nm + "ot", [128, D], F32)
    ones1 = sbuf(nm + "ones1", [1, 128], F32)
    identf = em.sb["ident_f32"]
    ps = [em.ps[f"pb{i}"] for i in range(7)]
    pbt = em.ps["pbt"]
    wTqb = em.sb["moe_wTqb"]
    wrE = em.sb["moe_wrE"]

    em.op("vector", lambda e: e.memset(ones1[:, :], 1.0), writes=[nm + "ones1"])
    em.op("sync", lambda e: e.dma_start(out=brt[:, :], in_=aps["br"][:, :]),
          writes=[nm + "brt"], dma=True)
    for q in range(NQT):
        qoff = q * QT
        em.op("sync", lambda e, qoff=qoff: e.dma_start(
            out=xq[:, :].rearrange("p (k t) -> p k t", k=KC),
            in_=aps["xn2"][:, :, qoff:qoff + QT].rearrange("k p t -> p k t")),
            writes=[nm + "xq"], dma=True)
        for tb in range(QT // 128):
            for k in range(KC):
                em.op("tensor", lambda e, k=k, tb=tb: e.matmul(
                    ps[0][:, 0:E1],
                    xq[:, k * QT + tb * 128:k * QT + (tb + 1) * 128],
                    wrE[:, k * E1:(k + 1) * E1],
                    start=(k == 0), stop=(k == KC - 1)),
                    reads=[nm + "xq", nm + "wrE"],
                    writes=(["pb0"] if k in (0, KC - 1) else []))
            em.op("vector", lambda e: e.tensor_tensor(
                l32[:, :], ps[0][:, 0:E1], brt[:, :], OP.add),
                reads=["pb0", nm + "brt"], writes=[nm + "l32"])
            em.op("vector", lambda e: e.reduce_max(v1[:, 0:1], l32[:, :], axis=AX.X),
                  reads=[nm + "l32"], writes=[nm + "v1c0"])
            em.op("vector", lambda e: e.tensor_scalar_mul(v1[:, 1:2], v1[:, 0:1], -1.0),
                  reads=[nm + "v1c0"], writes=[nm + "v1c1"])
            em.op("vector", lambda e: e.tensor_scalar(
                e32[:, :], l32[:, :], v1[:, 0:1], None, OP.is_ge),
                reads=[nm + "l32", nm + "v1c0"], writes=[nm + "e32"])
            em.op("vector", lambda e: e.scalar_tensor_tensor(
                e32[:, :], e32[:, :], -1.0e30, l32[:, :], OP.mult, OP.add),
                reads=[nm + "e32", nm + "l32"], writes=[nm + "e32"])
            em.op("vector", lambda e: e.reduce_max(v1[:, 2:3], e32[:, :], axis=AX.X),
                  reads=[nm + "e32"], writes=[nm + "v1c2"])
            em.op("scalar", lambda e: e.activation(
                out=e32[:, :], in_=l32[:, :], func=AF.Exp, bias=v1[:, 1:2]),
                reads=[nm + "l32", nm + "v1c1"], writes=[nm + "e32"])
            em.op("vector", lambda e: e.tensor_scalar(
                l32[:, :], l32[:, :], v1[:, 2:3], None, OP.is_ge),
                reads=[nm + "l32", nm + "v1c2"], writes=[nm + "l32"])
            em.op("vector", lambda e: e.tensor_tensor(
                e32[:, :], e32[:, :], l32[:, :], OP.mult),
                reads=[nm + "e32", nm + "l32"], writes=[nm + "e32"])
            em.op("vector", lambda e: e.reduce_sum(v1[:, 3:4], e32[:, :], axis=AX.X),
                  reads=[nm + "e32"], writes=[nm + "v1c3"])
            em.op("vector", lambda e: e.reciprocal(v1[:, 2:3], v1[:, 3:4]),
                  reads=[nm + "v1c3"], writes=[nm + "v1c2"])
            em.op("vector", lambda e: e.tensor_scalar_mul(
                e32[:, :], e32[:, :], v1[:, 2:3]),
                reads=[nm + "e32", nm + "v1c2"], writes=[nm + "e32"])
            em.op("tensor", lambda e: e.transpose(
                ps[1][0:E1, 0:128], e32[:, :], identf[:, :]),
                reads=[nm + "e32", "ident_f32"], writes=["pb1"])
            em.op("scalar", lambda e, tb=tb: e.activation(
                out=wTq[:, tb * 128:(tb + 1) * 128], in_=ps[1][0:E1, 0:128],
                func=AF.Copy),
                reads=["pb1"], writes=[nm + f"wTq{tb}"])
        em.op("gpsimd", lambda e, qoff=qoff: e.dma_start(
            out=aps["wt_scratch"][:, qoff:qoff + QT], in_=wTq[:, :]),
            reads=[nm + f"wTq{tb}" for tb in range(QT // 128)],
            writes=["WTS"], dma=True)
        em.op("sync", lambda e, qoff=qoff: e.dma_start(
            out=xc[:, :].rearrange("p (k t) -> p k t", k=KC),
            in_=aps["x1"][:, :, qoff:qoff + QT].rearrange("k p t -> p k t")),
            writes=[nm + "xc"], dma=True)
        em.op("sync", lambda e, qoff=qoff: e.dma_start(
            out=sc[:, :].rearrange("p (k t) -> p k t", k=KC),
            in_=aps["s1"][:, :, qoff:qoff + QT].rearrange("k p t -> p k t")),
            writes=[nm + "sc"], dma=True)
        em.op("vector", lambda e: e.memset(oa[:, :], 0.0), writes=[nm + "oa"])
        for ex in range(E1):
            em.op("sync", lambda e, ex=ex, qoff=qoff: e.dma_start(
                out=wrow[:, :], in_=aps["wt_scratch"][ex:ex + 1, qoff:qoff + QT]),
                reads=["WTS"], writes=[nm + "wrow"], dma=True)
            for half in range(2):
                em.op("tensor", lambda e, half=half: e.matmul(
                    ps[2][:, 0:512], ones1[:, :],
                    wrow[0:1, half * 512:(half + 1) * 512],
                    start=True, stop=True),
                    reads=[nm + "wrow", nm + "ones1"], writes=["pb2"])
                em.op("scalar", lambda e, half=half: e.activation(
                    out=wbc[:, half * 512:(half + 1) * 512], in_=ps[2][:, 0:512],
                    func=AF.Copy),
                    reads=["pb2"], writes=[nm + "wbc"])
            em.op("sync", lambda e, ex=ex: e.dma_start(
                out=b1t[:, :], in_=aps["b1"][ex]),
                writes=[nm + "b1t"], dma=True)
            for mh in range(HID1 // 128):
                em.op("gpsimd", lambda e, ex=ex, mh=mh: e.dma_start(
                    out=wroll[:, :].rearrange("p (k m) -> p k m", k=KC),
                    in_=aps["w1"][ex, :, mh * 128:(mh + 1) * 128]
                    .rearrange("(k p) m -> p k m", p=128)),
                    reads=[], writes=[nm + "wroll"], dma=True)
                for g2 in range(2):
                    for k in range(KC):
                        em.op("tensor", lambda e, k=k, g2=g2, mh=mh: e.matmul(
                            ps[3][:, 0:512],
                            wroll[:, k * 128:(k + 1) * 128],
                            xq[:, k * QT + g2 * 512:k * QT + (g2 + 1) * 512],
                            start=(k == 0), stop=(k == KC - 1)),
                            reads=[nm + "wroll", nm + "xq"],
                            writes=(["pb3"] if k in (0, KC - 1) else []))
                    em.op("scalar", lambda e, mh=mh, g2=g2: e.activation(
                        out=gt[:, :], in_=ps[3][:, 0:512], func=AF.Relu,
                        bias=b1t[:, mh:mh + 1]),
                        reads=["pb3", nm + "b1t"], writes=[nm + "gt"])
                    em.op("vector", lambda e, mh=mh, g2=g2: e.tensor_tensor(
                        hb[:, mh * QT + g2 * 512:mh * QT + (g2 + 1) * 512],
                        gt[:, :], wbc[:, g2 * 512:(g2 + 1) * 512], OP.mult),
                        reads=[nm + "gt", nm + "wbc"], writes=[nm + f"hb{mh}_{g2}"])
            for m in range(KC):
                em.op("gpsimd", lambda e, ex=ex, m=m: e.dma_start(
                    out=w2roll[:, :].rearrange("p (k m) -> p k m", k=HID1 // 128),
                    in_=aps["w2"][ex, :, m * 128:(m + 1) * 128]
                    .rearrange("(k p) m -> p k m", p=128)),
                    reads=[], writes=[nm + "w2roll"], dma=True)
                for g2 in range(2):
                    for k2 in range(HID1 // 128):
                        em.op("tensor", lambda e, k2=k2, m=m, g2=g2: e.matmul(
                            ps[4][:, 0:512],
                            w2roll[:, k2 * 128:(k2 + 1) * 128],
                            hb[:, k2 * QT + g2 * 512:k2 * QT + (g2 + 1) * 512],
                            start=(k2 == 0), stop=(k2 == HID1 // 128 - 1)),
                            reads=[nm + "w2roll", nm + f"hb{k2}_{g2}"],
                            writes=(["pb4"] if k2 in (0, HID1 // 128 - 1) else []))
                    em.op("vector", lambda e, m=m, g2=g2: e.tensor_tensor(
                        oa[:, m * QT + g2 * 512:m * QT + (g2 + 1) * 512],
                        oa[:, m * QT + g2 * 512:m * QT + (g2 + 1) * 512],
                        ps[4][:, 0:512], OP.add),
                        reads=[nm + "oa", "pb4"], writes=[nm + "oa"])
        em.op("vector", lambda e: e.tensor_copy(wTqb[:, :], wTq[:, :]),
              reads=[nm + f"wTq{tb}" for tb in range(QT // 128)],
              writes=[nm + "wTqb"], dma=False)
        for m in range(KC):
            em.op("gpsimd", lambda e, m=m: e.dma_start(
                out=b2t[:, :], in_=aps["b2"][:, m * 128:(m + 1) * 128]),
                writes=[nm + "b2t"], dma=True)
            for g2 in range(2):
                em.op("tensor", lambda e, m=m, g2=g2: e.matmul(
                    ps[5][:, 0:512], b2t[:, :],
                    wTqb[:, g2 * 512:(g2 + 1) * 512],
                    start=True, stop=True),
                    reads=[nm + "b2t", nm + "wTqb"], writes=["pb5"])
                em.op("vector", lambda e, m=m, g2=g2: e.tensor_tensor(
                    oa[:, m * QT + g2 * 512:m * QT + (g2 + 1) * 512],
                    oa[:, m * QT + g2 * 512:m * QT + (g2 + 1) * 512],
                    ps[5][:, 0:512], OP.add),
                    reads=[nm + "oa", "pb5"], writes=[nm + "oa"])
        for m in range(KC):
            for g2 in range(2):
                em.op("scalar", lambda e, m=m, g2=g2: e.activation(
                    out=gt[:, :], in_=oa[:, m * QT + g2 * 512:m * QT + (g2 + 1) * 512],
                    func=AF.Silu),
                    reads=[nm + "oa"], writes=[nm + "gt"])
                em.op("vector", lambda e, m=m, g2=g2: e.tensor_tensor(
                    sc[:, m * QT + g2 * 512:m * QT + (g2 + 1) * 512],
                    sc[:, m * QT + g2 * 512:m * QT + (g2 + 1) * 512],
                    gt[:, :], OP.mult),
                    reads=[nm + "sc", nm + "gt"], writes=[nm + "sc"])
                em.op("vector", lambda e, m=m, g2=g2: e.tensor_tensor(
                    xc[:, m * QT + g2 * 512:m * QT + (g2 + 1) * 512],
                    xc[:, m * QT + g2 * 512:m * QT + (g2 + 1) * 512],
                    sc[:, m * QT + g2 * 512:m * QT + (g2 + 1) * 512], OP.add),
                    reads=[nm + "xc", nm + "sc"], writes=[nm + "xc"])
        for tb in range(QT // 128):
            for m in range(KC):
                em.op("tensor", lambda e, m=m, tb=tb: e.transpose(
                    ps[6][:, 0:128], xc[:, m * QT + tb * 128:m * QT + (tb + 1) * 128],
                    identf[:, :]),
                    reads=[nm + "xc", "ident_f32"], writes=["pb6"])
                em.op("scalar", lambda e, m=m: e.activation(
                    out=ot[:, m * 128:(m + 1) * 128], in_=ps[6][:, 0:128],
                    func=AF.Copy),
                    reads=["pb6"], writes=[nm + f"ot{m}"])
            em.op("gpsimd", lambda e, qoff=qoff, tb=tb: e.dma_start(
                out=aps["out_tok"][qoff + tb * 128:qoff + (tb + 1) * 128, :],
                in_=ot[:, :]),
                reads=[nm + f"ot{m}" for m in range(KC)], writes=["OUT"], dma=True)


def moe_setup(em, aps):
    wrE = em.sbuf("moe_wrE", [128, KC * E1], BF16)
    em.op("gpsimd", lambda e: e.dma_start(
        out=wrE[:, :].rearrange("p (k e) -> p k e", k=KC),
        in_=aps["wr"].rearrange("(k p) e -> p k e", p=128)),
        writes=["moe_wrE"], dma=True)
    em.sbuf("moe_wTqb", [8, 1024], BF16)


def build_moe():
    nc = bass.Bass("TRN2", target_bir_lowering=False, debug=False)
    ap = {}
    ap["xn2"] = nc.dram_tensor("xn2", [KC, 128, T], BF16, kind="ExternalInput").ap()
    ap["x1"] = nc.dram_tensor("x1", [KC, 128, T], F32, kind="ExternalInput").ap()
    ap["s1"] = nc.dram_tensor("s1", [KC, 128, T], F16, kind="ExternalInput").ap()
    ap["wr"] = nc.dram_tensor("wr", [D, E1], F32, kind="ExternalInput").ap()
    ap["br"] = nc.dram_tensor("br", [128, E1], F32, kind="ExternalInput").ap()
    ap["w1"] = nc.dram_tensor("w1", [E1, D, HID1], F32, kind="ExternalInput").ap()
    ap["b1"] = nc.dram_tensor("b1", [E1, 128, HID1 // 128], F32,
                              kind="ExternalInput").ap()
    ap["w2"] = nc.dram_tensor("w2", [E1, HID1, D], F32, kind="ExternalInput").ap()
    ap["b2"] = nc.dram_tensor("b2", [E1, D], F32, kind="ExternalInput").ap()
    identf_d = nc.dram_tensor("identf_d", [128, 128], F32, kind="ExternalInput").ap()
    ap["wt_scratch"] = nc.dram_tensor("wts", [E1, T], F32, kind="Internal").ap()
    ap["out_tok"] = nc.dram_tensor("out_tok", [T, D], F32, kind="ExternalOutput").ap()
    with ExitStack() as ctx:
        em = Em(nc, ctx)
        for i in range(7):
            em.psum_bank(f"pb{i}")
        em.ps["pbt"] = ctx.enter_context(nc.psum_tensor("pbt", [128, 1024], BF16))
        tf = em.sbuf("ident_f32", [128, 128], F32)
        em.op("sync", lambda e: e.dma_start(out=tf[:, :], in_=identf_d[:, :]),
              writes=["ident_f32"], dma=True)
        moe_setup(em, ap)
        emit_l1_moe(em, ap)
        blk = ctx.enter_context(nc.Block())
        em.finalize(blk)
    return nc


_BUILD_CACHE = {}


def _get(name, fn):
    if name not in _BUILD_CACHE:
        _BUILD_CACHE[name] = fn()
    return _BUILD_CACHE[name]


def kernel_multi(**inputs):
    import ml_dtypes
    X = np.asarray(inputs["L0_attn_X"])
    carr = l0_const_arrays()
    ident = np.eye(128, dtype=np.float32)
    ones128 = np.ones((128, 128), dtype=np.float32)
    bd = np.zeros((128, 128), np.float32)
    bd[:64, :64] = 1.0
    bd[64:, 64:] = 1.0

    colperm = np.concatenate([h * DH0 + np.concatenate([np.arange(0, DH0, 2),
                                                        np.arange(1, DH0, 2)])
                              for h in range(H0)])
    Wq_p = np.ascontiguousarray(np.asarray(inputs["L0_attn_Wq"])[:, :, colperm])
    Wk_p = np.ascontiguousarray(np.asarray(inputs["L0_attn_Wk"])[:, :, colperm])
    ls = np.asarray(inputs["L0_attn_ln_scale"])
    lb = np.asarray(inputs["L0_attn_ln_bias"])
    gs = np.asarray(inputs["L0_attn_gn_scale"])
    gb = np.asarray(inputs["L0_attn_gn_bias"])
    lnp = np.concatenate([ls.reshape(KC, 128).T, lb.reshape(KC, 128).T], 1)
    gnp = np.concatenate(sum([[gs[l].reshape(KC, 128).T, gb[l].reshape(KC, 128).T]
                              for l in range(L0L)], []), 1)
    wsc_t = np.asarray(inputs["L0_ffn_weight_scale"]).reshape(KC, 128).T
    wbi_t = np.asarray(inputs["L0_ffn_bias"]).reshape(KC, 128).T
    bqkv_t = np.asarray(inputs["L1_attn_bqkv"]).reshape(3 * KC, 128).T
    bproj_t = np.asarray(inputs["L1_attn_bproj"]).reshape(KC, 128).T
    scl_t = np.asarray(inputs["L1_attn_scale"]).reshape(1, 1)
    br_t = np.repeat(np.asarray(inputs["L1_ffn_br"])[None, :], 128, 0)
    b1_t = np.ascontiguousarray(
        np.asarray(inputs["L1_ffn_b1"]).reshape(E1, HID1 // 128, 128)
        .transpose(0, 2, 1))

    cores = list(range(NCORES))
    nc1 = _get("p1", build_p1)
    im1 = []
    for c in cores:
        xslice = np.ascontiguousarray(X[2 * c:2 * c + 2].reshape(T, D))
        im1.append({"x_tok": xslice, "ident": ident, "ones128": ones128})
    r1 = run_bass_kernel_spmd(nc1, im1, core_ids=cores)
    x_d = [r1.results[c]["x_d"] for c in cores]
    xn_d = [r1.results[c]["xn_d"] for c in cores]
    mxs = np.concatenate([r1.results[c]["mx"].reshape(1, 1) for c in cores],
                         axis=1)
    nc2 = _get("l0", lambda: build_l0_test(layers=(0, 1, 2, 3))[0])
    wl0 = {"wq": Wq_p, "wk": Wk_p, "wv": np.asarray(inputs["L0_attn_Wv"]),
           "wg": np.asarray(inputs["L0_attn_Wg"]),
           "wo": np.asarray(inputs["L0_attn_Wo"]),
           "w1": np.asarray(inputs["L0_attn_W1"]),
           "w2": np.asarray(inputs["L0_attn_W2"]),
           "lnp_d": lnp, "gnp_d": gnp}
    for k, v in carr.items():
        wl0["dc_" + k] = v
    s0 = [None] * NCORES
    for b in range(BLOC):
        im2 = []
        for c in cores:
            d = dict(wl0)
            d["xi0"] = np.ascontiguousarray(xn_d[c][:, :, b * S:(b + 1) * S])
            im2.append(d)
        r2 = run_bass_kernel_spmd(nc2, im2, core_ids=cores)
        for c in cores:
            part = r2.results[c]["xi_a"]
            s0[c] = part if b == 0 else np.concatenate([s0[c], part], axis=2)
    nc3 = _get("gc", build_gc_test)
    im3 = []
    for c in cores:
        im3.append({"xn": xn_d[c], "x": x_d[c], "s0": s0[c],
                    "wint8": np.asarray(inputs["L0_ffn_weight_int8"]),
                    "wsc": wsc_t, "wbias": wbi_t, "mxs": mxs,
                    "identb_d": ident})
    r3 = run_bass_kernel_spmd(nc3, im3, core_ids=cores)
    x1_d = [r3.results[c]["x1"] for c in cores]
    nc4 = _get("l1", build_l1_test)
    s1 = [None] * NCORES
    xn2 = [None] * NCORES
    for b in range(BLOC):
        im4 = []
        for c in cores:
            im4.append({"x1": np.ascontiguousarray(x1_d[c][:, :, b * S:(b + 1) * S]),
                        "wqkv": np.asarray(inputs["L1_attn_Wqkv"]),
                        "bqkv_d": bqkv_t,
                        "wproj": np.asarray(inputs["L1_attn_Wproj"]),
                        "bproj_d": bproj_t, "scl": scl_t,
                        "identb_d": ident, "blk_d": bd})
        r4 = run_bass_kernel_spmd(nc4, im4, core_ids=cores)
        for c in cores:
            p1_, p2_ = r4.results[c]["s1"], r4.results[c]["xn2"]
            s1[c] = p1_ if b == 0 else np.concatenate([s1[c], p1_], axis=2)
            xn2[c] = p2_ if b == 0 else np.concatenate([xn2[c], p2_], axis=2)
    nc5 = _get("moe", build_moe)
    im5 = []
    for c in cores:
        im5.append({"xn2": xn2[c], "x1": x1_d[c], "s1": s1[c],
                    "wr": np.asarray(inputs["L1_ffn_Wr"]), "br": br_t,
                    "w1": np.asarray(inputs["L1_ffn_W1"]), "b1": b1_t,
                    "w2": np.asarray(inputs["L1_ffn_W2"]),
                    "b2": np.asarray(inputs["L1_ffn_b2"]),
                    "identf_d": ident})
    r5 = run_bass_kernel_spmd(nc5, im5, core_ids=cores)
    out = np.empty((B, S, D), dtype=np.float32)
    for c in cores:
        out[2 * c:2 * c + 2] = np.asarray(r5.results[c]["out_tok"],
                                          dtype=np.float32).reshape(2, S, D)
    return out


def emit_maxscan(em, x_all, gmax_d):
    sbuf = em.sbuf
    nm = "mxs_"
    NTB = (B * S) // 128
    xt = [sbuf(nm + f"xt{i}", [128, D], F32) for i in range(2)]
    xsq = sbuf(nm + "xsq", [128, D], F32)
    col = sbuf(nm + "col", [128, 8], F32)
    mxb = sbuf(nm + "mxb", [128, NTB], F32)
    mxc = sbuf(nm + "mxc", [128, 2], F32)
    mxr = sbuf(nm + "mxr", [1, 128], F32)
    id_sb = em.sb["ident"]
    eps_sb = em.sb["cp_epsln"]
    mhalf = em.sb["cp_mhalf"]
    ps0 = em.ps["ig_ps0"]
    for tb in range(NTB):
        pi = tb % 2
        kxt = nm + f"xt{pi}"
        em.op("sync", lambda e, tb=tb, t=xt[pi]: e.dma_start(
            out=t[:, :], in_=x_all[tb * 128:(tb + 1) * 128, :]),
            writes=[kxt], dma=True)
        em.op("scalar", lambda e, t=xt[pi]: e.activation(
            out=xsq[:, :], in_=t[:, :], func=AF.Square,
            accum_out=col[:, 1:2]),
            reads=[kxt], writes=[nm + "xsq", nm + "c1"])
        em.op("vector", lambda e, t=xt[pi]: e.reduce_sum(
            col[:, 0:1], t[:, :], axis=AX.X),
            reads=[kxt], writes=[nm + "c0"])
        em.op("vector", lambda e, t=xt[pi]: e.reduce_max(
            col[:, 2:3], t[:, :], axis=AX.X),
            reads=[kxt], writes=[nm + "c2"])
        em.op("vector", lambda e, t=xt[pi]: e.tensor_reduce(
            col[:, 3:4], t[:, :], axis=AX.X, op=OP.min),
            reads=[kxt], writes=[nm + "c3"])
        em.op("vector", lambda e: e.tensor_scalar_mul(col[:, 4:5], col[:, 0:1],
                                                      1.0 / D),
              reads=[nm + "c0"], writes=[nm + "c4"])
        em.op("vector", lambda e: e.tensor_scalar_mul(col[:, 5:6], col[:, 1:2],
                                                      1.0 / D),
              reads=[nm + "c1"], writes=[nm + "c5"])
        em.op("vector", lambda e: e.scalar_tensor_tensor(
            col[:, 6:7], col[:, 4:5], -1.0, col[:, 4:5], OP.mult, OP.mult),
            reads=[nm + "c4"], writes=[nm + "c6"])
        em.op("vector", lambda e: e.tensor_tensor(col[:, 5:6], col[:, 5:6],
                                                  col[:, 6:7], OP.add),
              reads=[nm + "c5", nm + "c6"], writes=[nm + "c5"])
        em.op("scalar", lambda e: e.activation(out=col[:, 5:6], in_=col[:, 5:6],
                                               func=AF.Ln, bias=eps_sb[:, 0:1]),
              reads=[nm + "c5", "cp_epsln"], writes=[nm + "c5"])
        em.op("scalar", lambda e: e.activation(out=col[:, 5:6], in_=col[:, 5:6],
                                               func=AF.Exp, scale=mhalf[:, 0:1]),
              reads=[nm + "c5", "cp_mhalf"], writes=[nm + "c5"])
        em.op("vector", lambda e: e.tensor_tensor(col[:, 2:3], col[:, 2:3],
                                                  col[:, 4:5], OP.subtract),
              reads=[nm + "c2", nm + "c4"], writes=[nm + "c2"])
        em.op("vector", lambda e: e.tensor_tensor(col[:, 3:4], col[:, 4:5],
                                                  col[:, 3:4], OP.subtract),
              reads=[nm + "c3", nm + "c4"], writes=[nm + "c3"])
        em.op("vector", lambda e: e.tensor_tensor(col[:, 2:3], col[:, 2:3],
                                                  col[:, 3:4], OP.max),
              reads=[nm + "c2", nm + "c3"], writes=[nm + "c2"])
        em.op("vector", lambda e, tb=tb: e.tensor_tensor(
            mxb[:, tb:tb + 1], col[:, 2:3], col[:, 5:6], OP.mult),
            reads=[nm + "c2", nm + "c5"], writes=[nm + f"mb{tb}"])
    em.op("vector", lambda e: e.reduce_max(mxc[:, 0:1], mxb[:, :], axis=AX.X),
          reads=[nm + f"mb{tb}" for tb in range(NTB)], writes=[nm + "mxc"])
    em.op("tensor", lambda e: e.transpose(ps0[0:1, 0:128], mxc[:, 0:1],
                                          id_sb[:, :]),
          reads=[nm + "mxc", "ident"], writes=["ig_ps0"])
    em.op("scalar", lambda e: e.activation(out=mxr[:, :], in_=ps0[0:1, 0:128],
                                           func=AF.Copy),
          reads=["ig_ps0"], writes=[nm + "mxr"])
    em.op("vector", lambda e: e.reduce_max(mxc[0:1, 1:2], mxr[:, :], axis=AX.X),
          reads=[nm + "mxr"], writes=[nm + "gmax"])
    em.op("gpsimd", lambda e: e.dma_start(out=gmax_d[:, :], in_=mxc[0:1, 1:2]),
          reads=[nm + "gmax"], writes=["GMAX"], dma=True)


def build_fused():
    nc = bass.Bass("TRN2", target_bir_lowering=False, debug=False)
    AP_ = lambda n, s, d, k: nc.dram_tensor(n, s, d, kind=k).ap()
    x_tok = AP_("x_tok", [T, D], F32, "ExternalInput")
    wq = AP_("wq", [L0L, D, D], F32, "ExternalInput")
    wk = AP_("wk", [L0L, D, D], F32, "ExternalInput")
    wv = AP_("wv", [L0L, D, D], F32, "ExternalInput")
    wg = AP_("wg", [L0L, D, D], F32, "ExternalInput")
    wo = AP_("wo", [L0L, D, D], F32, "ExternalInput")
    w1 = AP_("w1", [L0L, D, F0], F32, "ExternalInput")
    w2 = AP_("w2", [L0L, F0, D], F32, "ExternalInput")
    lnp_d = AP_("lnp_d", [128, 2 * KC], F32, "ExternalInput")
    gnp_d = AP_("gnp_d", [128, L0L * 2 * KC], F32, "ExternalInput")
    wint8 = AP_("wint8", [D, D], mybir.dt.int8, "ExternalInput")
    wsc_d = AP_("wsc", [128, KC], F32, "ExternalInput")
    wbi_d = AP_("wbias", [128, KC], F32, "ExternalInput")
    wqkv = AP_("wqkv", [3 * D, D], F32, "ExternalInput")
    bqkv_d = AP_("bqkv_d", [128, 3 * KC], F32, "ExternalInput")
    wproj = AP_("wproj", [D, D], F32, "ExternalInput")
    bproj_d = AP_("bproj_d", [128, KC], F32, "ExternalInput")
    scl_d = AP_("scl", [1, 1], F32, "ExternalInput")
    wr_d = AP_("wr", [D, E1], F32, "ExternalInput")
    br_d = AP_("br", [128, E1], F32, "ExternalInput")
    mw1 = AP_("mw1", [E1, D, HID1], F32, "ExternalInput")
    mb1 = AP_("mb1", [E1, 128, HID1 // 128], F32, "ExternalInput")
    mw2 = AP_("mw2", [E1, HID1, D], F32, "ExternalInput")
    mb2 = AP_("mb2", [E1, D], F32, "ExternalInput")
    ident_d = AP_("ident", [128, 128], F32, "ExternalInput")
    ones_d = AP_("ones128", [128, 128], F32, "ExternalInput")
    blk_d = AP_("blk_d", [128, 128], F32, "ExternalInput")
    carr = l0_const_arrays()
    dcon = {k: AP_("dc_" + k, list(v.shape), F32, "ExternalInput")
            for k, v in carr.items()}
    xn_dram = AP_("sc_xn", [KC, 128, T], F32, "Internal")
    x_dram = AP_("sc_x", [KC, 128, T], F32, "Internal")
    xi_a = AP_("sc_xia", [KC, 128, S], F16, "Internal")
    xi_b = AP_("sc_xib", [KC, 128, S], F16, "Internal")
    yr_s = AP_("sc_yr", [KC, 128, S], F16, "Internal")
    s0_d = AP_("sc_s0", [KC, 128, T], F16, "Internal")
    x1_d = AP_("sc_x1", [KC, 128, T], F32, "Internal")
    s1_d = AP_("sc_s1", [KC, 128, T], F16, "Internal")
    xn2_d = AP_("sc_xn2", [KC, 128, T], BF16, "Internal")
    wts = AP_("sc_wts", [E1, T], F32, "Internal")
    x_all = AP_("x_all", [B * S, D], F32, "ExternalInput")
    cc_in = AP_("cc_in", [1, 1], F32, "Internal")
    gmax_d = AP_("gmax_d", [1, 1], F32, "Internal")
    out_tok = AP_("out_tok", [T, D], F32, "ExternalOutput")

    with ExitStack() as ctx:
        em = Em(nc, ctx)
        for name, ap in (("ident", ident_d), ("ones128", ones_d)):
            t = em.sbuf("sb_" + name, [128, 128], F32)
            em.sb[name] = t
            em.op("sync", lambda e, t=t, ap=ap: e.dma_start(out=t[:, :], in_=ap[:, :]),
                  writes=[name], dma=True)
        em.sb["ident_f32"] = em.sb["ident"]
        tb = em.sbuf("ident_bf", [128, 128], BF16)
        em.op("gpsimd", lambda e: e.dma_start(out=tb[:, :], in_=ident_d[:, :]),
              writes=["ident_bf"], dma=True)
        tb2 = em.sbuf("blkones64_bf", [128, 128], BF16)
        em.op("gpsimd", lambda e: e.dma_start(out=tb2[:, :], in_=blk_d[:, :]),
              writes=["blkones64_bf"], dma=True)
        setup_common_consts(em, dcon)
        for nmv, v in (("cp_mone", -1.0),):
            t = em.sbuf(nmv, [128, 1], F32)
            em.op("vector", lambda e, t=t, v=v: e.memset(t[:, :], v), writes=[nmv])
        t = em.sbuf("ones_f32", [128, 128], F32)
        em.op("vector", lambda e, t=t: e.memset(t[:, :], 1.0), writes=["ones_f32"])
        em.sb["lnp"] = em.sbuf("lnp", [128, 2 * KC], F32)
        em.op("sync", lambda e: e.dma_start(out=em.sb["lnp"][:, :], in_=lnp_d[:, :]),
              writes=["lnp"], dma=True)
        em.sb["gnp"] = em.sbuf("gnp", [128, L0L * 2 * KC], F32)
        em.op("sync", lambda e: e.dma_start(out=em.sb["gnp"][:, :], in_=gnp_d[:, :]),
              writes=["gnp"], dma=True)
        emit_decay_tables(em)
        em.barrier()
        em.push_scope()
        for i in range(4):
            em.psum_bank(f"ig_ps{i}")
        em.psum_bank("ig_pss")
        em.psum_bank("ig_psq")
        emit_ingress(em, x_tok, x_dram, xn_dram, cc_in)
        emit_maxscan(em, x_all, gmax_d)
        em.pop_scope()
        em.push_scope()
        for i in range(8):
            em.psum_bank(f"pb{i}")
        for b in range(BLOC):
            aps = {"xi0": xn_dram[:, :, b * S:(b + 1) * S],
                   "xi": [xi_a, xi_b], "yr": yr_s,
                   "sout": s0_d[:, :, b * S:(b + 1) * S],
                   "wq": wq, "wk": wk, "wv": wv, "wg": wg, "wo": wo,
                   "w1": w1, "w2": w2}
            emit_l0_stack(em, b, aps)
        em.pop_scope()
        em.push_scope()
        for i in range(7):
            em.psum_bank(f"pb{i}")
        em.psum_bank("pbt", BF16, 1024)
        em.sbuf("gc_cb2", [128, KC * 512], F32)
        emit_gate_combine(em, {"xn": xn_dram, "x": x_dram, "s0": s0_d,
                               "wint8": wint8, "wsc": wsc_d, "wbias": wbi_d,
                               "mxs": gmax_d, "x1": x1_d})
        em.pop_scope()
        em.push_scope()
        for i in range(7):
            em.psum_bank(f"pb{i}")
        em.psum_bank("pbt", BF16, 1024)
        for b in range(BLOC):
            emit_l1_attn(em, b, {"x1": x1_d, "wqkv": wqkv, "bqkv": bqkv_d,
                                 "wproj": wproj, "bproj": bproj_d, "scl": scl_d,
                                 "s1": s1_d, "xn2": xn2_d})
        em.pop_scope()
        em.push_scope()
        for i in range(7):
            em.psum_bank(f"pb{i}")
        em.psum_bank("pbt", BF16, 1024)
        mo = {"xn2": xn2_d, "x1": x1_d, "s1": s1_d, "wr": wr_d, "br": br_d,
              "w1": mw1, "b1": mb1, "w2": mw2, "b2": mb2,
              "wt_scratch": wts, "out_tok": out_tok}
        moe_setup(em, mo)
        emit_l1_moe(em, mo)
        em.pop_scope()
        blk = ctx.enter_context(nc.Block())
        em.finalize(blk)
    return nc


def kernel_fused(**inputs):
    X = np.asarray(inputs["L0_attn_X"])
    carr = l0_const_arrays()
    ident = np.eye(128, dtype=np.float32)
    ones128 = np.ones((128, 128), dtype=np.float32)
    bd = np.zeros((128, 128), np.float32)
    bd[:64, :64] = 1.0
    bd[64:, 64:] = 1.0
    colperm = np.concatenate([h * DH0 + np.concatenate([np.arange(0, DH0, 2),
                                                        np.arange(1, DH0, 2)])
                              for h in range(H0)])
    ls = np.asarray(inputs["L0_attn_ln_scale"])
    lb = np.asarray(inputs["L0_attn_ln_bias"])
    gs = np.asarray(inputs["L0_attn_gn_scale"])
    gb = np.asarray(inputs["L0_attn_gn_bias"])
    base = {
        "wq": np.ascontiguousarray(np.asarray(inputs["L0_attn_Wq"])[:, :, colperm]),
        "wk": np.ascontiguousarray(np.asarray(inputs["L0_attn_Wk"])[:, :, colperm]),
        "wv": np.asarray(inputs["L0_attn_Wv"]),
        "wg": np.asarray(inputs["L0_attn_Wg"]),
        "wo": np.asarray(inputs["L0_attn_Wo"]),
        "w1": np.asarray(inputs["L0_attn_W1"]),
        "w2": np.asarray(inputs["L0_attn_W2"]),
        "lnp_d": np.concatenate([ls.reshape(KC, 128).T, lb.reshape(KC, 128).T], 1),
        "gnp_d": np.concatenate(sum([[gs[l].reshape(KC, 128).T,
                                      gb[l].reshape(KC, 128).T]
                                     for l in range(L0L)], []), 1),
        "wint8": np.asarray(inputs["L0_ffn_weight_int8"]),
        "wsc": np.asarray(inputs["L0_ffn_weight_scale"]).reshape(KC, 128).T.copy(),
        "wbias": np.asarray(inputs["L0_ffn_bias"]).reshape(KC, 128).T.copy(),
        "wqkv": np.asarray(inputs["L1_attn_Wqkv"]),
        "bqkv_d": np.asarray(inputs["L1_attn_bqkv"]).reshape(3 * KC, 128).T.copy(),
        "wproj": np.asarray(inputs["L1_attn_Wproj"]),
        "bproj_d": np.asarray(inputs["L1_attn_bproj"]).reshape(KC, 128).T.copy(),
        "scl": np.asarray(inputs["L1_attn_scale"]).reshape(1, 1),
        "wr": np.asarray(inputs["L1_ffn_Wr"]),
        "br": np.repeat(np.asarray(inputs["L1_ffn_br"])[None, :], 128, 0),
        "mw1": np.asarray(inputs["L1_ffn_W1"]),
        "mb1": np.ascontiguousarray(
            np.asarray(inputs["L1_ffn_b1"]).reshape(E1, HID1 // 128, 128)
            .transpose(0, 2, 1)),
        "mw2": np.asarray(inputs["L1_ffn_W2"]),
        "mb2": np.asarray(inputs["L1_ffn_b2"]),
        "ident": ident, "ones128": ones128, "blk_d": bd,
    }
    for k, v in carr.items():
        base["dc_" + k] = v
    nc = _get("fused", build_fused)
    X_flat = np.ascontiguousarray(X.reshape(B * S, D))
    cores = list(range(NCORES))
    im = []
    for c in cores:
        d = dict(base)
        d["x_tok"] = np.ascontiguousarray(X[2 * c:2 * c + 2].reshape(T, D))
        d["x_all"] = X_flat
        im.append(d)
    r = run_bass_kernel_spmd(nc, im, core_ids=cores)
    out = np.empty((B, S, D), dtype=np.float32)
    for c in cores:
        out[2 * c:2 * c + 2] = np.asarray(r.results[c]["out_tok"],
                                          dtype=np.float32).reshape(2, S, D)
    return out


kernel = kernel_fused


# version 06 (final)
# speedup vs baseline: 1.4166x; 1.4166x over previous
import numpy as np
from contextlib import ExitStack

import concourse.bass as bass
import concourse.mybir as mybir
from concourse.bass_utils import run_bass_kernel_spmd

F32 = mybir.dt.float32
F16 = mybir.dt.float16
BF16 = mybir.dt.bfloat16
AF = mybir.ActivationFunctionType
OP = mybir.AluOpType
AX = mybir.AxisListType

NCORES = 8
B = 16
S = 2048
D = 1024
KC = D // 128
BLOC = B // NCORES
T = BLOC * S
L0L = 4
H0 = 8
DH0 = 128
F0 = 2048
H1 = 16
DH1 = 64
E1 = 8
HID1 = 4096
EPS_LN = 1e-5
EPS_GN = 1e-5
EPS_L1 = 1e-3

SB = S // 128
SG = S // 512
TG = T // 512
TB = T // 128


class Em:

    ENGS = ("sync", "tensor", "vector", "scalar", "gpsimd")

    def __init__(self, nc, ctx):
        self.nc = nc
        self.ctx = ctx
        self.steps = {e: [] for e in self.ENGS}
        self.writer = {}
        self.readers = {}
        self.opcnt = {e: 0 for e in self.ENGS}
        self.sb = {}
        self.ps = {}
        self.sems = {}

    def _uniq(self, name):
        n = getattr(self, "_scope_n", 0)
        return name if n == 0 else f"{name}_s{n}"

    def sbuf(self, name, shape, dtype):
        if name in self.sb:
            return self.sb[name]
        t = self._scope().enter_context(
            self.nc.sbuf_tensor(self._uniq(name), shape, dtype))
        self.sb[name] = t
        return t

    def psum_bank(self, name, dtype=F32, cols=512):
        if name in self.ps:
            return self.ps[name]
        t = self._scope().enter_context(
            self.nc.psum_tensor(self._uniq(name), [128, cols], dtype))
        self.ps[name] = t
        return t

    def _scope(self):
        return self.scopes[-1] if getattr(self, "scopes", None) else self.ctx

    def push_scope(self):
        if not hasattr(self, "scopes"):
            self.scopes = []
        self._scope_n = getattr(self, "_scope_n", 0) + 1
        self.scopes.append(ExitStack())
        self._scope_saved = getattr(self, "_scope_saved", [])
        self._scope_saved.append((dict(self.sb), dict(self.ps)))

    def pop_scope(self):
        self.barrier()
        sc = self.scopes.pop()
        sc.close()
        sb0, ps0 = self._scope_saved.pop()
        self.sb = sb0
        self.ps = ps0

    def barrier(self):
        for e in self.ENGS:
            self.steps[e].append(["BARRIER", None, (e, self.opcnt[e], False), False])
            self.opcnt[e] += 1
        self.writer = {}
        self.readers = {}
        self.nbar = getattr(self, "nbar", 0) + 1

    def op(self, eng, fn, reads=(), writes=(), dma=False):
        deps = []
        for k in reads:
            w = self.writer.get(k)
            if w is not None:
                deps.append(w)
        for k in writes:
            w = self.writer.get(k)
            if w is not None:
                deps.append(w)
            for r in self.readers.get(k, ()):
                deps.append(r)
        opid = (eng, self.opcnt[eng], dma)
        self.steps[eng].append([fn, deps, opid, dma])
        self.opcnt[eng] += 1
        for k in reads:
            self.readers.setdefault(k, []).append(opid)
        for k in writes:
            self.writer[k] = opid
            self.readers[k] = []
        return opid

    def finalize(self, block):
        needed = {e: set() for e in self.ENGS}
        for e in self.ENGS:
            for fn, deps, opid, dma in self.steps[e]:
                if fn == "BARRIER":
                    continue
                for (pe, pidx, pdma) in deps:
                    if pdma or pe != e:
                        needed[pe].add(pidx)
        NQ = 8
        crank = {e: {} for e in self.ENGS}
        dslot = {e: {} for e in self.ENGS}
        for e in self.ENGS:
            ci = di = 0
            slotcnt = [0] * NQ
            for fn, deps, opid, dma in self.steps[e]:
                if fn == "BARRIER":
                    continue
                idx = opid[1]
                if dma:
                    s = di % NQ
                    slotcnt[s] += 1
                    dslot[e][idx] = (s, slotcnt[s])
                    di += 1
                elif idx in needed[e]:
                    ci += 1
                    crank[e][idx] = ci
        for e in self.ENGS:
            for pre in ("c_", "d_"):
                self.sems[pre + e] = self.ctx.enter_context(
                    self.nc.semaphore(pre + e))
            for s in range(NQ):
                self.sems[f"q_{e}_{s}"] = self.ctx.enter_context(
                    self.nc.semaphore(f"q_{e}_{s}"))
        self.sems["bar"] = self.ctx.enter_context(self.nc.semaphore("bar"))
        em = self

        def make_body(e):
            def body(eng):
                waited = {}
                drain_cover = -1
                drains = 0
                dcount_sofar = {en: [0] * NQ for en in em.ENGS}
                nbar_seen = 0
                for fn, deps, opid, dma in em.steps[e]:
                    if fn == "BARRIER":
                        for s in range(NQ):
                            sname = f"q_{e}_{s}"
                            v = 16 * dcount_sofar[e][s]
                            if v and waited.get(sname, 0) < v:
                                eng.wait_ge(em.sems[sname], v)
                                waited[sname] = v
                        if e in ("vector", "scalar", "gpsimd"):
                            eng.drain().then_inc(em.sems["bar"], 1)
                        else:
                            eng.nop().then_inc(em.sems["bar"], 1)
                        nbar_seen += 1
                        eng.wait_ge(em.sems["bar"], 5 * nbar_seen)
                        continue
                    need_drain_past = -1
                    for (pe, pidx, pdma) in deps:
                        if pe == e and not pdma:
                            if e != "tensor":
                                need_drain_past = max(need_drain_past, pidx)
                        else:
                            if pdma:
                                ps_, pc_ = dslot[pe][pidx]
                                sname = f"q_{pe}_{ps_}"
                                v = 16 * pc_
                            else:
                                sname = "c_" + pe
                                v = crank[pe][pidx]
                            if waited.get(sname, 0) < v:
                                eng.wait_ge(em.sems[sname], v)
                                waited[sname] = v
                    if need_drain_past >= 0 and need_drain_past > drain_cover:
                        drains += 1
                        eng.drain().then_inc(em.sems["d_" + e], 1)
                        eng.wait_ge(em.sems["d_" + e], drains)
                        drain_cover = opid[1] - 1
                    if dma:
                        s, cnt = dslot[e][opid[1]]
                        sname = f"q_{e}_{s}"
                        if cnt >= 2 and waited.get(sname, 0) < 16 * (cnt - 1):
                            eng.wait_ge(em.sems[sname], 16 * (cnt - 1))
                            waited[sname] = 16 * (cnt - 1)
                        dcount_sofar[e][s] = cnt
                    ins = fn(eng)
                    idx = opid[1]
                    if dma:
                        s, cnt = dslot[e][idx]
                        ins.then_inc(em.sems[f"q_{e}_{s}"], 16)
                    elif idx in crank[e]:
                        ins.then_inc(em.sems["c_" + e], 1)
            return body

        for e in self.ENGS:
            if self.steps[e]:
                getattr(block, e)(make_body(e))


def host_constants():
    c = {}
    c["ident"] = np.eye(128, dtype=np.float32)
    pos = np.arange(S, dtype=np.float64)
    inv = 1.0 / (10000.0 ** (np.arange(0, DH0, 2, dtype=np.float64) / DH0))
    ang = pos[None, :] * inv[:, None]
    c["rope_cos"] = np.concatenate([np.cos(ang)] * 2, 0).astype(np.float32)
    sn = np.sin(ang)
    c["rope_sinq"] = np.concatenate([-sn, sn], 0).astype(np.float32)
    c["rope_sink"] = np.concatenate([sn, -sn], 0).astype(np.float32)
    P = np.zeros((128, 128), dtype=np.float32)
    for i in range(64):
        P[i, 64 + i] = 1.0
        P[64 + i, i] = 1.0
    c["swapT"] = P.T.copy()
    g = 1.0 - np.exp(np.linspace(np.log(1 / 32), np.log(1 / 512), H0))
    lng = np.log(g)
    c["lngam"] = np.repeat(lng[None, :], 128, 0).astype(np.float32)
    ev = np.zeros((128, H0 * SB), dtype=np.float64)
    for h in range(H0):
        for j in range(SB):
            m = 128 * j + np.arange(128)
            ev[:, h * SB + j] = np.exp(-lng[h] * m)
    c["attev"] = ev.astype(np.float32)
    ramp = np.repeat(np.arange(512, dtype=np.float64)[None, :], 128, 0)
    c["ramp512"] = ramp.astype(np.float32)
    qb = np.zeros((128, H0 * SG), dtype=np.float64)
    for h in range(H0):
        for igr in range(SG):
            qb[:, h * SG + igr] = lng[h] * (512 * igr)
    c["qdecb"] = qb.astype(np.float32)
    dm = (np.arange(128)[None, :] >= np.arange(128)[:, None])
    c["diagmask"] = dm.astype(np.float32)
    bd = np.zeros((128, 128), dtype=np.float32)
    bd[:64, :64] = 1.0
    bd[64:, 64:] = 1.0
    c["blkones64"] = bd
    c["ones128"] = np.ones((128, 128), dtype=np.float32)
    return c


def emit_ingress(em, x_tok, x_d, xn_d, mx):
    sbuf, pb = em.sbuf, em.psum_bank
    xt = [sbuf(f"ig_xt{i}", [128, 1024], F32) for i in range(2)]
    xdt = [sbuf(f"ig_xd{i}", [128, 1024], F32) for i in range(2)]
    xsq = sbuf("ig_xsq", [128, 1024], F32)
    mu = sbuf("ig_mu", [128, 128], F32)
    var = sbuf("ig_var", [128, 128], F32)
    Av = sbuf("ig_A", [128, 128], F32)
    Bv = sbuf("ig_B", [128, 128], F32)
    xnt = [sbuf(f"ig_xn{i}", [128, 1024], F32) for i in range(2)]
    mxb = sbuf("ig_mxb", [128, TB], F32)
    mxc = sbuf("ig_mxc", [128, 2], F32)
    mxr = sbuf("ig_mxr", [1, 128], F32)
    id_sb = em.sb["ident"]
    ones_sb = em.sb["ones128"]
    eps_sb = em.sb["cp_epsln"]
    ps = [pb(f"ig_ps{i}") for i in range(4)]
    pss = pb("ig_pss")
    psq = pb("ig_psq")

    for tb in range(TB):
        pi = tb % 2
        xtt, xdd, xnn = xt[pi], xdt[pi], xnt[pi]
        kxt, kxd, kxn = f"ig_xt{pi}", f"ig_xd{pi}", f"ig_xn{pi}"
        em.op("sync",
              lambda e, xtt=xtt, tb=tb: e.dma_start(
                  out=xtt[:, :], in_=x_tok[tb * 128:(tb + 1) * 128, :]),
              writes=[kxt], dma=True)
        for c in range(KC):
            pk = f"ig_ps{c % 4}"
            em.op("tensor",
                  lambda e, xtt=xtt, c=c, p=ps[c % 4]: e.transpose(
                      p[:, 0:128], xtt[:, c * 128:(c + 1) * 128], id_sb[:, :]),
                  reads=[kxt, "ident"], writes=[pk])
            em.op("scalar",
                  lambda e, xdd=xdd, c=c, p=ps[c % 4]: e.activation(
                      out=xdd[:, c * 128:(c + 1) * 128], in_=p[:, 0:128],
                      func=AF.Copy),
                  reads=[pk], writes=[kxd + f"c{c}"])
        rx = [kxd + f"c{c}" for c in range(KC)]
        em.op("gpsimd",
              lambda e, xdd=xdd, tb=tb: e.dma_start(
                  out=x_d[:, :, tb * 128:(tb + 1) * 128].rearrange("k p t -> p k t"),
                  in_=xdd[:, :].rearrange("p (k c) -> p k c", k=KC)),
              reads=rx, dma=True)
        em.op("scalar",
              lambda e, xdd=xdd: e.activation(out=xsq[:, :], in_=xdd[:, :],
                                              func=AF.Square),
              reads=rx, writes=["ig_xsq"])
        for c in range(KC):
            em.op("tensor",
                  lambda e, xdd=xdd, c=c: e.matmul(
                      pss[:, 0:128], ones_sb[:, :],
                      xdd[:, c * 128:(c + 1) * 128],
                      start=(c == 0), stop=(c == KC - 1)),
                  reads=rx + ["ones128"], writes=(["ig_pss"] if c in (0, KC - 1) else []))
        for c in range(KC):
            em.op("tensor",
                  lambda e, c=c: e.matmul(
                      psq[:, 0:128], ones_sb[:, :],
                      xsq[:, c * 128:(c + 1) * 128],
                      start=(c == 0), stop=(c == KC - 1)),
                  reads=["ig_xsq", "ones128"], writes=(["ig_psq"] if c in (0, KC - 1) else []))
        em.op("vector", lambda e: e.tensor_scalar_mul(mu[:, :], pss[:, 0:128], 1.0 / D),
              reads=["ig_pss"], writes=["ig_mu"])
        em.op("vector", lambda e: e.tensor_scalar_mul(var[:, :], psq[:, 0:128], 1.0 / D),
              reads=["ig_psq"], writes=["ig_var"])
        em.op("vector",
              lambda e: e.tensor_tensor(Bv[:, :], mu[:, :], mu[:, :], OP.mult),
              reads=["ig_mu"], writes=["ig_B"])
        em.op("vector",
              lambda e: e.tensor_tensor(var[:, :], var[:, :], Bv[:, :], OP.subtract),
              reads=["ig_var", "ig_B"], writes=["ig_var"])
        em.op("scalar",
              lambda e: e.activation(out=Av[:, :], in_=var[:, :], func=AF.Ln,
                                     bias=eps_sb[:, 0:1]),
              reads=["ig_var", "cp_epsln"], writes=["ig_A"])
        em.op("scalar",
              lambda e: e.activation(out=Av[:, :], in_=Av[:, :], func=AF.Exp,
                                     scale=em.sb["cp_mhalf"][:, 0:1]),
              reads=["ig_A", "cp_mhalf"], writes=["ig_A"])
        em.op("vector",
              lambda e: e.scalar_tensor_tensor(Bv[:, :], mu[:, :], -1.0, Av[:, :],
                                               OP.mult, OP.mult),
              reads=["ig_mu", "ig_A"], writes=["ig_B"])
        em.op("vector",
              lambda e, xdd=xdd, xnn=xnn: e.tensor_tensor(
                  xnn[:, :].rearrange("p (c t) -> p c t", c=KC),
                  xdd[:, :].rearrange("p (c t) -> p c t", c=KC),
                  Av[:, :].unsqueeze(1).broadcast_to([128, KC, 128]), OP.mult),
              reads=rx + ["ig_A"], writes=[kxn])
        em.op("vector",
              lambda e, xnn=xnn: e.tensor_tensor(
                  xnn[:, :].rearrange("p (c t) -> p c t", c=KC),
                  xnn[:, :].rearrange("p (c t) -> p c t", c=KC),
                  Bv[:, :].unsqueeze(1).broadcast_to([128, KC, 128]), OP.add),
              reads=[kxn, "ig_B"], writes=[kxn])
        em.op("gpsimd",
              lambda e, xnn=xnn, tb=tb: e.dma_start(
                  out=xn_d[:, :, tb * 128:(tb + 1) * 128].rearrange("k p t -> p k t"),
                  in_=xnn[:, :].rearrange("p (k c) -> p k c", k=KC)),
              reads=[kxn], dma=True)
        em.op("vector",
              lambda e, xnn=xnn, tb=tb: e.reduce_max(
                  mxb[:, tb:tb + 1], xnn[:, :], axis=AX.X,
                  apply_absolute_value=True),
              reads=[kxn], writes=[f"ig_mxb{tb}"])
    em.op("vector",
          lambda e: e.reduce_max(mxc[:, 0:1], mxb[:, :], axis=AX.X),
          reads=[f"ig_mxb{tb}" for tb in range(TB)], writes=["ig_mxc"])
    em.op("tensor",
          lambda e: e.transpose(ps[0][0:1, 0:128], mxc[:, 0:1], id_sb[:, :]),
          reads=["ig_mxc", "ident"], writes=["ig_ps0"])
    em.op("scalar",
          lambda e: e.activation(out=mxr[:, :], in_=ps[0][0:1, 0:128], func=AF.Copy),
          reads=["ig_ps0"], writes=["ig_mxr"])
    em.op("vector",
          lambda e: e.reduce_max(mxc[0:1, 1:2], mxr[:, :], axis=AX.X),
          reads=["ig_mxr"], writes=["ig_mxf"])
    em.op("gpsimd",
          lambda e: e.dma_start(out=mx[:, :], in_=mxc[0:1, 1:2]),
          reads=["ig_mxf"], writes=["MX"], dma=True)


def load_consts(em, dram_consts):
    for name, ap in dram_consts.items():
        t = em.sbuf("sb_" + name, [ap.shape[0], ap.shape[1]], F32)
        em.sb[name] = t
        em.op("sync", lambda e, t=t, ap=ap: e.dma_start(out=t[:, :], in_=ap[:, :]),
              writes=[name], dma=True)
    for nm, v in [("cp_epsln", EPS_LN), ("cp_mhalf", -0.5), ("cp_epsgn", EPS_GN),
                  ("cp_epsl1", EPS_L1), ("cp_one", 1.0)]:
        t = em.sbuf(nm, [128, 1], F32)
        em.op("vector", lambda e, t=t, v=v: e.memset(t[:, :], v), writes=[nm])


def build_p1():
    nc = bass.Bass("TRN2", target_bir_lowering=False, debug=False)
    x_tok = nc.dram_tensor("x_tok", [T, D], F32, kind="ExternalInput").ap()
    dcon = {}
    for nm in ("ident", "ones128"):
        shp = {"ident": [128, 128], "ones128": [128, 128]}[nm]
        dcon[nm] = nc.dram_tensor(nm, shp, F32, kind="ExternalInput").ap()
    x_d = nc.dram_tensor("x_d", [KC, 128, T], F32, kind="ExternalOutput").ap()
    xn_d = nc.dram_tensor("xn_d", [KC, 128, T], F32, kind="ExternalOutput").ap()
    mx = nc.dram_tensor("mx", [1, 1], F32, kind="ExternalOutput").ap()
    with ExitStack() as ctx:
        em = Em(nc, ctx)
        load_consts(em, dcon)
        emit_ingress(em, x_tok, x_d, xn_d, mx)
        blk = ctx.enter_context(nc.Block())
        em.finalize(blk)
    return nc




def emit_decay_tables(em):
    sbuf = em.sbuf
    lngam = sbuf("lngam", [128, H0], F32)
    neglng = sbuf("neglng", [128, H0], F32)
    attev = sbuf("attev", [128, H0 * SB], F32)
    qdecb = sbuf("qdecb", [128, H0 * SG], F32)
    tt = sbuf("dt_t", [128, 8], F32)
    evb = sbuf("dt_evb", [128, H0 * SB], F32)
    cj = em.sb["cjramp"]
    hidx = em.sb["hidx"]
    j128 = em.sb["j128"]
    ig512 = em.sb["ig512"]
    cseed = em.sb["cseed"]
    em.op("scalar", lambda e: e.activation(out=tt[:, 0:1], in_=cseed[:, 0:1],
                                           func=AF.Ln),
          reads=["cseed"], writes=["dt_a0"])
    em.op("scalar", lambda e: e.activation(out=tt[:, 1:2], in_=cseed[:, 1:2],
                                           func=AF.Ln),
          reads=["cseed"], writes=["dt_a1"])
    em.op("vector", lambda e: e.reciprocal(tt[:, 2:3], cseed[:, 2:3]),
          reads=["cseed"], writes=["dt_r7"])
    em.op("vector", lambda e: e.tensor_tensor(tt[:, 3:4], tt[:, 1:2], tt[:, 0:1],
                                              OP.subtract),
          reads=["dt_a0", "dt_a1"], writes=["dt_d"])
    em.op("vector", lambda e: e.tensor_tensor(tt[:, 3:4], tt[:, 3:4], tt[:, 2:3],
                                              OP.mult),
          reads=["dt_d", "dt_r7"], writes=["dt_d"])
    em.op("vector", lambda e: e.tensor_scalar(tt[:, 4:5 + H0 - 1] if False else lngam[:, :],
                                              hidx[:, :], tt[:, 3:4], tt[:, 0:1],
                                              OP.mult, OP.add),
          reads=["hidx", "dt_d", "dt_a0"], writes=["dt_ah"])
    em.op("scalar", lambda e: e.activation(out=lngam[:, :], in_=lngam[:, :],
                                           func=AF.Exp),
          reads=["dt_ah"], writes=["dt_ea"])
    em.op("vector", lambda e: e.tensor_scalar(lngam[:, :], lngam[:, :], -1.0, 1.0,
                                              OP.mult, OP.add),
          reads=["dt_ea"], writes=["dt_gam"])
    em.op("scalar", lambda e: e.activation(out=lngam[:, :], in_=lngam[:, :],
                                           func=AF.Ln),
          reads=["dt_gam"], writes=["lngam"])
    em.op("vector", lambda e: e.tensor_scalar_mul(neglng[:, :], lngam[:, :], -1.0),
          reads=["lngam"], writes=["neglng"])
    for h in range(H0):
        em.op("vector", lambda e, h=h: e.tensor_scalar_mul(
            evb[:, h * SB:(h + 1) * SB], j128[:, :], neglng[:, h:h + 1]),
            reads=["j128", "neglng"], writes=[f"evb{h}"])
        em.op("vector", lambda e, h=h: e.tensor_scalar_mul(
            qdecb[:, h * SG:(h + 1) * SG], ig512[:, :], lngam[:, h:h + 1]),
            reads=["ig512", "lngam"], writes=["qdecb"])
        for j in range(SB):
            em.op("scalar", lambda e, h=h, j=j: e.activation(
                out=attev[:, h * SB + j:h * SB + j + 1], in_=cj[:, 0:1],
                func=AF.Exp, scale=neglng[:, h:h + 1],
                bias=evb[:, h * SB + j:h * SB + j + 1]),
                reads=["cjramp", "neglng", f"evb{h}"], writes=["attev"])


def emit_l0_stack(em, b, aps, layers=None, dbg=None):
    if layers is None:
        layers = list(range(L0L))
    sbuf = em.sbuf
    nm = "l0_"
    xn = sbuf(nm + "xn", [128, KC * S], BF16)
    qt = sbuf(nm + "qt", [128, H0 * S], BF16)
    kt = sbuf(nm + "kt", [128, H0 * S], BF16)
    vt = sbuf(nm + "vt", [128, SB * D], BF16)
    gg = sbuf(nm + "gg", [128, H0 * 512], BF16)
    wb = sbuf(nm + "wb", [128, KC * D], BF16)
    wob = sbuf(nm + "wob", [128, (KC // 2) * D], BF16)
    w1b = qt
    w2b = kt
    hb = vt
    Aall = sbuf(nm + "A", [128, S], BF16)
    Ball = sbuf(nm + "B", [128, S], BF16)
    xg = sbuf(nm + "xg", [128, KC * 512], F16)
    xg2 = sbuf(nm + "xg2", [128, KC * 512], F16)
    t1 = sbuf(nm + "t1", [128, 512], BF16)
    t2 = sbuf(nm + "t2", [128, 512], BF16)
    t3 = sbuf(nm + "t3", [128, 512], BF16)
    tf = sbuf(nm + "tf", [128, 512], F32)
    ybf = t1
    ysq = t3
    atb = Aall
    dq = t3
    mum = sbuf(nm + "mu", [128, 512], F32)
    msq = sbuf(nm + "msq", [128, 512], F32)
    rst = msq
    ps = [em.ps[f"pb{i}"] for i in range(8)]
    cos = em.sb["rope_cos_bf"]
    sinq = em.sb["rope_sinq_bf"]
    swapT = em.sb["swapT_bf"]
    ones = em.sb["ones_bf"]
    ones_hf = em.sb["ones_hf"]
    ramp = em.sb["ramp512_hf16"]
    qdecb = em.sb["qdecb"]
    attev = em.sb["attev"]
    lngam = em.sb["lngam"]
    diagm = em.sb["diagmask_bf"]
    lnp = em.sb["lnp"]
    gnp = em.sb["gnp"]
    epsl = em.sb["cp_epsln"]
    epsg = em.sb["cp_epsgn"]
    mhalf = em.sb["cp_mhalf"]

    def xi_load(dst, dkey, src_ap, src_key, g):
        eng = "gpsimd" if src_ap.dtype != F16 else "sync"
        em.op(eng, lambda e: e.dma_start(
            out=dst[:, :].rearrange("p (k t) -> p k t", k=KC),
            in_=src_ap[:, :, g * 512:(g + 1) * 512].rearrange("k p t -> p k t")),
            reads=[src_key], writes=[dkey], dma=True)

    def ln_stats(src_ap, src_key):
        for g in range(SG):
            xi_load(xg, nm + "xg", src_ap, src_key, g)
            em.op("scalar", lambda e: e.activation(
                out=xg2[:, :], in_=xg[:, :], func=AF.Square),
                reads=[nm + "xg"], writes=[nm + "xg2"])
            for c in range(KC):
                em.op("tensor", lambda e, c=c: e.matmul(
                    ps[0][:, 0:512], ones_hf[:, :], xg[:, c * 512:(c + 1) * 512],
                    start=(c == 0), stop=(c == KC - 1)),
                    reads=[nm + "xg", "ones_hf"],
                    writes=(["pb0"] if c in (0, KC - 1) else []))
            for c in range(KC):
                em.op("tensor", lambda e, c=c: e.matmul(
                    ps[1][:, 0:512], ones_hf[:, :], xg2[:, c * 512:(c + 1) * 512],
                    start=(c == 0), stop=(c == KC - 1)),
                    reads=[nm + "xg2", "ones_hf"],
                    writes=(["pb1"] if c in (0, KC - 1) else []))
            em.op("vector", lambda e: e.tensor_scalar_mul(
                mum[:, :], ps[0][:, 0:512], 1.0 / D),
                reads=["pb0"], writes=[nm + "mu"])
            em.op("vector", lambda e: e.tensor_scalar_mul(
                msq[:, :], ps[1][:, 0:512], 1.0 / D),
                reads=["pb1"], writes=[nm + "msq"])
            em.op("vector", lambda e: e.tensor_tensor(
                tf[:, :], mum[:, :], mum[:, :], OP.mult),
                reads=[nm + "mu"], writes=[nm + "tf"])
            em.op("vector", lambda e: e.tensor_tensor(
                rst[:, :], msq[:, :], tf[:, :], OP.subtract),
                reads=[nm + "msq", nm + "tf"], writes=[nm + "rst"])
            em.op("scalar", lambda e: e.activation(
                out=rst[:, :], in_=rst[:, :], func=AF.Ln, bias=epsl[:, 0:1]),
                reads=[nm + "rst", "cp_epsln"], writes=[nm + "rst"])
            em.op("scalar", lambda e, g=g: e.activation(
                out=Aall[:, g * 512:(g + 1) * 512], in_=rst[:, :], func=AF.Exp,
                scale=mhalf[:, 0:1]),
                reads=[nm + "rst", "cp_mhalf"], writes=[nm + f"A{g}"])
            em.op("vector", lambda e, g=g: e.scalar_tensor_tensor(
                Ball[:, g * 512:(g + 1) * 512], mum[:, :], -1.0,
                Aall[:, g * 512:(g + 1) * 512], OP.mult, OP.mult),
                reads=[nm + "mu", nm + f"A{g}"], writes=[nm + f"B{g}"])

    def ln_apply(src_ap, src_key, into_xg=True):
        for g in range(SG):
            xi_load(xg, nm + "xg", src_ap, src_key, g)
            for c in range(KC):
                em.op("vector", lambda e, c=c, g=g: e.tensor_tensor(
                    tf[:, :], xg[:, c * 512:(c + 1) * 512],
                    Aall[:, g * 512:(g + 1) * 512], OP.mult),
                    reads=[nm + "xg", nm + f"A{g}"], writes=[nm + "tf"])
                em.op("vector", lambda e, c=c, g=g: e.tensor_tensor(
                    tf[:, :], tf[:, :], Ball[:, g * 512:(g + 1) * 512], OP.add),
                    reads=[nm + "tf", nm + f"B{g}"], writes=[nm + "tf"])
                em.op("scalar", lambda e, c=c, g=g: e.activation(
                    out=xn[:, c * S + g * 512:c * S + (g + 1) * 512],
                    in_=tf[:, :], func=AF.Identity, scale=lnp[:, c:c + 1],
                    bias=lnp[:, KC + c:KC + c + 1]),
                    reads=[nm + "tf", "lnp"], writes=[nm + f"xn{c}_{g}"])

    def load_w(dst, dkey, w_ap, wkey):
        em.op("gpsimd", lambda e: e.dma_start(
            out=dst[:, :].rearrange("p (k d) -> p k d", k=KC),
            in_=w_ap.rearrange("(k p) d -> p k d", p=128)),
            reads=[wkey], writes=[dkey], dma=True)

    for li, l in enumerate(layers):
        src_ap = aps["xi0"] if li == 0 else aps["xi"][l % 2]
        src_key = "XI0" if li == 0 else f"XI{l % 2}"
        ln_stats(src_ap, src_key)
        ln_apply(src_ap, src_key)
        xnk = lambda c, g: nm + f"xn{c}_{g}"
        load_w(wb, nm + "wb", aps["wv"][l], "WV")
        for j in range(SB):
            gj = j // 4
            for hf in range(2):
                par = (j * 2 + hf) % 2
                bank, bkey = (ps[2], "pb2") if par == 0 else (ps[4], "pb4")
                for k in range(KC):
                    em.op("tensor", lambda e, j=j, hf=hf, k=k, bank=bank: e.matmul(
                        bank[:, 0:512],
                        xn[:, k * S + j * 128:k * S + (j + 1) * 128],
                        wb[:, k * D + hf * 512:k * D + (hf + 1) * 512],
                        start=(k == 0), stop=(k == KC - 1)),
                        reads=[xnk(k, gj), nm + "wb"],
                        writes=([bkey] if k in (0, KC - 1) else []))
                em.op("scalar", lambda e, j=j, hf=hf, bank=bank: e.activation(
                    out=vt[:, j * D + hf * 512:j * D + (hf + 1) * 512],
                    in_=bank[:, 0:512], func=AF.Copy),
                    reads=[bkey], writes=[nm + f"vt{j}_{hf}"])
        for which in ("k", "q"):
            load_w(wb, nm + "wb", aps["w" + which][l], "W" + which)
            for h in range(H0):
                for g in range(SG):
                    par = (h * SG + g) % 2
                    bank, bkey = (ps[2], "pb2") if par == 0 else (ps[4], "pb4")
                    bsw, bswk = (ps[3], "pb3") if par == 0 else (ps[5], "pb5")
                    for k in range(KC):
                        em.op("tensor", lambda e, h=h, g=g, k=k, bank=bank: e.matmul(
                            bank[:, 0:512],
                            wb[:, k * D + h * 128:k * D + (h + 1) * 128],
                            xn[:, k * S + g * 512:k * S + (g + 1) * 512],
                            start=(k == 0), stop=(k == KC - 1)),
                            reads=[xnk(k, g), nm + "wb"],
                            writes=([bkey] if k in (0, KC - 1) else []))
                    em.op("scalar", lambda e, bank=bank: e.activation(
                        out=t1[:, :], in_=bank[:, 0:512], func=AF.Copy),
                        reads=[bkey], writes=[nm + "t1"])
                    em.op("tensor", lambda e, bsw=bsw: e.matmul(
                        bsw[:, 0:512], swapT[:, :], t1[:, :],
                        start=True, stop=True),
                        reads=[nm + "t1", "swapT_bf"], writes=[bswk])
                    em.op("vector", lambda e, g=g: e.tensor_tensor(
                        t2[:, :], t1[:, :], cos[:, g * 512:(g + 1) * 512],
                        OP.mult),
                        reads=[nm + "t1", "rope_cos_bf"], writes=[nm + "t2"])
                    em.op("vector", lambda e, g=g, bsw=bsw: e.tensor_tensor(
                        t3[:, :], bsw[:, 0:512],
                        sinq[:, g * 512:(g + 1) * 512], OP.mult),
                        reads=[bswk, "rope_sinq_bf"], writes=[nm + "t3"])
                    if which == "k":
                        em.op("vector", lambda e, h=h, g=g: e.tensor_tensor(
                            kt[:, h * S + g * 512:h * S + (g + 1) * 512],
                            t2[:, :], t3[:, :], OP.subtract),
                            reads=[nm + "t2", nm + "t3"],
                            writes=[nm + f"kt{h}_{g}"])
                    else:
                        em.op("vector", lambda e: e.tensor_tensor(
                            t2[:, :], t2[:, :], t3[:, :], OP.add),
                            reads=[nm + "t2", nm + "t3"], writes=[nm + "t2"])
                        em.op("scalar", lambda e, h=h, g=g: e.activation(
                            out=dq[:, :], in_=ramp[:, :], func=AF.Exp,
                            scale=lngam[:, h:h + 1],
                            bias=qdecb[:, h * SG + g:h * SG + g + 1]),
                            reads=["ramp512_hf16", "lngam", "qdecb"],
                            writes=[nm + "t3"])
                        em.op("vector", lambda e, h=h, g=g: e.tensor_tensor(
                            qt[:, h * S + g * 512:h * S + (g + 1) * 512],
                            t2[:, :], dq[:, :], OP.mult),
                            reads=[nm + "t2", nm + "t3"],
                            writes=[nm + f"qt{h}_{g}"])
        if dbg is not None and l == layers[0]:
            dbg_keys = {
                "xn": [nm + f"xn{c}_{g}" for c in range(KC) for g in range(SG)],
                "qt": [nm + f"qt{h}_{g}" for h in range(H0) for g in range(SG)],
                "kt": [nm + f"kt{h}_{g}" for h in range(H0) for g in range(SG)],
                "vt": [nm + f"vt{j}_{hf}" for j in range(SB) for hf in range(2)],
            }
            for name, tile in (("xn", xn), ("qt", qt), ("kt", kt), ("vt", vt)):
                em.op("gpsimd", lambda e, name=name, tile=tile: e.dma_start(
                    out=dbg[name][:, :], in_=tile[:, :]),
                    reads=dbg_keys[name], writes=[], dma=True)
        em.barrier()
        load_w(wb, nm + "wb", aps["wg"][l], "WG")
        for ig in range(SG):
            for h in range(H0):
                bank, bkey = (ps[5], "pb5") if h % 2 == 0 else (ps[3], "pb3")
                for k in range(KC):
                    em.op("tensor", lambda e, h=h, ig=ig, k=k, bank=bank: e.matmul(
                        bank[:, 0:512],
                        wb[:, k * D + h * 128:k * D + (h + 1) * 128],
                        xn[:, k * S + ig * 512:k * S + (ig + 1) * 512],
                        start=(k == 0), stop=(k == KC - 1)),
                        reads=[xnk(k, ig), nm + "wb"],
                        writes=([bkey] if k in (0, KC - 1) else []))
                em.op("scalar", lambda e, h=h, bank=bank: e.activation(
                    out=gg[:, h * 512:(h + 1) * 512], in_=bank[:, 0:512],
                    func=AF.Silu),
                    reads=[bkey], writes=[nm + f"gg{h}"])
            for h in range(H0):
                jmax = 4 * ig + 3
                for j in range(jmax + 1):
                    off = max(0, (j - 4 * ig) * 128)
                    w = 512 - off
                    par = j % 2
                    bank, bkey = (ps[6], "pb6") if par == 0 else (ps[2], "pb2")
                    ab = par * 512
                    akey = nm + f"atb{par}"
                    em.op("tensor", lambda e, h=h, ig=ig, j=j, off=off, w=w,
                          bank=bank: e.matmul(
                        bank[:, 0:w],
                        kt[:, h * S + j * 128:h * S + (j + 1) * 128],
                        qt[:, h * S + ig * 512 + off:h * S + (ig + 1) * 512],
                        start=True, stop=True),
                        reads=[nm + f"kt{h}_{j // 4}", nm + f"qt{h}_{ig}"],
                        writes=[bkey])
                    em.op("scalar", lambda e, h=h, j=j, off=off, w=w, bank=bank,
                          ab=ab: e.activation(
                        out=atb[:, ab + off:ab + off + w], in_=bank[:, 0:w],
                        func=AF.Identity, scale=attev[:, h * SB + j:h * SB + j + 1]),
                        reads=[bkey, "attev"], writes=[akey])
                    if j >= 4 * ig:
                        em.op("vector", lambda e, off=off, ab=ab: e.tensor_tensor(
                            atb[:, ab + off:ab + off + 128],
                            atb[:, ab + off:ab + off + 128],
                            diagm[:, :], OP.mult),
                            reads=[akey, "diagmask_bf"], writes=[akey])
                        if off > 0:
                            em.op("vector", lambda e, off=off, ab=ab: e.memset(
                                atb[:, ab:ab + off], 0.0),
                                writes=[akey])
                    em.op("tensor", lambda e, h=h, j=j, jmax=jmax, ab=ab: e.matmul(
                        ps[7][:, 0:512],
                        vt[:, j * D + h * 128:j * D + (h + 1) * 128],
                        atb[:, ab:ab + 512],
                        start=(j == 0), stop=(j == jmax)),
                        reads=[nm + f"vt{j}_{h // 4}", akey],
                        writes=(["pb7"] if j in (0, jmax) else []))
                em.op("scalar", lambda e: e.activation(
                    out=ybf[:, :], in_=ps[7][:, 0:512], func=AF.Copy),
                    reads=["pb7"], writes=[nm + "ybf"])
                em.op("scalar", lambda e: e.activation(
                    out=ysq[:, :], in_=ybf[:, :], func=AF.Square),
                    reads=[nm + "ybf"], writes=[nm + "ysq"])
                em.op("tensor", lambda e: e.matmul(
                    ps[0][:, 0:512], ones[:, :], ybf[:, :], start=True, stop=True),
                    reads=[nm + "ybf", "ones_bf"], writes=["pb0"])
                em.op("tensor", lambda e: e.matmul(
                    ps[1][:, 0:512], ones[:, :], ysq[:, :], start=True, stop=True),
                    reads=[nm + "ysq", "ones_bf"], writes=["pb1"])
                em.op("vector", lambda e: e.tensor_scalar_mul(
                    mum[:, :], ps[0][:, 0:512], 1.0 / DH0),
                    reads=["pb0"], writes=[nm + "mu"])
                em.op("vector", lambda e: e.tensor_scalar_mul(
                    msq[:, :], ps[1][:, 0:512], 1.0 / DH0),
                    reads=["pb1"], writes=[nm + "msq"])
                em.op("vector", lambda e: e.tensor_tensor(
                    tf[:, :], mum[:, :], mum[:, :], OP.mult),
                    reads=[nm + "mu"], writes=[nm + "tf"])
                em.op("vector", lambda e: e.tensor_tensor(
                    rst[:, :], msq[:, :], tf[:, :], OP.subtract),
                    reads=[nm + "msq", nm + "tf"], writes=[nm + "rst"])
                em.op("scalar", lambda e: e.activation(
                    out=rst[:, :], in_=rst[:, :], func=AF.Ln, bias=epsg[:, 0:1]),
                    reads=[nm + "rst", "cp_epsgn"], writes=[nm + "rst"])
                em.op("scalar", lambda e: e.activation(
                    out=rst[:, :], in_=rst[:, :], func=AF.Exp, scale=mhalf[:, 0:1]),
                    reads=[nm + "rst", "cp_mhalf"], writes=[nm + "rst"])
                em.op("vector", lambda e: e.tensor_tensor(
                    tf[:, :], ybf[:, :], mum[:, :], OP.subtract),
                    reads=[nm + "ybf", nm + "mu"], writes=[nm + "tf"])
                em.op("vector", lambda e: e.tensor_tensor(
                    tf[:, :], tf[:, :], rst[:, :], OP.mult),
                    reads=[nm + "tf", nm + "rst"], writes=[nm + "tf"])
                em.op("scalar", lambda e, l=l, h=h: e.activation(
                    out=t2[:, :], in_=tf[:, :], func=AF.Identity,
                    scale=gnp[:, l * 2 * KC + h:l * 2 * KC + h + 1],
                    bias=gnp[:, l * 2 * KC + KC + h:l * 2 * KC + KC + h + 1]),
                    reads=[nm + "tf", "gnp"], writes=[nm + "t2"])
                em.op("vector", lambda e, h=h: e.tensor_tensor(
                    gg[:, h * 512:(h + 1) * 512], gg[:, h * 512:(h + 1) * 512],
                    t2[:, :], OP.mult),
                    reads=[nm + f"gg{h}", nm + "t2"], writes=[nm + f"gg{h}"])
            xi_load(xg2, nm + "xg2", src_ap, src_key, ig)
            for hhalf in range(2):
                em.op("gpsimd", lambda e, l=l, hhalf=hhalf: e.dma_start(
                    out=wob[:, :].rearrange("p (k d) -> p k d", k=KC // 2),
                    in_=aps["wo"][l][hhalf * 512:(hhalf + 1) * 512, :]
                    .rearrange("(k p) d -> p k d", p=128)),
                    reads=["WO"], writes=[nm + "wob"], dma=True)
                for m in range(KC):
                    for hh in range(H0 // 2):
                        h = hhalf * 4 + hh
                        em.op("tensor", lambda e, m=m, h=h, hh=hh: e.matmul(
                            ps[m][:, 0:512],
                            wob[:, hh * D + m * 128:hh * D + (m + 1) * 128],
                            gg[:, h * 512:(h + 1) * 512],
                            start=(h == 0), stop=(h == H0 - 1)),
                            reads=[nm + f"gg{h}", nm + "wob"],
                            writes=(["pb%d" % m] if h in (0, H0 - 1) else []))
            for m in range(KC):
                em.op("vector", lambda e, m=m: e.tensor_tensor(
                    xg2[:, m * 512:(m + 1) * 512], ps[m][:, 0:512],
                    xg2[:, m * 512:(m + 1) * 512], OP.add),
                    reads=["pb%d" % m], writes=[nm + "xg2"])
            em.op("gpsimd", lambda e, ig=ig: e.dma_start(
                out=aps["yr"][:, :, ig * 512:(ig + 1) * 512]
                .rearrange("k p t -> p k t"),
                in_=xg2[:, :].rearrange("p (k t) -> p k t", k=KC)),
                reads=[nm + "xg2"], writes=["YR"], dma=True)
        em.barrier()
        ln_stats(aps["yr"], "YR")
        load_w(w1b, nm + "w1b", aps["w1"][l], "W1")
        em.op("gpsimd", lambda e, l=l: e.dma_start(
            out=w2b[:, :].rearrange("p (k d) -> p k d", k=F0 // 128),
            in_=aps["w2"][l].rearrange("(k p) d -> p k d", p=128)),
            reads=["W2"], writes=[nm + "w2b"], dma=True)
        if li == len(layers) - 1 and "sout" in aps:
            dst_ap = aps["sout"]
            dst_key = "SOUT"
        else:
            dst_ap = aps["xi"][(l + 1) % 2]
            dst_key = f"XI{(l + 1) % 2}"
        for g in range(SG):
            xi_load(xg, nm + "xg", aps["yr"], "YR", g)
            for c in range(KC):
                em.op("vector", lambda e, c=c, g=g: e.tensor_tensor(
                    tf[:, :], xg[:, c * 512:(c + 1) * 512],
                    Aall[:, g * 512:(g + 1) * 512], OP.mult),
                    reads=[nm + "xg", nm + f"A{g}"], writes=[nm + "tf"])
                em.op("vector", lambda e, c=c, g=g: e.tensor_tensor(
                    tf[:, :], tf[:, :], Ball[:, g * 512:(g + 1) * 512], OP.add),
                    reads=[nm + "tf", nm + f"B{g}"], writes=[nm + "tf"])
                em.op("scalar", lambda e, c=c, g=g: e.activation(
                    out=xn[:, c * S + g * 512:c * S + (g + 1) * 512],
                    in_=tf[:, :], func=AF.Identity, scale=lnp[:, c:c + 1],
                    bias=lnp[:, KC + c:KC + c + 1]),
                    reads=[nm + "tf", "lnp"], writes=[nm + f"xn{c}_{g}"])
            for mh in range(F0 // 128):
                bank, bkey = (ps[3], "pb3") if mh % 2 == 0 else (ps[5], "pb5")
                for k in range(KC):
                    em.op("tensor", lambda e, mh=mh, g=g, k=k, bank=bank: e.matmul(
                        bank[:, 0:512],
                        w1b[:, k * F0 + mh * 128:k * F0 + (mh + 1) * 128],
                        xn[:, k * S + g * 512:k * S + (g + 1) * 512],
                        start=(k == 0), stop=(k == KC - 1)),
                        reads=[xnk(k, g), nm + "w1b"],
                        writes=([bkey] if k in (0, KC - 1) else []))
                em.op("scalar", lambda e, mh=mh, bank=bank: e.activation(
                    out=hb[:, mh * 512:(mh + 1) * 512], in_=bank[:, 0:512],
                    func=AF.Gelu_apprx_tanh),
                    reads=[bkey], writes=[nm + f"hb{mh}"])
            for m in range(KC):
                bank, bkey = (ps[4], "pb4") if m % 2 == 0 else (ps[6], "pb6")
                for k2 in range(F0 // 128):
                    em.op("tensor", lambda e, m=m, k2=k2, bank=bank: e.matmul(
                        bank[:, 0:512],
                        w2b[:, k2 * D + m * 128:k2 * D + (m + 1) * 128],
                        hb[:, k2 * 512:(k2 + 1) * 512],
                        start=(k2 == 0), stop=(k2 == F0 // 128 - 1)),
                        reads=[nm + f"hb{k2}", nm + "w2b"],
                        writes=([bkey] if k2 in (0, F0 // 128 - 1) else []))
                em.op("vector", lambda e, m=m, bank=bank: e.tensor_tensor(
                    xg[:, m * 512:(m + 1) * 512], bank[:, 0:512],
                    xg[:, m * 512:(m + 1) * 512], OP.add),
                    reads=[bkey, nm + "xg"], writes=[nm + "xg"])
            em.op("gpsimd", lambda e, g=g, dst_ap=dst_ap: e.dma_start(
                out=dst_ap[:, :, g * 512:(g + 1) * 512]
                .rearrange("k p t -> p k t"),
                in_=xg[:, :].rearrange("p (k t) -> p k t", k=KC)),
                reads=[nm + "xg"], writes=[dst_key], dma=True)
        em.barrier()


def setup_common_consts(em, dcon):
    for name, ap in dcon.items():
        dt = BF16 if name.endswith("_bf") else (F16 if name.endswith("_hf16") else F32)
        t = em.sbuf("c_" + name, [ap.shape[0], ap.shape[1]], dt)
        em.sb[name] = t
        eng = "gpsimd" if dt != F32 else "sync"
        em.op(eng, lambda e, t=t, ap=ap: e.dma_start(out=t[:, :], in_=ap[:, :]),
              writes=[name], dma=True)
    for nmv, v in [("cp_epsln", EPS_LN), ("cp_mhalf", -0.5), ("cp_epsgn", EPS_GN),
                   ("cp_epsl1", EPS_L1)]:
        t = em.sbuf(nmv, [128, 1], F32)
        em.op("vector", lambda e, t=t, v=v: e.memset(t[:, :], v), writes=[nmv])
    t = em.sbuf("ones_bf", [128, 128], BF16)
    em.op("vector", lambda e, t=t: e.memset(t[:, :], 1.0), writes=["ones_bf"])
    t = em.sbuf("ones_hf", [128, 128], F16)
    em.op("vector", lambda e, t=t: e.memset(t[:, :], 1.0), writes=["ones_hf"])


def l0_const_arrays():
    c = {}
    pos = np.arange(S, dtype=np.float64)
    inv = 1.0 / (10000.0 ** (np.arange(0, DH0, 2, dtype=np.float64) / DH0))
    ang = pos[None, :] * inv[:, None]
    c["rope_cos_bf"] = np.concatenate([np.cos(ang)] * 2, 0).astype(np.float32)
    sn = np.sin(ang)
    c["rope_sinq_bf"] = np.concatenate([-sn, sn], 0).astype(np.float32)
    P = np.zeros((128, 128), dtype=np.float32)
    for i in range(64):
        P[i, 64 + i] = 1.0
        P[64 + i, i] = 1.0
    c["swapT_bf"] = P.T.copy()
    c["diagmask_bf"] = (np.arange(128)[None, :] >= np.arange(128)[:, None]).astype(np.float32)
    c["ramp512_hf16"] = np.repeat(np.arange(512, dtype=np.float64)[None, :], 128, 0).astype(np.float32)
    c["cjramp"] = np.arange(128, dtype=np.float64)[:, None].astype(np.float32)
    c["hidx"] = np.repeat(np.arange(H0, dtype=np.float64)[None, :], 128, 0).astype(np.float32)
    c["j128"] = np.repeat((128.0 * np.arange(SB))[None, :], 128, 0).astype(np.float32)
    c["ig512"] = np.repeat((512.0 * np.arange(SG))[None, :], 128, 0).astype(np.float32)
    cs = np.zeros((128, 4))
    cs[:, 0] = 1.0 / 32
    cs[:, 1] = 1.0 / 512
    cs[:, 2] = 7.0
    c["cseed"] = cs.astype(np.float32)
    return c


def build_l0_test(layers=(0,), dbg_names=()):
    nc = bass.Bass("TRN2", target_bir_lowering=False, debug=False)
    xi0 = nc.dram_tensor("xi0", [KC, 128, S], F32, kind="ExternalInput").ap()
    wq = nc.dram_tensor("wq", [L0L, D, D], F32, kind="ExternalInput").ap()
    wk = nc.dram_tensor("wk", [L0L, D, D], F32, kind="ExternalInput").ap()
    wv = nc.dram_tensor("wv", [L0L, D, D], F32, kind="ExternalInput").ap()
    wg = nc.dram_tensor("wg", [L0L, D, D], F32, kind="ExternalInput").ap()
    wo = nc.dram_tensor("wo", [L0L, D, D], F32, kind="ExternalInput").ap()
    w1 = nc.dram_tensor("w1", [L0L, D, F0], F32, kind="ExternalInput").ap()
    w2 = nc.dram_tensor("w2", [L0L, F0, D], F32, kind="ExternalInput").ap()
    lnp = nc.dram_tensor("lnp_d", [128, 2 * KC], F32, kind="ExternalInput").ap()
    gnp = nc.dram_tensor("gnp_d", [128, L0L * 2 * KC], F32, kind="ExternalInput").ap()
    carr = l0_const_arrays()
    dcon = {k: nc.dram_tensor("dc_" + k, list(v.shape), F32,
                              kind="ExternalInput").ap()
            for k, v in carr.items()}
    dbgshapes = {"xn": [128, KC * S], "qt": [128, H0 * S], "kt": [128, H0 * S],
                 "vt": [128, SB * D]}
    dbg = {k: nc.dram_tensor("dbg_" + k, dbgshapes[k], F32,
                             kind="ExternalOutput").ap() for k in dbg_names} or None
    xi_a = nc.dram_tensor("xi_a", [KC, 128, S], F16, kind="ExternalOutput").ap()
    xi_b = nc.dram_tensor("xi_b", [KC, 128, S], F16, kind="ExternalOutput").ap()
    yr = nc.dram_tensor("yr_s", [KC, 128, S], F16, kind="ExternalOutput").ap()
    with ExitStack() as ctx:
        em = Em(nc, ctx)
        for i in range(8):
            em.psum_bank(f"pb{i}")
        setup_common_consts(em, dcon)
        em.sb["lnp"] = em.sbuf("lnp", [128, 2 * KC], F32)
        em.op("sync", lambda e: e.dma_start(out=em.sb["lnp"][:, :], in_=lnp[:, :]),
              writes=["lnp"], dma=True)
        em.sb["gnp"] = em.sbuf("gnp", [128, L0L * 2 * KC], F32)
        em.op("sync", lambda e: e.dma_start(out=em.sb["gnp"][:, :], in_=gnp[:, :]),
              writes=["gnp"], dma=True)
        emit_decay_tables(em)
        aps = {"xi0": xi0, "xi": [xi_a, xi_b], "yr": yr,
               "wq": wq, "wk": wk, "wv": wv, "wg": wg, "wo": wo,
               "w1": w1, "w2": w2}
        emit_l0_stack(em, 0, aps, layers=list(layers), dbg=dbg)
        blk = ctx.enter_context(nc.Block())
        em.finalize(blk)
    return nc, carr


def emit_gate_combine(em, aps):
    sbuf = em.sbuf
    nm = "gc_"
    wraw = sbuf(nm + "wraw", [128, KC * D], BF16)
    wT = sbuf(nm + "wT", [128, KC * D], BF16)
    xq = sbuf(nm + "xq", [128, KC * 512], F32)
    cb = sbuf(nm + "cb", [128, KC * 512], F32)
    qbf = sbuf(nm + "qbf", [128, KC * 512], BF16)
    xc = sbuf(nm + "xc", [128, KC * 512], F32)
    sc = sbuf(nm + "sc", [128, KC * 512], F16)
    gt = sbuf(nm + "gt", [128, 512], BF16)
    mxt = sbuf(nm + "mxt", [1, 8], F32)
    ones1 = sbuf(nm + "ones1", [1, 128], F32)
    vecs = sbuf(nm + "vecs", [128, 4], F32)
    wsc = sbuf(nm + "wsc", [128, KC], F32)
    wsx = sbuf(nm + "wsx", [128, KC], F32)
    wbi = sbuf(nm + "wbi", [128, KC], F32)
    identb = em.sb["ident_bf"]
    ps = [em.ps[f"pb{i}"] for i in range(7)]
    pbt = em.ps["pbt"]
    cb2 = em.sb[nm + "cb2"]

    em.op("vector", lambda e: e.memset(ones1[:, :], 1.0), writes=[nm + "ones1"])
    nmx = aps["mxs"].shape[1]
    em.op("sync", lambda e: e.dma_start(out=mxt[0:1, 0:nmx], in_=aps["mxs"][:, :]),
          writes=[nm + "mxt"], dma=True)
    em.op("sync", lambda e: e.dma_start(out=wsc[:, :], in_=aps["wsc"][:, :]),
          writes=[nm + "wsc"], dma=True)
    em.op("sync", lambda e: e.dma_start(out=wbi[:, :], in_=aps["wbias"][:, :]),
          writes=[nm + "wbi"], dma=True)
    em.op("vector", lambda e: e.reduce_max(mxt[0:1, 0:1], mxt[0:1, 0:nmx], axis=AX.X),
          reads=[nm + "mxt"], writes=[nm + "m1"])
    em.op("vector", lambda e: e.tensor_scalar(mxt[0:1, 1:2], mxt[0:1, 0:1],
                                              1.0 / 127.0, EPS_LN, OP.mult, OP.max),
          reads=[nm + "m1"], writes=[nm + "xs"])
    em.op("vector", lambda e: e.reciprocal(mxt[0:1, 2:3], mxt[0:1, 1:2]),
          reads=[nm + "xs"], writes=[nm + "rec"])
    em.op("tensor", lambda e: e.matmul(ps[0][:, 0:2], ones1[:, :],
                                       mxt[0:1, 1:3], start=True, stop=True),
          reads=[nm + "xs", nm + "rec", nm + "ones1"], writes=["pb0"])
    em.op("scalar", lambda e: e.activation(out=vecs[:, 0:2], in_=ps[0][:, 0:2],
                                           func=AF.Copy),
          reads=["pb0"], writes=[nm + "vecs"])
    em.op("vector", lambda e: e.tensor_scalar_mul(wsx[:, :], wsc[:, :],
                                                  vecs[:, 0:1]),
          reads=[nm + "wsc", nm + "vecs"], writes=[nm + "wsx"])
    em.op("gpsimd", lambda e: e.dma_start(
        out=wraw[:, :].rearrange("p (n k) -> p n k", n=KC),
        in_=aps["wint8"].rearrange("(n p) k -> p n k", p=128)),
        writes=[nm + "wraw"], dma=True)
    for n in range(KC):
        for k in range(KC):
            em.op("tensor", lambda e, n=n, k=k: e.transpose(
                pbt[:, 0:128],
                wraw[:, n * D + k * 128:n * D + (k + 1) * 128], identb[:, :]),
                reads=[nm + "wraw", "ident_bf"], writes=["pbt"])
            em.op("scalar", lambda e, n=n, k=k: e.activation(
                out=wT[:, k * D + n * 128:k * D + (n + 1) * 128],
                in_=pbt[:, 0:128], func=AF.Copy),
                reads=["pbt"], writes=[nm + f"wT{k}_{n}"])
    wTkeys = [nm + f"wT{k}_{n}" for k in range(KC) for n in range(KC)]
    for g in range(TG):
        em.op("sync", lambda e, g=g: e.dma_start(
            out=xq[:, :].rearrange("p (k t) -> p k t", k=KC),
            in_=aps["xn"][:, :, g * 512:(g + 1) * 512].rearrange("k p t -> p k t")),
            reads=[], writes=[nm + "xq"], dma=True)
        em.op("vector", lambda e: e.tensor_scalar_mul(xq[:, :], xq[:, :],
                                                      vecs[:, 1:2]),
            reads=[nm + "xq", nm + "vecs"], writes=[nm + "xq"])
        em.op("vector", lambda e: e.tensor_scalar_add(cb[:, :], xq[:, :], 128.5),
              reads=[nm + "xq"], writes=[nm + "cb"])
        em.op("vector", lambda e: e.tensor_scalar_add(xq[:, :], cb[:, :], 0.0),
              reads=[nm + "cb"], writes=[nm + "xq"])
        for bit in range(7, -1, -1):
            em.op("vector", lambda e, bit=bit: e.tensor_scalar(
                cb2[:, :],
                xq[:, :], float(2 ** bit), float(2 ** bit), OP.is_ge, OP.mult),
                reads=[nm + "xq"], writes=[nm + "cb2"])
            em.op("vector", lambda e: e.tensor_tensor(
                xq[:, :], xq[:, :], cb2[:, :], OP.subtract),
                reads=[nm + "xq", nm + "cb2"], writes=[nm + "xq"])
        em.op("vector", lambda e: e.tensor_tensor(cb[:, :], cb[:, :], xq[:, :],
                                                  OP.subtract),
              reads=[nm + "cb", nm + "xq"], writes=[nm + "cb"])
        em.op("vector", lambda e: e.tensor_scalar(cb[:, :], cb[:, :], -128.0, 127.0,
                                                  OP.add, OP.min),
              reads=[nm + "cb"], writes=[nm + "cb"])
        em.op("vector", lambda e: e.tensor_scalar_max(qbf[:, :], cb[:, :], -128.0),
              reads=[nm + "cb"], writes=[nm + "qbf"])
        em.op("sync", lambda e, g=g: e.dma_start(
            out=xc[:, :].rearrange("p (k t) -> p k t", k=KC),
            in_=aps["x"][:, :, g * 512:(g + 1) * 512].rearrange("k p t -> p k t")),
            writes=[nm + "xc"], dma=True)
        em.op("sync", lambda e, g=g: e.dma_start(
            out=sc[:, :].rearrange("p (k t) -> p k t", k=KC),
            in_=aps["s0"][:, :, g * 512:(g + 1) * 512].rearrange("k p t -> p k t")),
            writes=[nm + "sc"], dma=True)
        for m in range(KC):
            bank, bkey = (ps[2], "pb2") if m % 2 == 0 else (ps[3], "pb3")
            for k in range(KC):
                em.op("tensor", lambda e, m=m, k=k, bank=bank: e.matmul(
                    bank[:, 0:512],
                    wT[:, k * D + m * 128:k * D + (m + 1) * 128],
                    qbf[:, k * 512:(k + 1) * 512],
                    start=(k == 0), stop=(k == KC - 1)),
                    reads=[nm + "qbf"] + ([w for w in wTkeys] if k == 0 and m == 0 else []),
                    writes=([bkey] if k in (0, KC - 1) else []))
            em.op("scalar", lambda e, m=m, bank=bank: e.activation(
                out=gt[:, :], in_=bank[:, 0:512], func=AF.Silu,
                scale=wsx[:, m:m + 1], bias=wbi[:, m:m + 1]),
                reads=[bkey, nm + "wsx", nm + "wbi"], writes=[nm + "gt"])
            em.op("vector", lambda e, m=m: e.tensor_tensor(
                sc[:, m * 512:(m + 1) * 512], sc[:, m * 512:(m + 1) * 512],
                gt[:, :], OP.mult),
                reads=[nm + "sc", nm + "gt"], writes=[nm + "sc"])
            em.op("vector", lambda e, m=m: e.tensor_tensor(
                xc[:, m * 512:(m + 1) * 512], xc[:, m * 512:(m + 1) * 512],
                sc[:, m * 512:(m + 1) * 512], OP.add),
                reads=[nm + "xc", nm + "sc"], writes=[nm + "xc"])
        em.op("gpsimd", lambda e, g=g: e.dma_start(
            out=aps["x1"][:, :, g * 512:(g + 1) * 512].rearrange("k p t -> p k t"),
            in_=xc[:, :].rearrange("p (k t) -> p k t", k=KC)),
            reads=[nm + "xc"], writes=["X1"], dma=True)


def build_gc_test():
    nc = bass.Bass("TRN2", target_bir_lowering=False, debug=False)
    xn = nc.dram_tensor("xn", [KC, 128, T], F32, kind="ExternalInput").ap()
    x = nc.dram_tensor("x", [KC, 128, T], F32, kind="ExternalInput").ap()
    s0 = nc.dram_tensor("s0", [KC, 128, T], F16, kind="ExternalInput").ap()
    wint8 = nc.dram_tensor("wint8", [D, D], mybir.dt.int8, kind="ExternalInput").ap()
    wsc = nc.dram_tensor("wsc", [128, KC], F32, kind="ExternalInput").ap()
    wbias = nc.dram_tensor("wbias", [128, KC], F32, kind="ExternalInput").ap()
    mxs = nc.dram_tensor("mxs", [1, 8], F32, kind="ExternalInput").ap()
    identb_d = nc.dram_tensor("identb_d", [128, 128], F32, kind="ExternalInput").ap()
    x1 = nc.dram_tensor("x1", [KC, 128, T], F32, kind="ExternalOutput").ap()
    with ExitStack() as ctx:
        em = Em(nc, ctx)
        for i in range(7):
            em.psum_bank(f"pb{i}")
        t = em.ctx.enter_context(nc.psum_tensor("pbt", [128, 1024], BF16))
        em.ps["pbt"] = t
        tb = em.sbuf("ident_bf", [128, 128], BF16)
        em.op("gpsimd", lambda e: e.dma_start(out=tb[:, :], in_=identb_d[:, :]),
              writes=["ident_bf"], dma=True)
        em.sbuf("gc_cb2", [128, KC * 512], F32)
        emit_gate_combine(em, {"xn": xn, "x": x, "s0": s0, "wint8": wint8,
                               "wsc": wsc, "wbias": wbias, "mxs": mxs, "x1": x1})
        blk = ctx.enter_context(nc.Block())
        em.finalize(blk)
    return nc


def emit_l1_attn(em, b, aps):
    sbuf = em.sbuf
    nm = "l1_"
    qb = sbuf(nm + "q", [128, KC * S], BF16)
    kb = sbuf(nm + "k", [128, KC * S], BF16)
    vb = sbuf(nm + "v", [128, KC * S], BF16)
    xnb = sbuf(nm + "xn", [128, KC * S], BF16)
    wrw = sbuf(nm + "wrw", [128, D], BF16)
    wtr = sbuf(nm + "wtr", [128, D], BF16)
    wpT = sbuf(nm + "wpT", [128, KC * D], BF16)
    zer = sbuf(nm + "zeros", [128, S], BF16)
    xg = sbuf(nm + "xg", [128, KC * 512], F16)
    yg = sbuf(nm + "yg", [128, KC * 512], F32)
    Aall = sbuf(nm + "A", [128, S], BF16)
    Ball = sbuf(nm + "B", [128, S], BF16)
    t1 = sbuf(nm + "t1", [128, 512], F32)
    t2 = sbuf(nm + "t2", [128, 512], F32)
    tb1 = sbuf(nm + "tb1", [128, 512], BF16)
    mum = sbuf(nm + "mu", [128, 512], F32)
    msq = sbuf(nm + "msq", [128, 512], F32)
    scv = sbuf(nm + "scv", [128, 2], F32)
    bqs = sbuf(nm + "bqs", [128, 2 * KC], F32)
    ones1 = sbuf(nm + "ones1", [1, 128], F32)
    bqkv = sbuf(nm + "bqkv", [128, 3 * KC], F32)
    bproj = sbuf(nm + "bproj", [128, KC], F32)
    identb = em.sb["ident_bf"]
    ones_hf = em.sb["ones_hf"]
    blk64 = em.sb["blkones64_bf"]
    epsl = em.sb["cp_epsln"]
    mone = em.sb["cp_mone"]
    mhalf = em.sb["cp_mhalf"]
    ps = [em.ps[f"pb{i}"] for i in range(7)]
    pbt = em.ps["pbt"]

    em.op("vector", lambda e: e.memset(ones1[:, :], 1.0), writes=[nm + "ones1"])
    em.op("vector", lambda e: e.memset(zer[:, :], 0.0), writes=[nm + "zeros"])
    em.op("sync", lambda e: e.dma_start(out=bqkv[:, :], in_=aps["bqkv"][:, :]),
          writes=[nm + "bqkv"], dma=True)
    em.op("sync", lambda e: e.dma_start(out=bproj[:, :], in_=aps["bproj"][:, :]),
          writes=[nm + "bproj"], dma=True)
    em.op("sync", lambda e: e.dma_start(out=scv[0:1, 0:1], in_=aps["scl"][:, :]),
          writes=[nm + "scv0"], dma=True)
    em.op("tensor", lambda e: e.matmul(ps[0][:, 0:1], ones1[:, :], scv[0:1, 0:1],
                                       start=True, stop=True),
          reads=[nm + "scv0", nm + "ones1"], writes=["pb0"])
    em.op("scalar", lambda e: e.activation(out=scv[:, 1:2], in_=ps[0][:, 0:1],
                                           func=AF.Copy),
          reads=["pb0"], writes=[nm + "scv"])
    em.op("vector", lambda e: e.tensor_scalar_mul(bqs[:, :], bqkv[:, 0:2 * KC],
                                                  scv[:, 1:2]),
          reads=[nm + "bqkv", nm + "scv"], writes=[nm + "bqs"])

    src_ap, src_key = aps["x1"], "X1"
    off = b * S

    def xi_load(dst, dkey, g):
        em.op("gpsimd", lambda e, g=g, dst=dst: e.dma_start(
            out=dst[:, :].rearrange("p (k t) -> p k t", k=KC),
            in_=src_ap[:, :, off + g * 512:off + (g + 1) * 512]
            .rearrange("k p t -> p k t")),
            reads=[src_key], writes=[dkey], dma=True)

    for g in range(SG):
        xi_load(xg, nm + "xg", g)
        em.op("scalar", lambda e: e.activation(
            out=yg[:, :], in_=xg[:, :], func=AF.Square),
            reads=[nm + "xg"], writes=[nm + "ygs"])
        for c in range(KC):
            em.op("tensor", lambda e, c=c: e.matmul(
                ps[0][:, 0:512], ones_hf[:, :], xg[:, c * 512:(c + 1) * 512],
                start=(c == 0), stop=(c == KC - 1)),
                reads=[nm + "xg", "ones_hf"],
                writes=(["pb0"] if c in (0, KC - 1) else []))
        for c in range(KC):
            em.op("tensor", lambda e, c=c: e.matmul(
                ps[1][:, 0:512], em.sb["ones_f32"][:, :],
                yg[:, c * 512:(c + 1) * 512],
                start=(c == 0), stop=(c == KC - 1)),
                reads=[nm + "ygs", "ones_f32"],
                writes=(["pb1"] if c in (0, KC - 1) else []))
        em.op("vector", lambda e: e.tensor_scalar_mul(mum[:, :], ps[0][:, 0:512],
                                                      1.0 / D),
              reads=["pb0"], writes=[nm + "mu"])
        em.op("vector", lambda e: e.tensor_scalar_mul(msq[:, :], ps[1][:, 0:512],
                                                      1.0 / D),
              reads=["pb1"], writes=[nm + "msq"])
        em.op("vector", lambda e: e.tensor_tensor(t1[:, :], mum[:, :], mum[:, :],
                                                  OP.mult),
              reads=[nm + "mu"], writes=[nm + "t1"])
        em.op("vector", lambda e: e.tensor_tensor(msq[:, :], msq[:, :], t1[:, :],
                                                  OP.subtract),
              reads=[nm + "msq", nm + "t1"], writes=[nm + "msq"])
        em.op("scalar", lambda e: e.activation(out=msq[:, :], in_=msq[:, :],
                                               func=AF.Ln, bias=epsl[:, 0:1]),
              reads=[nm + "msq", "cp_epsln"], writes=[nm + "msq"])
        em.op("scalar", lambda e, g=g: e.activation(
            out=Aall[:, g * 512:(g + 1) * 512], in_=msq[:, :], func=AF.Exp,
            scale=mhalf[:, 0:1]),
            reads=[nm + "msq", "cp_mhalf"], writes=[nm + f"A{g}"])
        em.op("vector", lambda e, g=g: e.scalar_tensor_tensor(
            Ball[:, g * 512:(g + 1) * 512], mum[:, :], -1.0,
            Aall[:, g * 512:(g + 1) * 512], OP.mult, OP.mult),
            reads=[nm + "mu", nm + f"A{g}"], writes=[nm + f"B{g}"])
        for c in range(KC):
            em.op("vector", lambda e, c=c, g=g: e.tensor_tensor(
                t1[:, :], xg[:, c * 512:(c + 1) * 512],
                Aall[:, g * 512:(g + 1) * 512], OP.mult),
                reads=[nm + "xg", nm + f"A{g}"], writes=[nm + "t1"])
            em.op("vector", lambda e, c=c, g=g: e.tensor_tensor(
                xnb[:, c * S + g * 512:c * S + (g + 1) * 512], t1[:, :],
                Ball[:, g * 512:(g + 1) * 512], OP.add),
                reads=[nm + "t1", nm + f"B{g}"], writes=[nm + f"xn{c}_{g}"])
        em.op("gpsimd", lambda e, g=g: e.dma_start(
            out=aps["xn2"][:, :, off + g * 512:off + (g + 1) * 512]
            .rearrange("k p t -> p k t"),
            in_=xnb.rearrange("p (k s) -> p k s", k=KC)[:, :, g * 512:(g + 1) * 512]),
            reads=[nm + f"xn{c}_{g}" for c in range(KC)], writes=["XN2"], dma=True)

    for dc in range(3 * KC):
        em.op("gpsimd", lambda e, dc=dc: e.dma_start(
            out=wrw[:, :], in_=aps["wqkv"][dc * 128:(dc + 1) * 128, :]),
            writes=[nm + "wrw"], dma=True)
        for k in range(KC):
            em.op("tensor", lambda e, k=k: e.transpose(
                pbt[:, 0:128], wrw[:, k * 128:(k + 1) * 128], identb[:, :]),
                reads=[nm + "wrw", "ident_bf"], writes=["pbt"])
            em.op("scalar", lambda e, k=k: e.activation(
                out=wtr[:, k * 128:(k + 1) * 128], in_=pbt[:, 0:128],
                func=AF.Copy),
                reads=["pbt"], writes=[nm + f"wtr{k}"])
        dst = qb if dc < KC else (kb if dc < 2 * KC else vb)
        c = dc % KC
        pref = "q" if dc < KC else ("k" if dc < 2 * KC else "v")
        for g in range(SG):
            par = g % 2
            bank, bkey = (ps[2], "pb2") if par == 0 else (ps[5], "pb5")
            for k in range(KC):
                em.op("tensor", lambda e, k=k, g=g, bank=bank: e.matmul(
                    bank[:, 0:512], wtr[:, k * 128:(k + 1) * 128],
                    xnb[:, k * S + g * 512:k * S + (g + 1) * 512],
                    start=(k == 0), stop=(k == KC - 1)),
                    reads=[nm + f"wtr{kk}" for kk in range(KC)] + [nm + f"xn{k}_{g}"],
                    writes=([bkey] if k in (0, KC - 1) else []))
            if dc < 2 * KC:
                em.op("scalar", lambda e, dc=dc, bank=bank: e.activation(
                    out=t1[:, :], in_=bank[:, 0:512], func=AF.Identity,
                    scale=scv[:, 1:2], bias=bqs[:, dc:dc + 1]),
                    reads=[bkey, nm + "scv", nm + "bqs"], writes=[nm + "t1"])
                em.op("vector", lambda e: e.tensor_scalar_min(t2[:, :], t1[:, :], 0.0),
                      reads=[nm + "t1"], writes=[nm + "t2"])
                em.op("scalar", lambda e: e.activation(out=t2[:, :], in_=t2[:, :],
                                                       func=AF.Exp),
                      reads=[nm + "t2"], writes=[nm + "t2"])
                em.op("vector", lambda e, dst=dst, c=c, g=g, pref=pref:
                      e.scalar_tensor_tensor(
                          dst[:, c * S + g * 512:c * S + (g + 1) * 512], t1[:, :],
                          0.0, t2[:, :], OP.max, OP.add),
                      reads=[nm + "t1", nm + "t2"],
                      writes=[nm + f"{pref}{c}_{g}"])
            else:
                em.op("scalar", lambda e, dst=dst, c=c, g=g, dc=dc,
                      bank=bank: e.activation(
                    out=dst[:, c * S + g * 512:c * S + (g + 1) * 512],
                    in_=bank[:, 0:512], func=AF.Identity,
                    bias=bqkv[:, dc:dc + 1]),
                    reads=[bkey, nm + "bqkv"], writes=[nm + f"v{c}_{g}"])

    for c in range(KC):
        kkeys = [nm + f"k{c}_{g}" for g in range(SG)]
        vkeys = [nm + f"v{c}_{g}" for g in range(SG)]
        em.op("vector", lambda e, c=c: e.tensor_tensor(
            vb[:, c * S:(c + 1) * S], kb[:, c * S:(c + 1) * S],
            vb[:, c * S:(c + 1) * S], OP.mult),
            reads=kkeys + vkeys, writes=[nm + f"kv{c}"])
        em.op("vector", lambda e, c=c: e.tensor_tensor_scan(
            kb[:, c * S:(c + 1) * S], kb[:, c * S:(c + 1) * S],
            zer[:, 0:S], 0.0, OP.add, OP.add),
            reads=kkeys + [nm + "zeros"], writes=[nm + f"kc{c}"])
        em.op("vector", lambda e, c=c: e.tensor_tensor_scan(
            vb[:, c * S:(c + 1) * S], vb[:, c * S:(c + 1) * S],
            zer[:, 0:S], 0.0, OP.add, OP.add),
            reads=[nm + f"kv{c}", nm + "zeros"], writes=[nm + f"kvc{c}"])

    for g in range(SG):
        for c in range(KC):
            em.op("vector", lambda e, c=c, g=g: e.tensor_tensor(
                tb1[:, :], qb[:, c * S + g * 512:c * S + (g + 1) * 512],
                kb[:, c * S + g * 512:c * S + (g + 1) * 512], OP.mult),
                reads=[nm + f"q{c}_{g}", nm + f"kc{c}"], writes=[nm + "tb1"])
            em.op("tensor", lambda e, c=c: e.matmul(
                ps[3][:, 0:512], blk64[:, :], tb1[:, :], start=True, stop=True),
                reads=[nm + "tb1", "blkones64_bf"], writes=["pb3"])
            em.op("vector", lambda e: e.tensor_scalar_max(
                t1[:, :], ps[3][:, 0:512], EPS_L1),
                reads=["pb3"], writes=[nm + "t1"])
            em.op("scalar", lambda e: e.activation(out=t1[:, :], in_=t1[:, :],
                                                   func=AF.Ln),
                  reads=[nm + "t1"], writes=[nm + "t1"])
            em.op("scalar", lambda e: e.activation(out=t1[:, :], in_=t1[:, :],
                                                   func=AF.Exp,
                                                   scale=mone[:, 0:1]),
                  reads=[nm + "t1", "cp_mone"], writes=[nm + "t1"])
            em.op("vector", lambda e, c=c, g=g: e.tensor_tensor(
                t2[:, :], qb[:, c * S + g * 512:c * S + (g + 1) * 512],
                vb[:, c * S + g * 512:c * S + (g + 1) * 512], OP.mult),
                reads=[nm + f"q{c}_{g}", nm + f"kvc{c}"], writes=[nm + "t2"])
            em.op("vector", lambda e, c=c, g=g: e.tensor_tensor(
                qb[:, c * S + g * 512:c * S + (g + 1) * 512], t2[:, :], t1[:, :],
                OP.mult),
                reads=[nm + "t2", nm + "t1"], writes=[nm + f"y{c}_{g}"])

    for k in range(KC):
        em.op("gpsimd", lambda e, k=k: e.dma_start(
            out=wrw[:, :], in_=aps["wproj"][k * 128:(k + 1) * 128, :]),
            writes=[nm + "wrw"], dma=True)
        for m in range(KC):
            em.op("tensor", lambda e, m=m: e.transpose(
                pbt[:, 0:128], wrw[:, m * 128:(m + 1) * 128], identb[:, :]),
                reads=[nm + "wrw", "ident_bf"], writes=["pbt"])
            em.op("scalar", lambda e, k=k, m=m: e.activation(
                out=wpT[:, k * D + m * 128:k * D + (m + 1) * 128],
                in_=pbt[:, 0:128], func=AF.Copy),
                reads=["pbt"], writes=[nm + f"wpT{k}_{m}"])
    for g in range(SG):
        for m in range(KC):
            bank, bkey = (ps[4], "pb4") if m % 2 == 0 else (ps[6], "pb6")
            for k in range(KC):
                em.op("tensor", lambda e, k=k, m=m, g=g, bank=bank: e.matmul(
                    bank[:, 0:512],
                    wpT[:, k * D + m * 128:k * D + (m + 1) * 128],
                    qb[:, k * S + g * 512:k * S + (g + 1) * 512],
                    start=(k == 0), stop=(k == KC - 1)),
                    reads=[nm + f"wpT{k}_{m}", nm + f"y{k}_{g}"],
                    writes=([bkey] if k in (0, KC - 1) else []))
            em.op("scalar", lambda e, m=m, bank=bank: e.activation(
                out=xg[:, m * 512:(m + 1) * 512], in_=bank[:, 0:512],
                func=AF.Identity, bias=bproj[:, m:m + 1]),
                reads=[bkey, nm + "bproj"], writes=[nm + "xg"])
        em.op("gpsimd", lambda e, g=g: e.dma_start(
            out=aps["s1"][:, :, off + g * 512:off + (g + 1) * 512]
            .rearrange("k p t -> p k t"),
            in_=xg[:, :].rearrange("p (k t) -> p k t", k=KC)),
            reads=[nm + "xg"], writes=["S1"], dma=True)


def build_l1_test():
    nc = bass.Bass("TRN2", target_bir_lowering=False, debug=False)
    x1 = nc.dram_tensor("x1", [KC, 128, S], F32, kind="ExternalInput").ap()
    wqkv = nc.dram_tensor("wqkv", [3 * D, D], F32, kind="ExternalInput").ap()
    bqkv = nc.dram_tensor("bqkv_d", [128, 3 * KC], F32, kind="ExternalInput").ap()
    wproj = nc.dram_tensor("wproj", [D, D], F32, kind="ExternalInput").ap()
    bproj = nc.dram_tensor("bproj_d", [128, KC], F32, kind="ExternalInput").ap()
    scl = nc.dram_tensor("scl", [1, 1], F32, kind="ExternalInput").ap()
    identb_d = nc.dram_tensor("identb_d", [128, 128], F32, kind="ExternalInput").ap()
    blk_d = nc.dram_tensor("blk_d", [128, 128], F32, kind="ExternalInput").ap()
    s1 = nc.dram_tensor("s1", [KC, 128, S], F16, kind="ExternalOutput").ap()
    xn2 = nc.dram_tensor("xn2", [KC, 128, S], BF16, kind="ExternalOutput").ap()
    with ExitStack() as ctx:
        em = Em(nc, ctx)
        for i in range(7):
            em.psum_bank(f"pb{i}")
        t = em.ctx.enter_context(nc.psum_tensor("pbt", [128, 1024], BF16))
        em.ps["pbt"] = t
        for cn, ap2, dt in (("ident_bf", identb_d, BF16), ("blkones64_bf", blk_d, BF16)):
            tb = em.sbuf(cn, [128, 128], dt)
            em.op("gpsimd", lambda e, tb=tb, ap2=ap2: e.dma_start(out=tb[:, :],
                                                                  in_=ap2[:, :]),
                  writes=[cn], dma=True)
        th = em.sbuf("ones_hf", [128, 128], F16)
        em.op("vector", lambda e: e.memset(th[:, :], 1.0), writes=["ones_hf"])
        tf32 = em.sbuf("ones_f32", [128, 128], F32)
        em.op("vector", lambda e: e.memset(tf32[:, :], 1.0), writes=["ones_f32"])
        for nmv, v in (("cp_epsln", EPS_LN), ("cp_mhalf", -0.5), ("cp_mone", -1.0)):
            tv = em.sbuf(nmv, [128, 1], F32)
            em.op("vector", lambda e, tv=tv, v=v: e.memset(tv[:, :], v), writes=[nmv])
        emit_l1_attn(em, 0, {"x1": x1, "wqkv": wqkv, "bqkv": bqkv, "wproj": wproj,
                             "bproj": bproj, "scl": scl, "s1": s1, "xn2": xn2})
        blk = ctx.enter_context(nc.Block())
        em.finalize(blk)
    return nc


def emit_l1_moe(em, aps):
    sbuf = em.sbuf
    nm = "moe_"
    QT = 1024
    NQT = T // QT
    xq = sbuf(nm + "xq", [128, KC * QT], BF16)
    hb = sbuf(nm + "hb", [128, (HID1 // 128) * QT], BF16)
    oa = sbuf(nm + "oa", [128, KC * QT], F16)
    xc = sbuf(nm + "xc", [128, KC * QT], F32)
    sc = sbuf(nm + "sc", [128, KC * QT], F16)
    wroll = sbuf(nm + "wroll", [128, KC * 128], BF16)
    sbuf(nm + "wrollb", [128, KC * 128], BF16)
    w2roll = sbuf(nm + "w2roll", [128, (HID1 // 128) * 128], BF16)
    sbuf(nm + "w2rollb", [128, (HID1 // 128) * 128], BF16)
    wbc = sbuf(nm + "wbc", [128, QT], F16)
    wTq = sbuf(nm + "wTq", [8, QT], F32)
    wrow = sbuf(nm + "wrow", [1, QT], F32)
    brt = sbuf(nm + "brt", [128, E1], F32)
    b1t = sbuf(nm + "b1t", [128, HID1 // 128], F32)
    b2t = sbuf(nm + "b2t", [8, 128], BF16)
    l32 = sbuf(nm + "l32", [128, E1], F32)
    e32 = sbuf(nm + "e32", [128, E1], F32)
    v1 = sbuf(nm + "v1", [128, 4], F32)
    gt = sbuf(nm + "gt", [128, 512], BF16)
    sbuf(nm + "gtb", [128, 512], BF16)
    ot = sbuf(nm + "ot", [128, D], F32)
    ones1 = sbuf(nm + "ones1", [1, 128], F32)
    identf = em.sb["ident_f32"]
    ps = [em.ps[f"pb{i}"] for i in range(7)]
    pbt = em.ps["pbt"]
    wTqb = em.sb["moe_wTqb"]
    wrE = em.sb["moe_wrE"]

    em.op("vector", lambda e: e.memset(ones1[:, :], 1.0), writes=[nm + "ones1"])
    em.op("sync", lambda e: e.dma_start(out=brt[:, :], in_=aps["br"][:, :]),
          writes=[nm + "brt"], dma=True)
    for q in range(NQT):
        qoff = q * QT
        em.op("sync", lambda e, qoff=qoff: e.dma_start(
            out=xq[:, :].rearrange("p (k t) -> p k t", k=KC),
            in_=aps["xn2"][:, :, qoff:qoff + QT].rearrange("k p t -> p k t")),
            writes=[nm + "xq"], dma=True)
        for tb in range(QT // 128):
            for k in range(KC):
                em.op("tensor", lambda e, k=k, tb=tb: e.matmul(
                    ps[0][:, 0:E1],
                    xq[:, k * QT + tb * 128:k * QT + (tb + 1) * 128],
                    wrE[:, k * E1:(k + 1) * E1],
                    start=(k == 0), stop=(k == KC - 1)),
                    reads=[nm + "xq", nm + "wrE"],
                    writes=(["pb0"] if k in (0, KC - 1) else []))
            em.op("vector", lambda e: e.tensor_tensor(
                l32[:, :], ps[0][:, 0:E1], brt[:, :], OP.add),
                reads=["pb0", nm + "brt"], writes=[nm + "l32"])
            em.op("vector", lambda e: e.reduce_max(v1[:, 0:1], l32[:, :], axis=AX.X),
                  reads=[nm + "l32"], writes=[nm + "v1c0"])
            em.op("vector", lambda e: e.tensor_scalar_mul(v1[:, 1:2], v1[:, 0:1], -1.0),
                  reads=[nm + "v1c0"], writes=[nm + "v1c1"])
            em.op("vector", lambda e: e.tensor_scalar(
                e32[:, :], l32[:, :], v1[:, 0:1], None, OP.is_ge),
                reads=[nm + "l32", nm + "v1c0"], writes=[nm + "e32"])
            em.op("vector", lambda e: e.scalar_tensor_tensor(
                e32[:, :], e32[:, :], -1.0e30, l32[:, :], OP.mult, OP.add),
                reads=[nm + "e32", nm + "l32"], writes=[nm + "e32"])
            em.op("vector", lambda e: e.reduce_max(v1[:, 2:3], e32[:, :], axis=AX.X),
                  reads=[nm + "e32"], writes=[nm + "v1c2"])
            em.op("scalar", lambda e: e.activation(
                out=e32[:, :], in_=l32[:, :], func=AF.Exp, bias=v1[:, 1:2]),
                reads=[nm + "l32", nm + "v1c1"], writes=[nm + "e32"])
            em.op("vector", lambda e: e.tensor_scalar(
                l32[:, :], l32[:, :], v1[:, 2:3], None, OP.is_ge),
                reads=[nm + "l32", nm + "v1c2"], writes=[nm + "l32"])
            em.op("vector", lambda e: e.tensor_tensor(
                e32[:, :], e32[:, :], l32[:, :], OP.mult),
                reads=[nm + "e32", nm + "l32"], writes=[nm + "e32"])
            em.op("vector", lambda e: e.reduce_sum(v1[:, 3:4], e32[:, :], axis=AX.X),
                  reads=[nm + "e32"], writes=[nm + "v1c3"])
            em.op("vector", lambda e: e.reciprocal(v1[:, 2:3], v1[:, 3:4]),
                  reads=[nm + "v1c3"], writes=[nm + "v1c2"])
            em.op("vector", lambda e: e.tensor_scalar_mul(
                e32[:, :], e32[:, :], v1[:, 2:3]),
                reads=[nm + "e32", nm + "v1c2"], writes=[nm + "e32"])
            em.op("tensor", lambda e: e.transpose(
                ps[1][0:E1, 0:128], e32[:, :], identf[:, :]),
                reads=[nm + "e32", "ident_f32"], writes=["pb1"])
            em.op("scalar", lambda e, tb=tb: e.activation(
                out=wTq[:, tb * 128:(tb + 1) * 128], in_=ps[1][0:E1, 0:128],
                func=AF.Copy),
                reads=["pb1"], writes=[nm + f"wTq{tb}"])
        em.op("gpsimd", lambda e, qoff=qoff: e.dma_start(
            out=aps["wt_scratch"][:, qoff:qoff + QT], in_=wTq[:, :]),
            reads=[nm + f"wTq{tb}" for tb in range(QT // 128)],
            writes=["WTS"], dma=True)
        em.op("sync", lambda e, qoff=qoff: e.dma_start(
            out=xc[:, :].rearrange("p (k t) -> p k t", k=KC),
            in_=aps["x1"][:, :, qoff:qoff + QT].rearrange("k p t -> p k t")),
            writes=[nm + "xc"], dma=True)
        em.op("sync", lambda e, qoff=qoff: e.dma_start(
            out=sc[:, :].rearrange("p (k t) -> p k t", k=KC),
            in_=aps["s1"][:, :, qoff:qoff + QT].rearrange("k p t -> p k t")),
            writes=[nm + "sc"], dma=True)
        em.op("vector", lambda e: e.memset(oa[:, :], 0.0), writes=[nm + "oa"])
        for ex in range(E1):
            em.op("sync", lambda e, ex=ex, qoff=qoff: e.dma_start(
                out=wrow[:, :], in_=aps["wt_scratch"][ex:ex + 1, qoff:qoff + QT]),
                reads=["WTS"], writes=[nm + "wrow"], dma=True)
            for half in range(2):
                em.op("tensor", lambda e, half=half: e.matmul(
                    ps[2][:, 0:512], ones1[:, :],
                    wrow[0:1, half * 512:(half + 1) * 512],
                    start=True, stop=True),
                    reads=[nm + "wrow", nm + "ones1"], writes=["pb2"])
                em.op("scalar", lambda e, half=half: e.activation(
                    out=wbc[:, half * 512:(half + 1) * 512], in_=ps[2][:, 0:512],
                    func=AF.Copy),
                    reads=["pb2"], writes=[nm + "wbc"])
            em.op("sync", lambda e, ex=ex: e.dma_start(
                out=b1t[:, :], in_=aps["b1"][ex]),
                writes=[nm + "b1t"], dma=True)
            for mh in range(HID1 // 128):
                wr_i, wrk = (wroll, nm + "wroll") if mh % 2 == 0 else (
                    em.sb[nm + "wrollb"], nm + "wrollb")
                em.op("gpsimd", lambda e, ex=ex, mh=mh, wr_i=wr_i: e.dma_start(
                    out=wr_i[:, :].rearrange("p (k m) -> p k m", k=KC),
                    in_=aps["w1"][ex, :, mh * 128:(mh + 1) * 128]
                    .rearrange("(k p) m -> p k m", p=128)),
                    reads=[], writes=[wrk], dma=True)
                for g2 in range(2):
                    par = (mh * 2 + g2) % 2
                    bank, bkey = (ps[3], "pb3") if par == 0 else (ps[6], "pb6")
                    gt_i, gtk = (gt, nm + "gt") if par == 0 else (
                        em.sb[nm + "gtb"], nm + "gtb")
                    for k in range(KC):
                        em.op("tensor", lambda e, k=k, g2=g2, mh=mh, bank=bank,
                              wr_i=wr_i: e.matmul(
                            bank[:, 0:512],
                            wr_i[:, k * 128:(k + 1) * 128],
                            xq[:, k * QT + g2 * 512:k * QT + (g2 + 1) * 512],
                            start=(k == 0), stop=(k == KC - 1)),
                            reads=[wrk, nm + "xq"],
                            writes=([bkey] if k in (0, KC - 1) else []))
                    em.op("scalar", lambda e, mh=mh, g2=g2, bank=bank,
                          gt_i=gt_i: e.activation(
                        out=gt_i[:, :], in_=bank[:, 0:512], func=AF.Relu,
                        bias=b1t[:, mh:mh + 1]),
                        reads=[bkey, nm + "b1t"], writes=[gtk])
                    em.op("vector", lambda e, mh=mh, g2=g2, gt_i=gt_i: e.tensor_tensor(
                        hb[:, mh * QT + g2 * 512:mh * QT + (g2 + 1) * 512],
                        gt_i[:, :], wbc[:, g2 * 512:(g2 + 1) * 512], OP.mult),
                        reads=[gtk, nm + "wbc"], writes=[nm + f"hb{mh}_{g2}"])
            for m in range(KC):
                w2_i, w2k = (w2roll, nm + "w2roll") if m % 2 == 0 else (
                    em.sb[nm + "w2rollb"], nm + "w2rollb")
                em.op("gpsimd", lambda e, ex=ex, m=m, w2_i=w2_i: e.dma_start(
                    out=w2_i[:, :].rearrange("p (k m) -> p k m", k=HID1 // 128),
                    in_=aps["w2"][ex, :, m * 128:(m + 1) * 128]
                    .rearrange("(k p) m -> p k m", p=128)),
                    reads=[], writes=[w2k], dma=True)
                for g2 in range(2):
                    par = (m * 2 + g2) % 2
                    bank, bkey = (ps[4], "pb4") if par == 0 else (ps[5], "pb5")
                    for k2 in range(HID1 // 128):
                        em.op("tensor", lambda e, k2=k2, m=m, g2=g2, bank=bank,
                              w2_i=w2_i: e.matmul(
                            bank[:, 0:512],
                            w2_i[:, k2 * 128:(k2 + 1) * 128],
                            hb[:, k2 * QT + g2 * 512:k2 * QT + (g2 + 1) * 512],
                            start=(k2 == 0), stop=(k2 == HID1 // 128 - 1)),
                            reads=[w2k, nm + f"hb{k2}_{g2}"],
                            writes=([bkey] if k2 in (0, HID1 // 128 - 1) else []))
                    em.op("vector", lambda e, m=m, g2=g2, bank=bank: e.tensor_tensor(
                        oa[:, m * QT + g2 * 512:m * QT + (g2 + 1) * 512],
                        oa[:, m * QT + g2 * 512:m * QT + (g2 + 1) * 512],
                        bank[:, 0:512], OP.add),
                        reads=[nm + "oa", bkey], writes=[nm + "oa"])
        em.op("vector", lambda e: e.tensor_copy(wTqb[:, :], wTq[:, :]),
              reads=[nm + f"wTq{tb}" for tb in range(QT // 128)],
              writes=[nm + "wTqb"], dma=False)
        for m in range(KC):
            em.op("gpsimd", lambda e, m=m: e.dma_start(
                out=b2t[:, :], in_=aps["b2"][:, m * 128:(m + 1) * 128]),
                writes=[nm + "b2t"], dma=True)
            for g2 in range(2):
                em.op("tensor", lambda e, m=m, g2=g2: e.matmul(
                    ps[5][:, 0:512], b2t[:, :],
                    wTqb[:, g2 * 512:(g2 + 1) * 512],
                    start=True, stop=True),
                    reads=[nm + "b2t", nm + "wTqb"], writes=["pb5"])
                em.op("vector", lambda e, m=m, g2=g2: e.tensor_tensor(
                    oa[:, m * QT + g2 * 512:m * QT + (g2 + 1) * 512],
                    oa[:, m * QT + g2 * 512:m * QT + (g2 + 1) * 512],
                    ps[5][:, 0:512], OP.add),
                    reads=[nm + "oa", "pb5"], writes=[nm + "oa"])
        for m in range(KC):
            for g2 in range(2):
                em.op("scalar", lambda e, m=m, g2=g2: e.activation(
                    out=gt[:, :], in_=oa[:, m * QT + g2 * 512:m * QT + (g2 + 1) * 512],
                    func=AF.Silu),
                    reads=[nm + "oa"], writes=[nm + "gt"])
                em.op("vector", lambda e, m=m, g2=g2: e.tensor_tensor(
                    sc[:, m * QT + g2 * 512:m * QT + (g2 + 1) * 512],
                    sc[:, m * QT + g2 * 512:m * QT + (g2 + 1) * 512],
                    gt[:, :], OP.mult),
                    reads=[nm + "sc", nm + "gt"], writes=[nm + "sc"])
                em.op("vector", lambda e, m=m, g2=g2: e.tensor_tensor(
                    xc[:, m * QT + g2 * 512:m * QT + (g2 + 1) * 512],
                    xc[:, m * QT + g2 * 512:m * QT + (g2 + 1) * 512],
                    sc[:, m * QT + g2 * 512:m * QT + (g2 + 1) * 512], OP.add),
                    reads=[nm + "xc", nm + "sc"], writes=[nm + "xc"])
        for tb in range(QT // 128):
            for m in range(KC):
                em.op("tensor", lambda e, m=m, tb=tb: e.transpose(
                    ps[6][:, 0:128], xc[:, m * QT + tb * 128:m * QT + (tb + 1) * 128],
                    identf[:, :]),
                    reads=[nm + "xc", "ident_f32"], writes=["pb6"])
                em.op("scalar", lambda e, m=m: e.activation(
                    out=ot[:, m * 128:(m + 1) * 128], in_=ps[6][:, 0:128],
                    func=AF.Copy),
                    reads=["pb6"], writes=[nm + f"ot{m}"])
            em.op("gpsimd", lambda e, qoff=qoff, tb=tb: e.dma_start(
                out=aps["out_tok"][qoff + tb * 128:qoff + (tb + 1) * 128, :],
                in_=ot[:, :]),
                reads=[nm + f"ot{m}" for m in range(KC)], writes=["OUT"], dma=True)


def moe_setup(em, aps):
    wrE = em.sbuf("moe_wrE", [128, KC * E1], BF16)
    em.op("gpsimd", lambda e: e.dma_start(
        out=wrE[:, :].rearrange("p (k e) -> p k e", k=KC),
        in_=aps["wr"].rearrange("(k p) e -> p k e", p=128)),
        writes=["moe_wrE"], dma=True)
    em.sbuf("moe_wTqb", [8, 1024], BF16)


def build_moe():
    nc = bass.Bass("TRN2", target_bir_lowering=False, debug=False)
    ap = {}
    ap["xn2"] = nc.dram_tensor("xn2", [KC, 128, T], BF16, kind="ExternalInput").ap()
    ap["x1"] = nc.dram_tensor("x1", [KC, 128, T], F32, kind="ExternalInput").ap()
    ap["s1"] = nc.dram_tensor("s1", [KC, 128, T], F16, kind="ExternalInput").ap()
    ap["wr"] = nc.dram_tensor("wr", [D, E1], F32, kind="ExternalInput").ap()
    ap["br"] = nc.dram_tensor("br", [128, E1], F32, kind="ExternalInput").ap()
    ap["w1"] = nc.dram_tensor("w1", [E1, D, HID1], F32, kind="ExternalInput").ap()
    ap["b1"] = nc.dram_tensor("b1", [E1, 128, HID1 // 128], F32,
                              kind="ExternalInput").ap()
    ap["w2"] = nc.dram_tensor("w2", [E1, HID1, D], F32, kind="ExternalInput").ap()
    ap["b2"] = nc.dram_tensor("b2", [E1, D], F32, kind="ExternalInput").ap()
    identf_d = nc.dram_tensor("identf_d", [128, 128], F32, kind="ExternalInput").ap()
    ap["wt_scratch"] = nc.dram_tensor("wts", [E1, T], F32, kind="Internal").ap()
    ap["out_tok"] = nc.dram_tensor("out_tok", [T, D], F32, kind="ExternalOutput").ap()
    with ExitStack() as ctx:
        em = Em(nc, ctx)
        for i in range(7):
            em.psum_bank(f"pb{i}")
        em.ps["pbt"] = ctx.enter_context(nc.psum_tensor("pbt", [128, 1024], BF16))
        tf = em.sbuf("ident_f32", [128, 128], F32)
        em.op("sync", lambda e: e.dma_start(out=tf[:, :], in_=identf_d[:, :]),
              writes=["ident_f32"], dma=True)
        moe_setup(em, ap)
        emit_l1_moe(em, ap)
        blk = ctx.enter_context(nc.Block())
        em.finalize(blk)
    return nc


_BUILD_CACHE = {}


def _get(name, fn):
    if name not in _BUILD_CACHE:
        _BUILD_CACHE[name] = fn()
    return _BUILD_CACHE[name]


def kernel_multi(**inputs):
    import ml_dtypes
    X = np.asarray(inputs["L0_attn_X"])
    carr = l0_const_arrays()
    ident = np.eye(128, dtype=np.float32)
    ones128 = np.ones((128, 128), dtype=np.float32)
    bd = np.zeros((128, 128), np.float32)
    bd[:64, :64] = 1.0
    bd[64:, 64:] = 1.0

    colperm = np.concatenate([h * DH0 + np.concatenate([np.arange(0, DH0, 2),
                                                        np.arange(1, DH0, 2)])
                              for h in range(H0)])
    Wq_p = np.ascontiguousarray(np.asarray(inputs["L0_attn_Wq"])[:, :, colperm])
    Wk_p = np.ascontiguousarray(np.asarray(inputs["L0_attn_Wk"])[:, :, colperm])
    ls = np.asarray(inputs["L0_attn_ln_scale"])
    lb = np.asarray(inputs["L0_attn_ln_bias"])
    gs = np.asarray(inputs["L0_attn_gn_scale"])
    gb = np.asarray(inputs["L0_attn_gn_bias"])
    lnp = np.concatenate([ls.reshape(KC, 128).T, lb.reshape(KC, 128).T], 1)
    gnp = np.concatenate(sum([[gs[l].reshape(KC, 128).T, gb[l].reshape(KC, 128).T]
                              for l in range(L0L)], []), 1)
    wsc_t = np.asarray(inputs["L0_ffn_weight_scale"]).reshape(KC, 128).T
    wbi_t = np.asarray(inputs["L0_ffn_bias"]).reshape(KC, 128).T
    bqkv_t = np.asarray(inputs["L1_attn_bqkv"]).reshape(3 * KC, 128).T
    bproj_t = np.asarray(inputs["L1_attn_bproj"]).reshape(KC, 128).T
    scl_t = np.asarray(inputs["L1_attn_scale"]).reshape(1, 1)
    br_t = np.repeat(np.asarray(inputs["L1_ffn_br"])[None, :], 128, 0)
    b1_t = np.ascontiguousarray(
        np.asarray(inputs["L1_ffn_b1"]).reshape(E1, HID1 // 128, 128)
        .transpose(0, 2, 1))

    cores = list(range(NCORES))
    nc1 = _get("p1", build_p1)
    im1 = []
    for c in cores:
        xslice = np.ascontiguousarray(X[2 * c:2 * c + 2].reshape(T, D))
        im1.append({"x_tok": xslice, "ident": ident, "ones128": ones128})
    r1 = run_bass_kernel_spmd(nc1, im1, core_ids=cores)
    x_d = [r1.results[c]["x_d"] for c in cores]
    xn_d = [r1.results[c]["xn_d"] for c in cores]
    mxs = np.concatenate([r1.results[c]["mx"].reshape(1, 1) for c in cores],
                         axis=1)
    nc2 = _get("l0", lambda: build_l0_test(layers=(0, 1, 2, 3))[0])
    wl0 = {"wq": Wq_p, "wk": Wk_p, "wv": np.asarray(inputs["L0_attn_Wv"]),
           "wg": np.asarray(inputs["L0_attn_Wg"]),
           "wo": np.asarray(inputs["L0_attn_Wo"]),
           "w1": np.asarray(inputs["L0_attn_W1"]),
           "w2": np.asarray(inputs["L0_attn_W2"]),
           "lnp_d": lnp, "gnp_d": gnp}
    for k, v in carr.items():
        wl0["dc_" + k] = v
    s0 = [None] * NCORES
    for b in range(BLOC):
        im2 = []
        for c in cores:
            d = dict(wl0)
            d["xi0"] = np.ascontiguousarray(xn_d[c][:, :, b * S:(b + 1) * S])
            im2.append(d)
        r2 = run_bass_kernel_spmd(nc2, im2, core_ids=cores)
        for c in cores:
            part = r2.results[c]["xi_a"]
            s0[c] = part if b == 0 else np.concatenate([s0[c], part], axis=2)
    nc3 = _get("gc", build_gc_test)
    im3 = []
    for c in cores:
        im3.append({"xn": xn_d[c], "x": x_d[c], "s0": s0[c],
                    "wint8": np.asarray(inputs["L0_ffn_weight_int8"]),
                    "wsc": wsc_t, "wbias": wbi_t, "mxs": mxs,
                    "identb_d": ident})
    r3 = run_bass_kernel_spmd(nc3, im3, core_ids=cores)
    x1_d = [r3.results[c]["x1"] for c in cores]
    nc4 = _get("l1", build_l1_test)
    s1 = [None] * NCORES
    xn2 = [None] * NCORES
    for b in range(BLOC):
        im4 = []
        for c in cores:
            im4.append({"x1": np.ascontiguousarray(x1_d[c][:, :, b * S:(b + 1) * S]),
                        "wqkv": np.asarray(inputs["L1_attn_Wqkv"]),
                        "bqkv_d": bqkv_t,
                        "wproj": np.asarray(inputs["L1_attn_Wproj"]),
                        "bproj_d": bproj_t, "scl": scl_t,
                        "identb_d": ident, "blk_d": bd})
        r4 = run_bass_kernel_spmd(nc4, im4, core_ids=cores)
        for c in cores:
            p1_, p2_ = r4.results[c]["s1"], r4.results[c]["xn2"]
            s1[c] = p1_ if b == 0 else np.concatenate([s1[c], p1_], axis=2)
            xn2[c] = p2_ if b == 0 else np.concatenate([xn2[c], p2_], axis=2)
    nc5 = _get("moe", build_moe)
    im5 = []
    for c in cores:
        im5.append({"xn2": xn2[c], "x1": x1_d[c], "s1": s1[c],
                    "wr": np.asarray(inputs["L1_ffn_Wr"]), "br": br_t,
                    "w1": np.asarray(inputs["L1_ffn_W1"]), "b1": b1_t,
                    "w2": np.asarray(inputs["L1_ffn_W2"]),
                    "b2": np.asarray(inputs["L1_ffn_b2"]),
                    "identf_d": ident})
    r5 = run_bass_kernel_spmd(nc5, im5, core_ids=cores)
    out = np.empty((B, S, D), dtype=np.float32)
    for c in cores:
        out[2 * c:2 * c + 2] = np.asarray(r5.results[c]["out_tok"],
                                          dtype=np.float32).reshape(2, S, D)
    return out


def emit_maxscan(em, x_all, gmax_d):
    sbuf = em.sbuf
    nm = "mxs_"
    NTB = (B * S) // 128
    xt = [sbuf(nm + f"xt{i}", [128, D], F32) for i in range(2)]
    xsq = sbuf(nm + "xsq", [128, D], F32)
    col = sbuf(nm + "col", [128, 8], F32)
    mxb = sbuf(nm + "mxb", [128, NTB], F32)
    mxc = sbuf(nm + "mxc", [128, 2], F32)
    mxr = sbuf(nm + "mxr", [1, 128], F32)
    id_sb = em.sb["ident"]
    eps_sb = em.sb["cp_epsln"]
    mhalf = em.sb["cp_mhalf"]
    ps0 = em.ps["ig_ps0"]
    for tb in range(NTB):
        pi = tb % 2
        kxt = nm + f"xt{pi}"
        em.op("sync", lambda e, tb=tb, t=xt[pi]: e.dma_start(
            out=t[:, :], in_=x_all[tb * 128:(tb + 1) * 128, :]),
            writes=[kxt], dma=True)
        em.op("scalar", lambda e, t=xt[pi]: e.activation(
            out=xsq[:, :], in_=t[:, :], func=AF.Square,
            accum_out=col[:, 1:2]),
            reads=[kxt], writes=[nm + "xsq", nm + "c1"])
        em.op("vector", lambda e, t=xt[pi]: e.reduce_sum(
            col[:, 0:1], t[:, :], axis=AX.X),
            reads=[kxt], writes=[nm + "c0"])
        em.op("vector", lambda e, t=xt[pi]: e.reduce_max(
            col[:, 2:3], t[:, :], axis=AX.X),
            reads=[kxt], writes=[nm + "c2"])
        em.op("vector", lambda e, t=xt[pi]: e.tensor_reduce(
            col[:, 3:4], t[:, :], axis=AX.X, op=OP.min),
            reads=[kxt], writes=[nm + "c3"])
        em.op("vector", lambda e: e.tensor_scalar_mul(col[:, 4:5], col[:, 0:1],
                                                      1.0 / D),
              reads=[nm + "c0"], writes=[nm + "c4"])
        em.op("vector", lambda e: e.tensor_scalar_mul(col[:, 5:6], col[:, 1:2],
                                                      1.0 / D),
              reads=[nm + "c1"], writes=[nm + "c5"])
        em.op("vector", lambda e: e.scalar_tensor_tensor(
            col[:, 6:7], col[:, 4:5], -1.0, col[:, 4:5], OP.mult, OP.mult),
            reads=[nm + "c4"], writes=[nm + "c6"])
        em.op("vector", lambda e: e.tensor_tensor(col[:, 5:6], col[:, 5:6],
                                                  col[:, 6:7], OP.add),
              reads=[nm + "c5", nm + "c6"], writes=[nm + "c5"])
        em.op("scalar", lambda e: e.activation(out=col[:, 5:6], in_=col[:, 5:6],
                                               func=AF.Ln, bias=eps_sb[:, 0:1]),
              reads=[nm + "c5", "cp_epsln"], writes=[nm + "c5"])
        em.op("scalar", lambda e: e.activation(out=col[:, 5:6], in_=col[:, 5:6],
                                               func=AF.Exp, scale=mhalf[:, 0:1]),
              reads=[nm + "c5", "cp_mhalf"], writes=[nm + "c5"])
        em.op("vector", lambda e: e.tensor_tensor(col[:, 2:3], col[:, 2:3],
                                                  col[:, 4:5], OP.subtract),
              reads=[nm + "c2", nm + "c4"], writes=[nm + "c2"])
        em.op("vector", lambda e: e.tensor_tensor(col[:, 3:4], col[:, 4:5],
                                                  col[:, 3:4], OP.subtract),
              reads=[nm + "c3", nm + "c4"], writes=[nm + "c3"])
        em.op("vector", lambda e: e.tensor_tensor(col[:, 2:3], col[:, 2:3],
                                                  col[:, 3:4], OP.max),
              reads=[nm + "c2", nm + "c3"], writes=[nm + "c2"])
        em.op("vector", lambda e, tb=tb: e.tensor_tensor(
            mxb[:, tb:tb + 1], col[:, 2:3], col[:, 5:6], OP.mult),
            reads=[nm + "c2", nm + "c5"], writes=[nm + f"mb{tb}"])
    em.op("vector", lambda e: e.reduce_max(mxc[:, 0:1], mxb[:, :], axis=AX.X),
          reads=[nm + f"mb{tb}" for tb in range(NTB)], writes=[nm + "mxc"])
    em.op("tensor", lambda e: e.transpose(ps0[0:1, 0:128], mxc[:, 0:1],
                                          id_sb[:, :]),
          reads=[nm + "mxc", "ident"], writes=["ig_ps0"])
    em.op("scalar", lambda e: e.activation(out=mxr[:, :], in_=ps0[0:1, 0:128],
                                           func=AF.Copy),
          reads=["ig_ps0"], writes=[nm + "mxr"])
    em.op("vector", lambda e: e.reduce_max(mxc[0:1, 1:2], mxr[:, :], axis=AX.X),
          reads=[nm + "mxr"], writes=[nm + "gmax"])
    em.op("gpsimd", lambda e: e.dma_start(out=gmax_d[:, :], in_=mxc[0:1, 1:2]),
          reads=[nm + "gmax"], writes=["GMAX"], dma=True)


def build_fused():
    nc = bass.Bass("TRN2", target_bir_lowering=False, debug=False)
    AP_ = lambda n, s, d, k: nc.dram_tensor(n, s, d, kind=k).ap()
    x_tok = AP_("x_tok", [T, D], F32, "ExternalInput")
    wq = AP_("wq", [L0L, D, D], F32, "ExternalInput")
    wk = AP_("wk", [L0L, D, D], F32, "ExternalInput")
    wv = AP_("wv", [L0L, D, D], F32, "ExternalInput")
    wg = AP_("wg", [L0L, D, D], F32, "ExternalInput")
    wo = AP_("wo", [L0L, D, D], F32, "ExternalInput")
    w1 = AP_("w1", [L0L, D, F0], F32, "ExternalInput")
    w2 = AP_("w2", [L0L, F0, D], F32, "ExternalInput")
    lnp_d = AP_("lnp_d", [128, 2 * KC], F32, "ExternalInput")
    gnp_d = AP_("gnp_d", [128, L0L * 2 * KC], F32, "ExternalInput")
    wint8 = AP_("wint8", [D, D], mybir.dt.int8, "ExternalInput")
    wsc_d = AP_("wsc", [128, KC], F32, "ExternalInput")
    wbi_d = AP_("wbias", [128, KC], F32, "ExternalInput")
    wqkv = AP_("wqkv", [3 * D, D], F32, "ExternalInput")
    bqkv_d = AP_("bqkv_d", [128, 3 * KC], F32, "ExternalInput")
    wproj = AP_("wproj", [D, D], F32, "ExternalInput")
    bproj_d = AP_("bproj_d", [128, KC], F32, "ExternalInput")
    scl_d = AP_("scl", [1, 1], F32, "ExternalInput")
    wr_d = AP_("wr", [D, E1], F32, "ExternalInput")
    br_d = AP_("br", [128, E1], F32, "ExternalInput")
    mw1 = AP_("mw1", [E1, D, HID1], F32, "ExternalInput")
    mb1 = AP_("mb1", [E1, 128, HID1 // 128], F32, "ExternalInput")
    mw2 = AP_("mw2", [E1, HID1, D], F32, "ExternalInput")
    mb2 = AP_("mb2", [E1, D], F32, "ExternalInput")
    ident_d = AP_("ident", [128, 128], F32, "ExternalInput")
    ones_d = AP_("ones128", [128, 128], F32, "ExternalInput")
    blk_d = AP_("blk_d", [128, 128], F32, "ExternalInput")
    carr = l0_const_arrays()
    dcon = {k: AP_("dc_" + k, list(v.shape), F32, "ExternalInput")
            for k, v in carr.items()}
    xn_dram = AP_("sc_xn", [KC, 128, T], F32, "Internal")
    x_dram = AP_("sc_x", [KC, 128, T], F32, "Internal")
    xi_a = AP_("sc_xia", [KC, 128, S], F16, "Internal")
    xi_b = AP_("sc_xib", [KC, 128, S], F16, "Internal")
    yr_s = AP_("sc_yr", [KC, 128, S], F16, "Internal")
    s0_d = AP_("sc_s0", [KC, 128, T], F16, "Internal")
    x1_d = AP_("sc_x1", [KC, 128, T], F32, "Internal")
    s1_d = AP_("sc_s1", [KC, 128, T], F16, "Internal")
    xn2_d = AP_("sc_xn2", [KC, 128, T], BF16, "Internal")
    wts = AP_("sc_wts", [E1, T], F32, "Internal")
    x_all = AP_("x_all", [B * S, D], F32, "ExternalInput")
    cc_in = AP_("cc_in", [1, 1], F32, "Internal")
    gmax_d = AP_("gmax_d", [1, 1], F32, "Internal")
    out_tok = AP_("out_tok", [T, D], F32, "ExternalOutput")

    with ExitStack() as ctx:
        em = Em(nc, ctx)
        for name, ap in (("ident", ident_d), ("ones128", ones_d)):
            t = em.sbuf("sb_" + name, [128, 128], F32)
            em.sb[name] = t
            em.op("sync", lambda e, t=t, ap=ap: e.dma_start(out=t[:, :], in_=ap[:, :]),
                  writes=[name], dma=True)
        em.sb["ident_f32"] = em.sb["ident"]
        tb = em.sbuf("ident_bf", [128, 128], BF16)
        em.op("gpsimd", lambda e: e.dma_start(out=tb[:, :], in_=ident_d[:, :]),
              writes=["ident_bf"], dma=True)
        tb2 = em.sbuf("blkones64_bf", [128, 128], BF16)
        em.op("gpsimd", lambda e: e.dma_start(out=tb2[:, :], in_=blk_d[:, :]),
              writes=["blkones64_bf"], dma=True)
        setup_common_consts(em, dcon)
        for nmv, v in (("cp_mone", -1.0),):
            t = em.sbuf(nmv, [128, 1], F32)
            em.op("vector", lambda e, t=t, v=v: e.memset(t[:, :], v), writes=[nmv])
        t = em.sbuf("ones_f32", [128, 128], F32)
        em.op("vector", lambda e, t=t: e.memset(t[:, :], 1.0), writes=["ones_f32"])
        em.sb["lnp"] = em.sbuf("lnp", [128, 2 * KC], F32)
        em.op("sync", lambda e: e.dma_start(out=em.sb["lnp"][:, :], in_=lnp_d[:, :]),
              writes=["lnp"], dma=True)
        em.sb["gnp"] = em.sbuf("gnp", [128, L0L * 2 * KC], F32)
        em.op("sync", lambda e: e.dma_start(out=em.sb["gnp"][:, :], in_=gnp_d[:, :]),
              writes=["gnp"], dma=True)
        emit_decay_tables(em)
        em.barrier()
        em.push_scope()
        for i in range(4):
            em.psum_bank(f"ig_ps{i}")
        em.psum_bank("ig_pss")
        em.psum_bank("ig_psq")
        emit_ingress(em, x_tok, x_dram, xn_dram, cc_in)
        emit_maxscan(em, x_all, gmax_d)
        em.pop_scope()
        em.push_scope()
        for i in range(8):
            em.psum_bank(f"pb{i}")
        for b in range(BLOC):
            aps = {"xi0": xn_dram[:, :, b * S:(b + 1) * S],
                   "xi": [xi_a, xi_b], "yr": yr_s,
                   "sout": s0_d[:, :, b * S:(b + 1) * S],
                   "wq": wq, "wk": wk, "wv": wv, "wg": wg, "wo": wo,
                   "w1": w1, "w2": w2}
            emit_l0_stack(em, b, aps)
        em.pop_scope()
        em.push_scope()
        for i in range(7):
            em.psum_bank(f"pb{i}")
        em.psum_bank("pbt", BF16, 1024)
        em.sbuf("gc_cb2", [128, KC * 512], F32)
        emit_gate_combine(em, {"xn": xn_dram, "x": x_dram, "s0": s0_d,
                               "wint8": wint8, "wsc": wsc_d, "wbias": wbi_d,
                               "mxs": gmax_d, "x1": x1_d})
        em.pop_scope()
        em.push_scope()
        for i in range(7):
            em.psum_bank(f"pb{i}")
        em.psum_bank("pbt", BF16, 1024)
        for b in range(BLOC):
            emit_l1_attn(em, b, {"x1": x1_d, "wqkv": wqkv, "bqkv": bqkv_d,
                                 "wproj": wproj, "bproj": bproj_d, "scl": scl_d,
                                 "s1": s1_d, "xn2": xn2_d})
        em.pop_scope()
        em.push_scope()
        for i in range(7):
            em.psum_bank(f"pb{i}")
        em.psum_bank("pbt", BF16, 1024)
        mo = {"xn2": xn2_d, "x1": x1_d, "s1": s1_d, "wr": wr_d, "br": br_d,
              "w1": mw1, "b1": mb1, "w2": mw2, "b2": mb2,
              "wt_scratch": wts, "out_tok": out_tok}
        moe_setup(em, mo)
        emit_l1_moe(em, mo)
        em.pop_scope()
        blk = ctx.enter_context(nc.Block())
        em.finalize(blk)
    return nc


def kernel_fused(**inputs):
    X = np.asarray(inputs["L0_attn_X"])
    carr = l0_const_arrays()
    ident = np.eye(128, dtype=np.float32)
    ones128 = np.ones((128, 128), dtype=np.float32)
    bd = np.zeros((128, 128), np.float32)
    bd[:64, :64] = 1.0
    bd[64:, 64:] = 1.0
    colperm = np.concatenate([h * DH0 + np.concatenate([np.arange(0, DH0, 2),
                                                        np.arange(1, DH0, 2)])
                              for h in range(H0)])
    ls = np.asarray(inputs["L0_attn_ln_scale"])
    lb = np.asarray(inputs["L0_attn_ln_bias"])
    gs = np.asarray(inputs["L0_attn_gn_scale"])
    gb = np.asarray(inputs["L0_attn_gn_bias"])
    base = {
        "wq": np.ascontiguousarray(np.asarray(inputs["L0_attn_Wq"])[:, :, colperm]),
        "wk": np.ascontiguousarray(np.asarray(inputs["L0_attn_Wk"])[:, :, colperm]),
        "wv": np.asarray(inputs["L0_attn_Wv"]),
        "wg": np.asarray(inputs["L0_attn_Wg"]),
        "wo": np.asarray(inputs["L0_attn_Wo"]),
        "w1": np.asarray(inputs["L0_attn_W1"]),
        "w2": np.asarray(inputs["L0_attn_W2"]),
        "lnp_d": np.concatenate([ls.reshape(KC, 128).T, lb.reshape(KC, 128).T], 1),
        "gnp_d": np.concatenate(sum([[gs[l].reshape(KC, 128).T,
                                      gb[l].reshape(KC, 128).T]
                                     for l in range(L0L)], []), 1),
        "wint8": np.asarray(inputs["L0_ffn_weight_int8"]),
        "wsc": np.asarray(inputs["L0_ffn_weight_scale"]).reshape(KC, 128).T.copy(),
        "wbias": np.asarray(inputs["L0_ffn_bias"]).reshape(KC, 128).T.copy(),
        "wqkv": np.asarray(inputs["L1_attn_Wqkv"]),
        "bqkv_d": np.asarray(inputs["L1_attn_bqkv"]).reshape(3 * KC, 128).T.copy(),
        "wproj": np.asarray(inputs["L1_attn_Wproj"]),
        "bproj_d": np.asarray(inputs["L1_attn_bproj"]).reshape(KC, 128).T.copy(),
        "scl": np.asarray(inputs["L1_attn_scale"]).reshape(1, 1),
        "wr": np.asarray(inputs["L1_ffn_Wr"]),
        "br": np.repeat(np.asarray(inputs["L1_ffn_br"])[None, :], 128, 0),
        "mw1": np.asarray(inputs["L1_ffn_W1"]),
        "mb1": np.ascontiguousarray(
            np.asarray(inputs["L1_ffn_b1"]).reshape(E1, HID1 // 128, 128)
            .transpose(0, 2, 1)),
        "mw2": np.asarray(inputs["L1_ffn_W2"]),
        "mb2": np.asarray(inputs["L1_ffn_b2"]),
        "ident": ident, "ones128": ones128, "blk_d": bd,
    }
    for k, v in carr.items():
        base["dc_" + k] = v
    nc = _get("fused", build_fused)
    X_flat = np.ascontiguousarray(X.reshape(B * S, D))
    cores = list(range(NCORES))
    im = []
    for c in cores:
        d = dict(base)
        d["x_tok"] = np.ascontiguousarray(X[2 * c:2 * c + 2].reshape(T, D))
        d["x_all"] = X_flat
        im.append(d)
    r = run_bass_kernel_spmd(nc, im, core_ids=cores)
    out = np.empty((B, S, D), dtype=np.float32)
    for c in cores:
        out[2 * c:2 * c + 2] = np.asarray(r.results[c]["out_tok"],
                                          dtype=np.float32).reshape(2, S, D)
    return out


kernel = kernel_fused
